# version 15
# baseline (speedup 1.0000x reference)
import math
import contextlib
import numpy as np
import concourse.bass as bass
import concourse.mybir as mybir

F32 = mybir.dt.float32
BF16 = mybir.dt.bfloat16
F16 = mybir.dt.float16
I32 = mybir.dt.int32
AF = mybir.ActivationFunctionType
ALU = mybir.AluOpType
AX = mybir.AxisListType

B, S, E, H, D = 4, 4096, 2048, 16, 128
F1 = 8192
TOWN = 2048
NEC = E // 128
INVSQ = 1.0 / math.sqrt(D)
EPS = 1e-5

QCLAMP = 2.0 ** -40
EXPMASK = 0x7F800000
SIXSHIFT = 6 << 23
INVBASE = 254 << 23
MAGIC = 12582912.0
PRECLIP = 127.4

NENG = 5


def bc(ap2d, nblk, bk):
    return bass.AP(ap2d.tensor, ap2d.offset, [ap2d.ap[0], [ap2d.ap[1][0], nblk], [0, bk]])


def r3(ap2d, bk):
    return ap2d.rearrange("p (b k) -> p b k", k=bk)


def build(upto=99, dbg=()):
    nc = bass.Bass("TRN2", target_bir_lowering=False, debug=False)
    stack = contextlib.ExitStack()
    sems = {}

    def sem(name):
        if name not in sems:
            sems[name] = stack.enter_context(nc.semaphore(name))
        return sems[name]

    class Ring:

        def __init__(self, name, n):
            self.name, self.n = name, n
            self.issued = 0

        def issue(self, eng, out_ap, in_ap, **kw):
            i = self.issued
            slot, rnd = i % self.n, i // self.n
            s = sem(f"{self.name}{slot}")
            if rnd > 0:
                eng.wait_ge(s, 16 * rnd)
            eng.dma_start(out_ap, in_ap, **kw).then_inc(s, 16)
            self.issued += 1
            return i

        def wait(self, eng, i):
            eng.wait_ge(sem(f"{self.name}{i % self.n}"), 16 * (i // self.n + 1))

        def wait_all(self, eng):
            for slot in range(min(self.n, self.issued)):
                cnt = (self.issued - 1 - slot) // self.n + 1
                eng.wait_ge(sem(f"{self.name}{slot}"), 16 * cnt)

    R_SY = Ring("rsy", 16)
    R_GS = Ring("rgs", 8)
    R_GL = Ring("rgl", 4)
    R_SC = Ring("rsc", 6)

    def dram(name, shape, dtype, kind):
        return nc.dram_tensor(name, shape, dtype, kind=kind).ap()

    xs = dram("xs", [S, E], F32, "ExternalInput")
    wqt = dram("wqt", [E, E], F32, "ExternalInput")
    wkt = dram("wkt", [E, E], F32, "ExternalInput")
    wvt = dram("wvt", [E, E], F32, "ExternalInput")
    wot = dram("wot", [E, E], F32, "ExternalInput")
    w1 = dram("w1", [F1, E], F32, "ExternalInput")
    w2 = dram("w2", [E, F1], F32, "ExternalInput")
    bq = dram("bq", [E], F32, "ExternalInput")
    bkk = dram("bk", [E], F32, "ExternalInput")
    bv = dram("bv", [E], F32, "ExternalInput")
    bo = dram("bo", [E], F32, "ExternalInput")
    b1 = dram("b1", [F1], F32, "ExternalInput")
    b2 = dram("b2", [E], F32, "ExternalInput")
    b1_t = dram("b1_t", [128, F1], F32, "ExternalInput")
    lnw_t = dram("lnw_t", [128, E], F32, "ExternalInput")
    lnb_t = dram("lnb_t", [128, E], F32, "ExternalInput")
    cos_t = dram("cos_t", [S, D], F32, "ExternalInput")
    nsin_t = dram("nsin_t", [S, D], F32, "ExternalInput")

    out = dram("out", [TOWN, E], F32, "ExternalOutput")

    def scratch(name, shape, dtype):
        kind = "ExternalOutput" if name in dbg else "Internal"
        return dram(name, shape, dtype, kind)

    xn_h = scratch("xn_h", [S, E], BF16)
    q_tm = scratch("q_tm", [TOWN, E], BF16)
    k_tm = scratch("k_tm", [S, E], BF16)
    v_h = scratch("v_h", [H, S, D], BF16)
    nrm_h = scratch("nrm_h", [2, 128, H], F32)
    crow_h = scratch("crow_h", [H, 1], F32)
    av_h = scratch("av_h", [H, TOWN, D], BF16)
    ao_h = scratch("ao_h", [TOWN, E], BF16)
    xlnq_h = scratch("xlnq_h", [TOWN, E], BF16)
    gq_h = scratch("gq_h", [TOWN, F1], BF16)
    w1q_h = scratch("w1q_h", [F1, E], BF16)
    w2q_h = scratch("w2q_h", [E, F1], BF16)
    w2qT_h = scratch("w2qT_h", [F1, E], BF16)
    cdbg_h = scratch("cdbg_h", [1, 4 * H], F32)

    def sb(name, shape, dtype, st=stack):
        return st.enter_context(nc.sbuf_tensor(name, shape, dtype))

    def ps(name, shape, st):
        return st.enter_context(nc.psum_tensor(name, shape, F32))

    ones_r = sb("ones_r", [1, 512], F32)
    ones_b = sb("ones_b", [1, 512], BF16)
    cneg_s = sb("cneg_s", [128, H], F32)

    fence = sem("fence")
    FCT = [0]

    CHAIN_OPS = {
        "memset", "memzero", "tensor_copy", "tensor_tensor", "tensor_add",
        "tensor_sub", "tensor_mul", "tensor_max", "tensor_scalar",
        "tensor_single_scalar", "tensor_scalar_add", "tensor_scalar_sub",
        "tensor_scalar_mul", "tensor_scalar_max", "tensor_scalar_min",
        "tensor_relu", "tensor_reduce", "reduce_max", "reduce_sum",
        "scalar_tensor_tensor", "reciprocal", "select", "activation",
        "square", "sqrt", "copy", "add", "mul", "sign",
    }
    chain_state = {"scalar": 0, "vector": 0, "gpsimd": 0}

    ext_counts = {}
    chain_ext = {"scalar": None, "vector": None, "gpsimd": None}

    class ChainEng:

        def __init__(self, eng, name):
            self._eng = eng
            self._name = name
            self._sem = sem(f"ch_{name}")
            self._pending_sig = None

        def signal_next(self, signame):
            self._pending_sig = signame
            return self

        def __getattr__(self, attr):
            fn = getattr(self._eng, attr)
            if attr not in CHAIN_OPS:
                return fn

            def wrapped(*a, **k):
                c = chain_state[self._name]
                if c > 0:
                    self._eng.wait_ge(self._sem, c)
                ext = chain_ext[self._name]
                if ext is not None:
                    self._eng.wait_ge(sem(ext[0]), ext[1])
                    chain_ext[self._name] = None
                ins = fn(*a, **k)
                if self._pending_sig is not None:
                    nm = self._pending_sig
                    self._pending_sig = None
                    val = ext_counts.get(nm, 0) + 1
                    ext_counts[nm] = val
                    ins.then_inc(sem(nm), 1)
                    chain_ext[self._name] = (nm, val)
                else:
                    ins.then_inc(self._sem, 1)
                    chain_state[self._name] = c + 1
                return ins
            return wrapped

        def drain(self):
            return self._eng.drain()

    @contextlib.contextmanager
    def phase(name, run=True):
        class Engs:
            pass
        if not run:
            yield None
            return
        with nc.Block() as blk:
            bodies = {}

            class Reg:
                def __getattr__(self, eng):
                    def reg(fn):
                        bodies[eng] = fn
                        return fn
                    return reg
            yield Reg()
            base = FCT[0]
            for eng_name in ("gpsimd", "sync", "scalar", "vector", "tensor"):
                body = bodies.get(eng_name)
                dec = getattr(blk, eng_name)

                def make(body_fn, engn):
                    def _(eng):
                        eng.wait_ge(fence, base)
                        if body_fn is not None:
                            if engn in ("scalar", "vector", "gpsimd"):
                                body_fn(ChainEng(eng, engn))
                            else:
                                body_fn(eng)
                        eng.drain().then_inc(fence, 1)
                    return _
                dec(make(body, eng_name))
            FCT[0] = base + NENG

    def emit_quant2(v, src_ap, out_ap, nblk, am, scl, inv, xdv, y16, sig=None):
        fd = nblk * 32
        v.tensor_reduce(am[:, :nblk], r3(src_ap, 32), AX.X, ALU.max,
                        apply_absolute_value=True)
        v.tensor_scalar_max(am[:, :nblk], am[:, :nblk], QCLAMP)
        v.tensor_single_scalar(scl[:, :nblk].bitcast(I32),
                               am[:, :nblk].bitcast(I32), EXPMASK,
                               ALU.bitwise_and)
        v.tensor_single_scalar(scl[:, :nblk].bitcast(I32),
                               scl[:, :nblk].bitcast(I32), SIXSHIFT,
                               ALU.subtract)
        v.tensor_scalar(inv[:, :nblk].bitcast(I32), scl[:, :nblk].bitcast(I32),
                        -1, INVBASE, ALU.mult, ALU.add)
        v.tensor_tensor(r3(xdv[:, :fd], 32), r3(src_ap, 32),
                        bc(inv[:, :nblk], nblk, 32), ALU.mult)
        v.tensor_scalar(xdv[:, :fd], xdv[:, :fd], PRECLIP, MAGIC,
                        ALU.min, ALU.add)
        if sig is not None:
            v.signal_next(sig)
        ins = v.scalar_tensor_tensor(r3(out_ap, 32), r3(xdv[:, :fd], 32), MAGIC,
                                     bc(scl[:, :nblk], nblk, 32),
                                     ALU.subtract, ALU.mult)
        return ins

    with phase("ph0") as engs:
        @engs.vector
        def _(v):
            v.memset(ones_r[:], 1.0)
            v.memset(ones_b[:], 1.0)

    NT1 = S // 128
    ph1_st = contextlib.ExitStack()
    xbuf = sb("xbuf", [128, 2, E], F32, ph1_st)
    jnk = sb("jnk", [128, 2, E], F32, ph1_st)
    xnbuf = sb("xnbuf", [128, 2, E], BF16, ph1_st)
    stat = sb("stat", [128, 2, 8], F32, ph1_st)

    rec1 = {}
    with phase("ph1", run=(upto >= 1)) as engs:
        @engs.sync
        def _(sy):
            for i in range(NT1):
                if i >= 2:
                    sy.wait_ge(sem("s_af"), i - 1)
                rec1["x", i] = R_SY.issue(sy, xbuf[:, i % 2, :],
                                          xs[i * 128:(i + 1) * 128, :])

        @engs.scalar
        def _(sc):
            for i in range(NT1):
                pa = i % 2
                R_SY.wait(sc, rec1["x", i])
                xt = xbuf[:, pa, :]
                sc.activation(jnk[:, 0, :], xt, AF.Copy,
                              accum_out=stat[:, pa, 0:1])
                sc.signal_next("s_st").activation(jnk[:, 1, :], xt, AF.Square,
                              accum_out=stat[:, pa, 1:2])
                sc.wait_ge(sem("v_t"), i + 1)
                sc.signal_next("s_sq").activation(stat[:, pa, 7:8],
                                                     stat[:, pa, 6:7], AF.Sqrt)
                sc.wait_ge(sem("v_st"), i + 1)
                if i >= 2:
                    R_GS.wait(sc, rec1["xn", i - 2])
                sc.signal_next("s_af").activation(xnbuf[:, pa, :], xt, AF.Identity,
                              scale=stat[:, pa, 4:5], bias=stat[:, pa, 5:6])

        @engs.vector
        def _(v):
            inv_n = 1.0 / E
            for i in range(NT1):
                pa = i % 2
                v.wait_ge(sem("s_st"), i + 1)
                st_ = stat[:, pa]
                v.tensor_scalar_mul(st_[:, 2:3], st_[:, 0:1], inv_n)
                v.tensor_scalar_mul(st_[:, 3:4], st_[:, 1:2], inv_n)
                v.scalar_tensor_tensor(st_[:, 6:7], st_[:, 2:3], -1.0, st_[:, 2:3],
                                       ALU.mult, ALU.mult)
                v.scalar_tensor_tensor(st_[:, 6:7], st_[:, 3:4], 1.0, st_[:, 6:7],
                                       ALU.mult, ALU.add)
                v.signal_next("v_t").tensor_scalar_add(st_[:, 6:7], st_[:, 6:7],
                                                          EPS)
                v.wait_ge(sem("s_sq"), i + 1)
                v.reciprocal(st_[:, 4:5], st_[:, 7:8])
                for _ in range(2):
                    v.scalar_tensor_tensor(st_[:, 7:8], st_[:, 4:5], 1.0, st_[:, 4:5],
                                           ALU.mult, ALU.mult)
                    v.scalar_tensor_tensor(st_[:, 7:8], st_[:, 7:8], -0.5, st_[:, 6:7],
                                           ALU.mult, ALU.mult)
                    v.tensor_scalar_add(st_[:, 7:8], st_[:, 7:8], 1.5)
                    v.scalar_tensor_tensor(st_[:, 4:5], st_[:, 7:8], 1.0, st_[:, 4:5],
                                           ALU.mult, ALU.mult)
                v.signal_next("v_st").scalar_tensor_tensor(
                    st_[:, 5:6], st_[:, 2:3], -1.0, st_[:, 4:5],
                    ALU.mult, ALU.mult)

        @engs.gpsimd
        def _(g):
            for i in range(NT1):
                g.wait_ge(sem("s_af"), i + 1)
                rec1["xn", i] = R_GS.issue(g, xn_h[i * 128:(i + 1) * 128, :],
                                           xnbuf[:, i % 2, :])
            R_GS.wait_all(g)

    ph1_st.close()

    ph2_st = contextlib.ExitStack()
    wbuf = sb("wbuf", [128, NEC, E], BF16, ph2_st)
    xnt = sb("xnt", [128, 2, NEC, 512], BF16, ph2_st)
    t1b = sb("t1b", [128, E], F32, ph2_st)
    t2b = sb("t2b", [128, E], F32, ph2_st)
    nqt = sb("nqt", [128, H], F32, ph2_st)
    nqacc_q = sb("nqacc_q", [128, H], F32, ph2_st)
    nqacc_k = sb("nqacc_k", [128, H], F32, ph2_st)
    vtile = sb("vtile", [128, 2, E], BF16, ph2_st)
    cos_tm = sb("cos_tm", [128, 32, D], BF16, ph2_st)
    nsin_tm = sb("nsin_tm", [128, 32, D], BF16, ph2_st)
    bq_r = sb("bq_r", [1, E], BF16, ph2_st)
    bk_r = sb("bk_r", [1, E], BF16, ph2_st)
    bv_r = sb("bv_r", [1, E], BF16, ph2_st)
    crow_s = sb("crow_s", [1, H], F32, ph2_st)
    nrmt_s = sb("nrmt_s", [16, 2, 128], F32, ph2_st)
    mx_s = sb("mx_s", [16, 4], F32, ph2_st)

    vps = ps("vps", [128, 2, E], ph2_st)

    def hview(ap2d):
        return ap2d.rearrange("p (h d) -> p h d", d=D)

    def hbc(ap1, off, num):
        return bass.AP(ap1.tensor, ap1.offset + off, [ap1.ap[0], [0, H], [1, num]])

    def emit_wload(g, ring, w_hbm, rec, dstbuf=None):
        dstbuf = wbuf if dstbuf is None else dstbuf
        rec["wl"] = []
        for c4 in range(4):
            srcw = w_hbm[c4 * 512:(c4 + 1) * 512, :].rearrange(
                "(c p) f -> p c f", p=128)
            rec["wl"].append(ring.issue(g, dstbuf[:, c4 * 4:(c4 + 1) * 4, :], srcw))

    def proj_tm(tag, w_hbm, b_row, nslabs, kind, run):
        rec = {}
        NTC = nslabs * 4
        dst_tm = {"q": q_tm, "k": k_tm, "v": None}[kind]
        nqacc = {"q": nqacc_q, "k": nqacc_k, "v": None}[kind]

        with phase(f"ph2{tag}", run=run) as engs:
            if engs is None:
                return

            @engs.gpsimd
            def _(g):
                if kind == "q":
                    rec["cos"] = R_GL.issue(g, cos_tm[:],
                                            cos_t.rearrange("(c p) d -> p c d", p=128))
                    rec["nsin"] = R_GL.issue(g, nsin_tm[:],
                                             nsin_t.rearrange("(c p) d -> p c d", p=128))
                    bidx = [R_GL.issue(g, dst[:], srcb[None, :])
                            for dst, srcb in ((bq_r, bq), (bk_r, bkk), (bv_r, bv))]
                    for i in bidx:
                        R_GL.wait(g, i)
                    g.nop().then_inc(sem("bi_done"), 1)
                emit_wload(g, R_GL, w_hbm, rec)
                for tcix in range(NTC):
                    g.wait_ge(sem(f"v_ev{tag}"), tcix + 1)
                    rows = slice(tcix * 128, (tcix + 1) * 128)
                    if kind == "v":
                        for hh in range(H):
                            rec["st", tcix, hh] = R_GS.issue(
                                g, v_h[hh, rows, :],
                                vtile[:, tcix % 2, hh * D:(hh + 1) * D])
                    else:
                        rec["st", tcix] = R_GS.issue(g, dst_tm[rows, :],
                                                     vtile[:, tcix % 2, :])
                R_GS.wait_all(g)

            @engs.sync
            def _(sy):
                for j in range(nslabs):
                    if j >= 2:
                        sy.wait_ge(sem(f"t_sl{tag}"), j - 1)
                    for ec in range(NEC):
                        rec["xt", j, ec] = R_SY.issue(
                            sy, xnt[:, j % 2, ec, :],
                            xn_h[j * 512:(j + 1) * 512, ec * 128:(ec + 1) * 128],
                            transpose=True)

            @engs.tensor
            def _(t):
                for i in rec["wl"]:
                    R_GL.wait(t, i)
                if kind == "q":
                    t.wait_ge(sem("bi_done"), 1)
                for j in range(nslabs):
                    for ec in range(NEC):
                        R_SY.wait(t, rec["xt", j, ec])
                    for tc4 in range(4):
                        tcix = j * 4 + tc4
                        if tcix >= 2:
                            t.wait_ge(sem(f"v_ev{tag}"), tcix - 1)
                        pt = vps[:, tcix % 2, :]
                        tsl = slice(tc4 * 128, (tc4 + 1) * 128)
                        for fs in range(4):
                            po = pt[:, fs * 512:(fs + 1) * 512]
                            for ec in range(NEC):
                                t.matmul(po, xnt[:, j % 2, ec, tsl],
                                         wbuf[:, ec, fs * 512:(fs + 1) * 512],
                                         start=(ec == 0), stop=False)
                            mm = t.matmul(po, ones_b[0:1, 0:128],
                                          b_row[0:1, fs * 512:(fs + 1) * 512],
                                          start=False, stop=True)
                        mm.then_inc(sem(f"t_mm{tag}"), 1)
                    t.drain().then_inc(sem(f"t_sl{tag}"), 1)

            @engs.vector
            def _(v):
                if kind != "v":
                    v.memset(nqacc[:], 0.0)
                    R_GL.wait(v, rec["cos"] if "cos" in rec else 0)
                    R_GL.wait(v, rec["nsin"] if "nsin" in rec else 0)
                for tcix in range(NTC):
                    pa = tcix % 2
                    v.wait_ge(sem(f"t_mm{tag}"), tcix + 1)
                    if tcix >= 2:
                        if kind == "v":
                            for hh in range(H):
                                R_GS.wait(v, rec["st", tcix - 2, hh])
                        else:
                            R_GS.wait(v, rec["st", tcix - 2])
                    pt = vps[:, pa, :]
                    if kind == "v":
                        for fs in range(4):
                            e2 = v.signal_next(f"v_ev{tag}") if fs == 3 else v
                            e2.tensor_copy(vtile[:, pa, fs * 512:(fs + 1) * 512],
                                           pt[:, fs * 512:(fs + 1) * 512])
                    else:
                        cosc = cos_tm[:, tcix, :]
                        nsinc = nsin_tm[:, tcix, :]
                        pt3 = hview(pt)
                        v.tensor_tensor(hview(t1b[:]), pt3, hbc(cosc, 0, D),
                                        ALU.mult)
                        ob = hview(t2b[:])
                        oa = bass.AP(ob.tensor, ob.offset, [ob.ap[0], [D, H], [1, 64]])
                        ia = bass.AP(pt3.tensor, pt3.offset + 64,
                                     [pt3.ap[0], [D, H], [1, 64]])
                        v.tensor_tensor(oa, ia, hbc(nsinc, 0, 64), ALU.mult)
                        ob2 = bass.AP(ob.tensor, ob.offset + 64,
                                      [ob.ap[0], [D, H], [1, 64]])
                        ib2 = bass.AP(pt3.tensor, pt3.offset,
                                      [pt3.ap[0], [D, H], [1, 64]])
                        v.tensor_tensor(ob2, ib2, hbc(nsinc, 64, 64), ALU.mult)
                        ro = vtile[:, pa, :]
                        v.tensor_add(ro, t1b[:], t2b[:])
                        v.tensor_mul(t1b[:], ro, ro)
                        v.tensor_reduce(nqt[:], hview(t1b[:]), AX.X, ALU.add)
                        v.signal_next(f"v_ev{tag}").tensor_max(
                            nqacc[:], nqacc[:], nqt[:])

    proj_tm("q", wqt, bq_r, TOWN // 512, "q", run=(upto >= 2))
    proj_tm("k", wkt, bk_r, S // 512, "k", run=(upto >= 2))
    proj_tm("v", wvt, bv_r, S // 512, "v", run=(upto >= 2))

    with phase("ph2d1", run=(upto >= 2)) as engs:
        if engs is not None:
            @engs.gpsimd
            def _(g):
                i1 = R_GS.issue(g, nrm_h[0], nqacc_q[:])
                i2 = R_GS.issue(g, nrm_h[1], nqacc_k[:])
                R_GS.wait(g, i1)
                R_GS.wait(g, i2)

    rec2d = {}
    with phase("ph2d2", run=(upto >= 2)) as engs:
        if engs is not None:
            @engs.sync
            def _(sy):
                for i in range(2):
                    rec2d["nl", i] = R_SY.issue(
                        sy, nrmt_s[:, i, :], nrm_h[i].rearrange("t h -> h t"),
                        allow_slow_non_contiguous=True)

            @engs.vector
            def _(v):
                R_SY.wait(v, rec2d["nl", 0])
                R_SY.wait(v, rec2d["nl", 1])
                v.tensor_reduce(mx_s[:, 0:1], nrmt_s[:, 0, :], AX.X, ALU.max)
                v.tensor_reduce(mx_s[:, 1:2], nrmt_s[:, 1, :], AX.X, ALU.max)
                v.signal_next("cd1").tensor_mul(mx_s[:, 2:3], mx_s[:, 0:1],
                                                 mx_s[:, 1:2])
                v.wait_ge(sem("cd2"), 1)
                v.signal_next("cd3").tensor_scalar_mul(mx_s[:, 3:4], mx_s[:, 3:4],
                                                        -INVSQ)

            @engs.scalar
            def _(sc):
                sc.wait_ge(sem("cd1"), 1)
                sc.signal_next("cd2").activation(mx_s[:, 3:4], mx_s[:, 2:3],
                                                  AF.Sqrt)

            @engs.gpsimd
            def _(g):
                g.wait_ge(sem("cd3"), 1)
                R_GS.wait(g, R_GS.issue(g, crow_h, mx_s[:, 3:4]))

    with phase("ph2d3", run=(upto >= 2)) as engs:
        if engs is not None:
            @engs.sync
            def _(sy):
                rec2d["cl"] = R_SY.issue(sy, crow_s[:],
                                         crow_h.rearrange("h one -> one h"),
                                         allow_slow_non_contiguous=True)

            @engs.tensor
            def _(t):
                R_SY.wait(t, rec2d["cl"])
                t.matmul(vps[:, 0, 0:H], ones_r[0:1, 0:128], crow_s[:],
                         start=True, stop=True).then_inc(sem("cd4"), 1)

            @engs.vector
            def _(v):
                v.wait_ge(sem("cd4"), 1)
                v.tensor_copy(cneg_s[:], vps[:, 0, 0:H])

    ph2_st.close()

    qw_chunks = []
    for r in range(F1 // 128):
        for c in range(2):
            qw_chunks.append((w1[r * 128:(r + 1) * 128, c * 1024:(c + 1) * 1024],
                              w1q_h[r * 128:(r + 1) * 128, c * 1024:(c + 1) * 1024]))
    for r in range(E // 128):
        for c in range(8):
            qw_chunks.append((w2[r * 128:(r + 1) * 128, c * 1024:(c + 1) * 1024],
                              w2q_h[r * 128:(r + 1) * 128, c * 1024:(c + 1) * 1024]))
    NQWC = len(qw_chunks)

    ph34_st = contextlib.ExitStack()
    av_all = sb("av_all", [128, H, TOWN], BF16, ph34_st)
    wo_s = sb("wo_s", [128, NEC, E], BF16, ph34_st)
    rec34 = {}

    ph3_st = contextlib.ExitStack()
    qw_in = sb("qw_in", [128, 4, 1024], F32, ph3_st)
    qxdv = sb("qxdv", [128, 1024], F32, ph3_st)
    qout = sb("qout", [128, 2, 1024], BF16, ph3_st)
    q3am = sb("q3am", [128, 32], F32, ph3_st)
    q3sc = sb("q3sc", [128, 32], F32, ph3_st)
    q3iv = sb("q3iv", [128, 32], F32, ph3_st)
    kth = sb("kth", [128, 2, S], BF16, ph3_st)
    qth = sb("qth", [128, 2, TOWN], BF16, ph3_st)
    vh3 = sb("vh3", [128, 2, 32, D], BF16, ph3_st)
    p_t = sb("p_t", [128, 4, 512], BF16, ph3_st)
    onec = sb("onec", [128, 1], BF16, ph3_st)
    rsum_s = sb("rsum_s", [1, 512], F32, ph3_st)
    rsb = sb("rsb", [128, 512], F32, ph3_st)
    sps = ps("sps", [128, 3, 512], ph3_st)
    avp = ps("avp", [128, 2, 512], ph3_st)
    ssum = ps("ssum", [128, 2, 512], ph3_st)
    rps = ps("rps", [128, 512], ph3_st)

    NKC = S // 128
    NKP = NKC // 2
    NQB = TOWN // 512

    with phase("ph3", run=(upto >= 3)) as engs:
        if engs is not None:
            @engs.gpsimd
            def _(g):
                g.signal_next("g_one").memset(onec[:], 1.0)
                emit_wload(g, R_GL, wot, rec34, dstbuf=wo_s)
                for i in range(NQWC):
                    g.wait_ge(sem("v_qw"), i + 1)
                    rec34["qwo", i] = R_GS.issue(g, qw_chunks[i][1],
                                                 qout[:, i % 2, :])
                R_GS.wait_all(g)

            @engs.sync
            def _(sy):
                for hh in range(H):
                    if hh >= 2:
                        sy.wait_ge(sem("t_h"), hh - 1)
                    pa = hh % 2
                    ld = [R_SY.issue(sy, kth[:, pa, :],
                                     k_tm[:, hh * 128:(hh + 1) * 128],
                                     transpose=True),
                          R_SY.issue(sy, qth[:, pa, :],
                                     q_tm[:, hh * 128:(hh + 1) * 128],
                                     transpose=True)]
                    for c8 in range(4):
                        ld.append(R_SY.issue(
                            sy, vh3[:, pa, c8 * 8:(c8 + 1) * 8, :],
                            v_h[hh, c8 * 1024:(c8 + 1) * 1024, :]
                            .rearrange("(c p) d -> p c d", p=128)))
                    rec34["ld", hh] = ld
                    for c4 in range(16):
                        i = hh * 16 + c4
                        if i < NQWC:
                            if i >= 4:
                                sy.wait_ge(sem("v_qw"), i - 3)
                            rec34["qwi", i] = R_SY.issue(sy, qw_in[:, i % 4, :],
                                                         qw_chunks[i][0])

            def av_one(t, hh, qb, kc, sums_first=False):
                gqi = hh * NQB + qb
                kcg = gqi * NKC + kc
                pslice = p_t[:, kcg % 4, :]
                if sums_first:
                    t.matmul(ssum[0:1, gqi % 2, :], onec[:], pslice,
                             start=(kc == 0), stop=(kc == NKC - 1))
                    t.matmul(avp[:, gqi % 2, :], vh3[:, hh % 2, kc, :], pslice,
                             start=(kc == 0), stop=(kc == NKC - 1)) \
                        .then_inc(sem("t_av"), 1)
                else:
                    t.matmul(avp[:, gqi % 2, :], vh3[:, hh % 2, kc, :], pslice,
                             start=(kc == 0), stop=(kc == NKC - 1))
                    t.matmul(ssum[0:1, gqi % 2, :], onec[:], pslice,
                             start=(kc == 0), stop=(kc == NKC - 1)) \
                        .then_inc(sem("t_av"), 1)

            @engs.tensor
            def _(t):
                t.wait_ge(sem("g_one"), 1)
                NGQ = H * NQB

                def prev(gqi):
                    return (gqi - 1) // NQB, (gqi - 1) % NQB

                for hh in range(H):
                    for i in rec34["ld", hh]:
                        R_SY.wait(t, i)
                    for qb in range(NQB):
                        gqi = hh * NQB + qb
                        for kc in range(NKC):
                            kcg = gqi * NKC + kc
                            if kcg >= 3:
                                t.wait_ge(sem("s_p"), kcg - 2)
                            t.matmul(sps[:, kcg % 3, :],
                                     kth[:, hh % 2, kc * 128:(kc + 1) * 128],
                                     qth[:, hh % 2, qb * 512:(qb + 1) * 512],
                                     start=True, stop=True).then_inc(sem("t_s"), 1)
                            if kc == 0:
                                if gqi >= 1:
                                    ph, pq = prev(gqi)
                                    t.wait_ge(sem("s_p"), gqi * NKC - 1)
                                    av_one(t, ph, pq, NKC - 2)
                            elif kc == 1:
                                if gqi >= 1:
                                    ph, pq = prev(gqi)
                                    t.wait_ge(sem("s_p"), gqi * NKC)
                                    av_one(t, ph, pq, NKC - 1)
                                    if qb == 0:
                                        t.drain().then_inc(sem("t_h"), 1)
                                    t.wait_ge(sem("v_rcp"), gqi)
                                    t.matmul(rps[:], ones_r[0:1, 0:128], rsum_s[:],
                                             start=True, stop=True) \
                                        .then_inc(sem("t_r"), 1)
                            else:
                                if kc == 2 and gqi >= 1:
                                    t.wait_ge(sem("v_av"), gqi - 1)
                                t.wait_ge(sem("s_p"), kcg - 1)
                                av_one(t, hh, qb, kc - 2, sums_first=(kc % 2 == 1))
                t.wait_ge(sem("s_p"), NGQ * NKC - 1)
                av_one(t, H - 1, NQB - 1, NKC - 2)
                t.wait_ge(sem("s_p"), NGQ * NKC)
                av_one(t, H - 1, NQB - 1, NKC - 1)
                t.wait_ge(sem("v_rcp"), NGQ)
                t.matmul(rps[:], ones_r[0:1, 0:128], rsum_s[:],
                         start=True, stop=True).then_inc(sem("t_r"), 1)

            @engs.scalar
            def _(sc):
                for gqi in range(H * NQB):
                    hh = gqi // NQB
                    for kc in range(NKC):
                        kcg = gqi * NKC + kc
                        sc.wait_ge(sem("t_s"), kcg + 1)
                        if kcg >= 4:
                            sc.wait_ge(sem("t_av"), kcg - 3)
                        sc.signal_next("s_p").activation(
                            p_t[:, kcg % 4, :], sps[:, kcg % 3, :],
                            AF.Exp, bias=cneg_s[:, hh:hh + 1], scale=INVSQ)

            @engs.vector
            def _(v):
                for gqi in range(H * NQB):
                    hh, qb = gqi // NQB, gqi % NQB
                    v.wait_ge(sem("t_av"), (gqi + 1) * NKC)
                    v.signal_next("v_rcp").tensor_copy(rsum_s[:],
                                                       ssum[0:1, gqi % 2, :])
                    v.wait_ge(sem("t_r"), gqi + 1)
                    v.reciprocal(rsb[:], rps[:])
                    v.signal_next("v_av").tensor_tensor(
                        av_all[:, hh, qb * 512:(qb + 1) * 512],
                        avp[:, gqi % 2, :], rsb[:], ALU.mult)
                    for c4 in range(4):
                        i = gqi * 4 + c4
                        if i < NQWC:
                            R_SY.wait(v, rec34["qwi", i])
                            if i >= 2:
                                R_GS.wait(v, rec34["qwo", i - 2])
                            emit_quant2(v, qw_in[:, i % 4, :], qout[:, i % 2, :],
                                        32, q3am, q3sc, q3iv, qxdv, None,
                                        sig="v_qw")

    ph3_st.close()

    ph4_st = contextlib.ExitStack()
    aot = sb("aot", [128, 2, E], BF16, ph4_st)
    bo_r = sb("bo_r", [1, E], F32, ph4_st)
    aops = ps("aops", [128, 2, E], ph4_st)

    with phase("ph4", run=(upto >= 4)) as engs:
        if engs is not None:
            @engs.gpsimd
            def _(g):
                for tcg in range(TOWN // 128):
                    g.wait_ge(sem("v_eo"), tcg + 1)
                    rec34["aostore", tcg] = R_GS.issue(
                        g, ao_h[tcg * 128:(tcg + 1) * 128, :], aot[:, tcg % 2, :])
                R_GS.wait_all(g)

            @engs.sync
            def _(sy):
                rec34["bo"] = R_SY.issue(sy, bo_r[:], bo[None, :])

            @engs.tensor
            def _(t):
                for i in rec34["wl"]:
                    R_GL.wait(t, i)
                R_SY.wait(t, rec34["bo"])
                for tcg in range(TOWN // 128):
                    if tcg >= 2:
                        t.wait_ge(sem("v_eo"), tcg - 1)
                    pt = aops[:, tcg % 2, :]
                    tsl = slice((tcg % 4) * 128 + (tcg // 4) * 512,
                                (tcg % 4) * 128 + (tcg // 4) * 512 + 128)
                    for os_ in range(4):
                        po = pt[:, os_ * 512:(os_ + 1) * 512]
                        for hh in range(H):
                            t.matmul(po, av_all[:, hh, tcg * 128:(tcg + 1) * 128],
                                     wo_s[:, hh, os_ * 512:(os_ + 1) * 512],
                                     start=(hh == 0), stop=False)
                        mm = t.matmul(po, ones_r[0:1, 0:128],
                                      bo_r[0:1, os_ * 512:(os_ + 1) * 512],
                                      start=False, stop=True)
                    mm.then_inc(sem("t_mo"), 1)

            @engs.vector
            def _(v):
                for tcg in range(TOWN // 128):
                    v.wait_ge(sem("t_mo"), tcg + 1)
                    if tcg >= 2:
                        R_GS.wait(v, rec34["aostore", tcg - 2])
                    for fs in range(4):
                        e2 = v.signal_next("v_eo") if fs == 3 else v
                        e2.tensor_copy(aot[:, tcg % 2, fs * 512:(fs + 1) * 512],
                                       aops[:, tcg % 2, fs * 512:(fs + 1) * 512])

    ph4_st.close()
    ph34_st.close()

    rec5b = {}
    ph5b_outer = contextlib.ExitStack()
    xlqt = sb("xlqt", [128, NEC, TOWN], BF16, ph5b_outer)
    ph5a_st = contextlib.ExitStack()
    xft = sb("xft", [128, 2, E], BF16, ph5a_st)
    jnk5 = sb("jnk5", [128, 2, E], F32, ph5a_st)
    xlnp = sb("xlnp", [128, E], F32, ph5a_st)
    xlnw = sb("xlnw", [128, E], F32, ph5a_st)
    xln5 = sb("xln5", [128, E], BF16, ph5a_st)
    stat5 = sb("stat5", [128, 2, 8], F32, ph5a_st)
    lnw_s = sb("lnw_s", [128, E], F32, ph5a_st)
    lnb_s = sb("lnb_s", [128, E], F32, ph5a_st)
    q5am = sb("q5am", [128, 64], F32, ph5a_st)
    q5sc = sb("q5sc", [128, 64], F32, ph5a_st)
    q5iv = sb("q5iv", [128, 64], F32, ph5a_st)
    q5dv = sb("q5dv", [128, E], F32, ph5a_st)
    q5y = sb("q5y", [128, E], F16, ph5a_st)
    xlq5 = sb("xlq5", [128, 2, E], BF16, ph5a_st)


    rec5 = {}
    NT5 = TOWN // 128

    with phase("ph5a", run=(upto >= 5)) as engs:
        if engs is not None:
            @engs.gpsimd
            def _(g):
                for i in range(NT5):
                    g.wait_ge(sem("v_x5"), i + 1)
                    rec5["xq", i] = R_GS.issue(
                        g, xlnq_h[i * 128:(i + 1) * 128, :], xlq5[:, i % 2, :])
                R_GS.wait_all(g)

            @engs.sync
            def _(sy):
                rec5["lnw"] = R_SY.issue(sy, lnw_s[:], lnw_t)
                rec5["lnb"] = R_SY.issue(sy, lnb_s[:], lnb_t)
                for i in range(NT5):
                    if i >= 2:
                        sy.wait_ge(sem("v_x5"), i - 1)
                    rec5["xf", i] = R_SY.issue(sy, xft[:, i % 2, :],
                                               xn_h[i * 128:(i + 1) * 128, :])
                    R_GS.wait(sy, rec5["xq", i])
                    for ec in range(NEC):
                        rec5b["xlt", ec, i] = R_SY.issue(
                            sy, xlqt[:, ec, i * 128:(i + 1) * 128],
                            xlnq_h[i * 128:(i + 1) * 128, ec * 128:(ec + 1) * 128],
                            transpose=True)

            @engs.scalar
            def _(sc):
                for i in range(NT5):
                    pa = i % 2
                    R_SY.wait(sc, rec5["xf", i])
                    xt = xft[:, pa, :]
                    sc.activation(jnk5[:, 0, :], xt, AF.Copy,
                                  accum_out=stat5[:, pa, 0:1])
                    sc.signal_next("s_st5").activation(
                        jnk5[:, 1, :], xt, AF.Square, accum_out=stat5[:, pa, 1:2])
                    sc.wait_ge(sem("v_t5"), i + 1)
                    sc.signal_next("s_sq5").activation(stat5[:, pa, 7:8],
                                                        stat5[:, pa, 6:7], AF.Sqrt)
                    sc.wait_ge(sem("v_st5"), i + 1)
                    sc.signal_next("s_af5").activation(
                        xlnp[:], xt, AF.Identity,
                        scale=stat5[:, pa, 4:5], bias=stat5[:, pa, 5:6])

            @engs.vector
            def _(v):
                R_SY.wait(v, rec5["lnw"])
                R_SY.wait(v, rec5["lnb"])
                inv_n = 1.0 / E
                for i in range(NT5):
                    pa = i % 2
                    st_ = stat5[:, pa]
                    v.wait_ge(sem("s_st5"), i + 1)
                    v.tensor_scalar_mul(st_[:, 2:3], st_[:, 0:1], inv_n)
                    v.tensor_scalar_mul(st_[:, 3:4], st_[:, 1:2], inv_n)
                    v.scalar_tensor_tensor(st_[:, 6:7], st_[:, 2:3], -1.0,
                                           st_[:, 2:3], ALU.mult, ALU.mult)
                    v.scalar_tensor_tensor(st_[:, 6:7], st_[:, 3:4], 1.0,
                                           st_[:, 6:7], ALU.mult, ALU.add)
                    v.signal_next("v_t5").tensor_scalar_add(st_[:, 6:7],
                                                             st_[:, 6:7], EPS)
                    v.wait_ge(sem("s_sq5"), i + 1)
                    v.reciprocal(st_[:, 4:5], st_[:, 7:8])
                    for _ in range(2):
                        v.scalar_tensor_tensor(st_[:, 7:8], st_[:, 4:5], 1.0,
                                               st_[:, 4:5], ALU.mult, ALU.mult)
                        v.scalar_tensor_tensor(st_[:, 7:8], st_[:, 7:8], -0.5,
                                               st_[:, 6:7], ALU.mult, ALU.mult)
                        v.tensor_scalar_add(st_[:, 7:8], st_[:, 7:8], 1.5)
                        v.scalar_tensor_tensor(st_[:, 4:5], st_[:, 7:8], 1.0,
                                               st_[:, 4:5], ALU.mult, ALU.mult)
                    v.signal_next("v_st5").scalar_tensor_tensor(
                        st_[:, 5:6], st_[:, 2:3], -1.0, st_[:, 4:5],
                        ALU.mult, ALU.mult)
                    v.wait_ge(sem("s_af5"), i + 1)
                    v.tensor_mul(xlnw[:], xlnp[:], lnw_s[:])
                    v.tensor_add(xln5[:], xlnw[:], lnb_s[:])
                    if i >= 2:
                        R_GS.wait(v, rec5["xq", i - 2])
                    emit_quant2(v, xln5[:], xlq5[:, pa, :], 64,
                                q5am, q5sc, q5iv, q5dv, q5y, sig="v_x5")

    ph5a_st.close()

    ph5b_st = contextlib.ExitStack()
    w1t = sb("w1t", [128, 2, NEC, 512], BF16, ph5b_st)
    g1b = sb("g1b", [128, 2, 512], BF16, ph5b_st)
    g2b = sb("g2b", [128, 2, 512], BF16, ph5b_st)
    gqb = sb("gqb", [128, 2, 512], BF16, ph5b_st)
    b1s = sb("b1s", [128, F1], F32, ph5b_st)
    wbt = sb("wbt", [128, 4, 512], BF16, ph5b_st)
    q6am = sb("q6am", [128, 2, 16], F32, ph5b_st)
    q6sc = sb("q6sc", [128, 2, 16], F32, ph5b_st)
    q6iv = sb("q6iv", [128, 2, 16], F32, ph5b_st)
    q6dv = sb("q6dv", [128, 2, 512], F32, ph5b_st)
    h1ps = ps("h1ps", [128, 2, 512], ph5b_st)

    wbl_ring = Ring("wbl", 4)
    NFS = F1 // 512
    NTC5 = TOWN // 128
    NG5 = NFS * NTC5
    NWB = (E * F1 * 2) // (512 * 128 * 2)

    with phase("ph5b", run=(upto >= 5)) as engs:
        if engs is not None:
            @engs.gpsimd
            def _(g):
                for i in range(NG5):
                    g.wait_ge(sem("v_gq"), i + 1)
                    fs, tcb = i // NTC5, i % NTC5
                    rows = slice(tcb * 128, (tcb + 1) * 128)
                    rec5b["gq", i] = R_GS.issue(
                        g, gq_h[rows, fs * 512:(fs + 1) * 512], gqb[:, i % 2, :])
                    wbl_ring.wait(g, i)
                    os_, fc = i // 64, i % 64
                    rec5b["wbo", i] = R_GS.issue(
                        g, w2qT_h[fc * 128:(fc + 1) * 128,
                                  os_ * 512:(os_ + 1) * 512], wbt[:, i % 4, :])
                R_GS.wait_all(g)

            @engs.sync
            def _(sy):
                rec5b["b1"] = R_SY.issue(sy, b1s[:], b1_t)
                for fs in range(NFS):
                    if fs >= 2:
                        sy.wait_ge(sem("t_fs"), fs - 1)
                    for ec in range(NEC):
                        rec5b["w1", fs, ec] = R_SY.issue(
                            sy, w1t[:, fs % 2, ec, :],
                            w1q_h[fs * 512:(fs + 1) * 512,
                                  ec * 128:(ec + 1) * 128],
                            transpose=True)
                    for k in range(16):
                        i = fs * 16 + k
                        os_, fc = i // 64, i % 64
                        if i >= 4:
                            R_GS.wait(sy, rec5b["wbo", i - 4])
                        wbl_ring.issue(sy, wbt[:, i % 4, :],
                                       w2q_h[os_ * 512:(os_ + 1) * 512,
                                             fc * 128:(fc + 1) * 128],
                                       transpose=True)


            @engs.tensor
            def _(t):
                for ec in range(NEC):
                    for i in range(NT5):
                        R_SY.wait(t, rec5b["xlt", ec, i])
                for fs in range(NFS):
                    for ec in range(NEC):
                        R_SY.wait(t, rec5b["w1", fs, ec])
                    for tcb in range(NTC5):
                        gidx = fs * NTC5 + tcb
                        if gidx >= 2:
                            t.wait_ge(sem("v_e1"), gidx - 1)
                        po = h1ps[:, gidx % 2, :]
                        for ec in range(NEC):
                            mm1 = t.matmul(po, xlqt[:, ec, tcb * 128:(tcb + 1) * 128],
                                           w1t[:, fs % 2, ec, :],
                                           start=(ec == 0), stop=(ec == NEC - 1))
                        mm1.then_inc(sem("t_m1"), 1)
                    t.drain().then_inc(sem("t_fs"), 1)

            @engs.scalar
            def _(sc):
                for gidx in range(NG5):
                    pa = gidx % 2
                    sc.wait_ge(sem("v_e1"), gidx + 1)
                    if gidx >= 2:
                        sc.wait_ge(sem("v_gq"), gidx - 1)
                    sc.signal_next("s_g1").activation(g2b[:, pa, :], g1b[:, pa, :],
                                                       AF.Gelu_apprx_tanh)

            @engs.vector
            def _(v):
                R_SY.wait(v, rec5b["b1"])
                for gidx in range(NG5):
                    pa = gidx % 2
                    v.wait_ge(sem("t_m1"), gidx + 1)
                    if gidx >= 2:
                        v.wait_ge(sem("s_g1"), gidx - 1)
                    v.signal_next("v_e1").tensor_tensor(
                        g1b[:, pa, :], h1ps[:, pa, :],
                        b1s[:, (gidx // NTC5) * 512:(gidx // NTC5 + 1) * 512],
                        ALU.add)
                    if gidx >= 1:
                        v.wait_ge(sem("s_g1"), gidx)
                        if gidx >= 3:
                            R_GS.wait(v, rec5b["gq", gidx - 3])
                        emit_quant2(v, g2b[:, (gidx - 1) % 2, :],
                                    gqb[:, (gidx - 1) % 2, :], 16,
                                    q6am[:, (gidx - 1) % 2, :],
                                    q6sc[:, (gidx - 1) % 2, :],
                                    q6iv[:, (gidx - 1) % 2, :],
                                    q6dv[:, (gidx - 1) % 2, :], None, sig="v_gq")
                g_last = NG5 - 1
                v.wait_ge(sem("s_g1"), g_last + 1)
                R_GS.wait(v, rec5b["gq", g_last - 2])
                emit_quant2(v, g2b[:, g_last % 2, :],
                            gqb[:, g_last % 2, :], 16,
                            q6am[:, g_last % 2, :], q6sc[:, g_last % 2, :],
                            q6iv[:, g_last % 2, :], q6dv[:, g_last % 2, :], None,
                            sig="v_gq")

    ph5b_st.close()
    ph5b_outer.close()

    ph5c_st = contextlib.ExitStack()
    gqt = sb("gqt", [128, 64, 512], BF16, ph5c_st)
    rec5c = {}
    w2t = sb("w2t", [128, 2, 32, 512], BF16, ph5c_st)
    h2s = sb("h2s", [128, 4, E], BF16, ph5c_st)
    xf6 = sb("xf6", [128, 2, E], BF16, ph5c_st)
    xr6 = sb("xr6", [128, 2, E], F32, ph5c_st)
    ao6 = sb("ao6", [128, 2, E], BF16, ph5c_st)
    ob6 = sb("ob6", [128, 2, E], F32, ph5c_st)
    fo6 = sb("fo6", [128, E], BF16, ph5c_st)
    b2_r = sb("b2_r", [1, E], F32, ph5c_st)
    h2ps = ps("h2ps", [128, 4, 512], ph5c_st)

    with phase("ph5c", run=(upto >= 5)) as engs:
        if engs is not None:
            @engs.gpsimd
            def _(g):
                for gt in range(16):
                    g.wait_ge(sem("v_fin"), gt + 1)
                    rec5c["out", gt] = R_GS.issue(
                        g, out[gt * 128:(gt + 1) * 128, :], ob6[:, gt % 2, :])
                R_GS.wait_all(g)

            @engs.sync
            def _(sy):
                rec5c["b2"] = R_SY.issue(sy, b2_r[:], b2[None, :])
                for tsb in range(4):
                    if tsb >= 1:
                        sy.wait_ge(sem("t_tsb"), tsb)
                    for fc in range(64):
                        rec5c["gqt", tsb, fc] = R_SY.issue(
                            sy, gqt[:, fc, :],
                            gq_h[tsb * 512:(tsb + 1) * 512,
                                 fc * 128:(fc + 1) * 128],
                            transpose=True)

                    for os_ in range(4):
                        gos = tsb * 4 + os_
                        for hf in range(0, 2, 2):
                            pass
                        hf = 0
                        if gos >= 1:
                            sy.wait_ge(sem("t_hf"), (gos - 1) * 2 + hf + 1)
                        for fc in range(32):
                            rec5c["w2", gos, hf, fc] = R_SY.issue(
                                sy, w2t[:, hf, fc, :],
                                w2qT_h[(hf * 32 + fc) * 128:
                                       (hf * 32 + fc + 1) * 128,
                                       os_ * 512:(os_ + 1) * 512])

            @engs.scalar
            def _(sc):
                for tsb in range(4):
                    for os_ in range(4):
                        gos = tsb * 4 + os_
                        hf = 1
                        if gos >= 1:
                            sc.wait_ge(sem("t_hf"), (gos - 1) * 2 + hf + 1)
                        for fc in range(32):
                            rec5c["w2", gos, hf, fc] = R_SC.issue(
                                sc, w2t[:, hf, fc, :],
                                w2qT_h[(hf * 32 + fc) * 128:
                                       (hf * 32 + fc + 1) * 128,
                                       os_ * 512:(os_ + 1) * 512])
                    for tt_ in range(4):
                        gt = tsb * 4 + tt_
                        if gt >= 2:
                            sc.wait_ge(sem("v_fin"), gt - 1)
                        rows = slice(gt * 128, (gt + 1) * 128)
                        rec5c["xr", gt] = R_SC.issue(sc, xr6[:, gt % 2, :], xs[rows, :])
                        rec5c["aor", gt] = R_SC.issue(sc, ao6[:, gt % 2, :], ao_h[rows, :])
                        rec5c["xfr", gt] = R_SC.issue(sc, xf6[:, gt % 2, :], xn_h[rows, :])

            @engs.tensor
            def _(t):
                R_SY.wait(t, rec5c["b2"])
                for tsb in range(4):
                    for fc in range(64):
                        R_SY.wait(t, rec5c["gqt", tsb, fc])
                    for os_ in range(4):
                        gos = tsb * 4 + os_
                        for hf in range(2):
                            w2r = R_SY if hf == 0 else R_SC
                            for ii in range(32):
                                w2r.wait(t, rec5c["w2", gos, hf, ii])
                            for tt_ in range(4):
                                gidx = gos * 4 + tt_
                                if hf == 0 and gos >= 1:
                                    t.wait_ge(sem("v_e2"), (gos - 1) * 4 + tt_ + 1)
                                po = h2ps[:, tt_, :]
                                for fc in range(32):
                                    t.matmul(po, gqt[:, hf * 32 + fc,
                                                     tt_ * 128:(tt_ + 1) * 128],
                                             w2t[:, hf, fc, :],
                                             start=(hf == 0 and fc == 0),
                                             stop=False)
                            t.drain().then_inc(sem("t_hf"), 1)
                        for tt_ in range(4):
                            t.matmul(h2ps[:, tt_, :], ones_r[0:1, 0:128],
                                     b2_r[0:1, os_ * 512:(os_ + 1) * 512],
                                     start=False, stop=True) \
                                .then_inc(sem("t_m2"), 1)
                    t.drain().then_inc(sem("t_tsb"), 1)

            @engs.vector
            def _(v):
                for tsb in range(4):
                    for os_ in range(4):
                        for tt_ in range(4):
                            gidx = (tsb * 4 + os_) * 4 + tt_
                            v.wait_ge(sem("t_m2"), gidx + 1)
                            v.signal_next("v_e2").tensor_copy(
                                h2s[:, tt_, os_ * 512:(os_ + 1) * 512],
                                h2ps[:, tt_, :])
                    for tt_ in range(4):
                        gt = tsb * 4 + tt_
                        R_SC.wait(v, rec5c["xr", gt])
                        R_SC.wait(v, rec5c["aor", gt])
                        R_SC.wait(v, rec5c["xfr", gt])
                        if gt >= 2:
                            R_GS.wait(v, rec5c["out", gt - 2])
                        v.tensor_add(fo6[:], xf6[:, gt % 2, :], h2s[:, tt_, :])
                        v.scalar_tensor_tensor(ob6[:, gt % 2, :], ao6[:, gt % 2, :],
                                               0.0, xr6[:, gt % 2, :],
                                               ALU.bypass, ALU.add)
                        v.signal_next("v_fin").tensor_add(ob6[:, gt % 2, :],
                                                           ob6[:, gt % 2, :], fo6[:])


    ph5c_st.close()

    stack.close()
    return nc




_ROPE_BASE = 10000.0


def _rope_tables():
    inv_freq = 1.0 / _ROPE_BASE ** (np.arange(0, D, 2, dtype=np.float32) / D)
    pos = np.arange(S, dtype=np.float32)
    freqs = np.outer(pos, inv_freq)
    emb = np.concatenate([freqs, freqs], axis=-1)
    cos = np.cos(emb).astype(np.float32)
    sin = np.sin(emb).astype(np.float32)
    nsin = np.concatenate([-sin[:, :64], sin[:, 64:]], axis=1)
    return cos, nsin


def _stage(inputs, core, shared, cos, nsin):
    b, h = core // 2, core % 2
    x = inputs["L0_attn_x"]
    perm = np.concatenate([np.arange(h * TOWN, (h + 1) * TOWN),
                           np.arange((1 - h) * TOWN, (2 - h) * TOWN)])
    d = dict(shared)
    d["xs"] = np.ascontiguousarray(x[b][perm])
    d["cos_t"] = np.ascontiguousarray(cos[perm])
    d["nsin_t"] = np.ascontiguousarray(nsin[perm])
    return d


def kernel(**inputs):
    from concourse.bass_utils import run_bass_kernel_spmd
    inputs = {k: np.ascontiguousarray(np.asarray(v, dtype=np.float32))
              for k, v in inputs.items()}
    shared = {
        "wqt": np.ascontiguousarray(inputs["L0_attn_Wq"].T),
        "wkt": np.ascontiguousarray(inputs["L0_attn_Wk"].T),
        "wvt": np.ascontiguousarray(inputs["L0_attn_Wv"].T),
        "wot": np.ascontiguousarray(inputs["L0_attn_Wo"].T),
        "w1": inputs["L0_ffn_W1"],
        "w2": inputs["L0_ffn_W2"],
        "bq": inputs["L0_attn_bq"],
        "bk": inputs["L0_attn_bk"],
        "bv": inputs["L0_attn_bv"],
        "bo": inputs["L0_attn_bo"],
        "b1": inputs["L0_ffn_b1"],
        "b2": inputs["L0_ffn_b2"],
        "b1_t": np.ascontiguousarray(
            np.broadcast_to(inputs["L0_ffn_b1"], (128, F1))),
        "lnw_t": np.ascontiguousarray(
            np.broadcast_to(inputs["L0_ffn_ln_w"], (128, E))),
        "lnb_t": np.ascontiguousarray(
            np.broadcast_to(inputs["L0_ffn_ln_b"], (128, E))),
    }
    cos, nsin = _rope_tables()
    in_maps = [_stage(inputs, c, shared, cos, nsin) for c in range(8)]
    nc = build()
    res = run_bass_kernel_spmd(nc, in_maps, core_ids=list(range(8)))
    outp = np.empty((B, S, E), dtype=np.float32)
    for c in range(8):
        b, h = c // 2, c % 2
        outp[b, h * TOWN:(h + 1) * TOWN, :] = res.results[c]["out"]
    return outp


# version 16
# speedup vs baseline: 1.0696x; 1.0696x over previous
import math
import contextlib
import numpy as np
import concourse.bass as bass
import concourse.mybir as mybir

F32 = mybir.dt.float32
BF16 = mybir.dt.bfloat16
F16 = mybir.dt.float16
I32 = mybir.dt.int32
AF = mybir.ActivationFunctionType
ALU = mybir.AluOpType
AX = mybir.AxisListType

B, S, E, H, D = 4, 4096, 2048, 16, 128
F1 = 8192
TOWN = 2048
NEC = E // 128
INVSQ = 1.0 / math.sqrt(D)
EPS = 1e-5

QCLAMP = 2.0 ** -40
EXPMASK = 0x7F800000
SIXSHIFT = 6 << 23
INVBASE = 254 << 23
MAGIC = 12582912.0
PRECLIP = 127.4

NENG = 5


def bc(ap2d, nblk, bk):
    return bass.AP(ap2d.tensor, ap2d.offset, [ap2d.ap[0], [ap2d.ap[1][0], nblk], [0, bk]])


def r3(ap2d, bk):
    return ap2d.rearrange("p (b k) -> p b k", k=bk)


def build(upto=99, dbg=()):
    nc = bass.Bass("TRN2", target_bir_lowering=False, debug=False)
    stack = contextlib.ExitStack()
    sems = {}

    def sem(name):
        if name not in sems:
            sems[name] = stack.enter_context(nc.semaphore(name))
        return sems[name]

    class Ring:

        def __init__(self, name, n):
            self.name, self.n = name, n
            self.issued = 0

        def issue(self, eng, out_ap, in_ap, **kw):
            i = self.issued
            slot, rnd = i % self.n, i // self.n
            s = sem(f"{self.name}{slot}")
            if rnd > 0:
                eng.wait_ge(s, 16 * rnd)
            eng.dma_start(out_ap, in_ap, **kw).then_inc(s, 16)
            self.issued += 1
            return i

        def wait(self, eng, i):
            eng.wait_ge(sem(f"{self.name}{i % self.n}"), 16 * (i // self.n + 1))

        def wait_all(self, eng):
            for slot in range(min(self.n, self.issued)):
                cnt = (self.issued - 1 - slot) // self.n + 1
                eng.wait_ge(sem(f"{self.name}{slot}"), 16 * cnt)

    R_SY = Ring("rsy", 16)
    R_GS = Ring("rgs", 8)
    R_GL = Ring("rgl", 4)
    R_SC = Ring("rsc", 6)

    def dram(name, shape, dtype, kind):
        return nc.dram_tensor(name, shape, dtype, kind=kind).ap()

    xs = dram("xs", [S, E], F32, "ExternalInput")
    wqt = dram("wqt", [E, E], F32, "ExternalInput")
    wkt = dram("wkt", [E, E], F32, "ExternalInput")
    wvt = dram("wvt", [E, E], F32, "ExternalInput")
    wot = dram("wot", [E, E], F32, "ExternalInput")
    w1 = dram("w1", [F1, E], F32, "ExternalInput")
    w2 = dram("w2", [E, F1], F32, "ExternalInput")
    bq = dram("bq", [E], F32, "ExternalInput")
    bkk = dram("bk", [E], F32, "ExternalInput")
    bv = dram("bv", [E], F32, "ExternalInput")
    bo = dram("bo", [E], F32, "ExternalInput")
    b1 = dram("b1", [F1], F32, "ExternalInput")
    b2 = dram("b2", [E], F32, "ExternalInput")
    b1_t = dram("b1_t", [128, F1], F32, "ExternalInput")
    lnw_t = dram("lnw_t", [128, E], F32, "ExternalInput")
    lnb_t = dram("lnb_t", [128, E], F32, "ExternalInput")
    cos_t = dram("cos_t", [S, D], F32, "ExternalInput")
    nsin_t = dram("nsin_t", [S, D], F32, "ExternalInput")

    out = dram("out", [TOWN, E], F32, "ExternalOutput")

    def scratch(name, shape, dtype):
        kind = "ExternalOutput" if name in dbg else "Internal"
        return dram(name, shape, dtype, kind)

    xn_h = scratch("xn_h", [S, E], BF16)
    q_tm = scratch("q_tm", [TOWN, E], BF16)
    k_tm = scratch("k_tm", [S, E], BF16)
    v_h = scratch("v_h", [H, S, D], BF16)
    nrm_h = scratch("nrm_h", [2, 128, H], F32)
    crow_h = scratch("crow_h", [H, 1], F32)
    av_h = scratch("av_h", [H, TOWN, D], BF16)
    ao_h = scratch("ao_h", [TOWN, E], BF16)
    xlnq_h = scratch("xlnq_h", [TOWN, E], BF16)
    gq_h = scratch("gq_h", [TOWN, F1], BF16)
    w1q_h = scratch("w1q_h", [F1, E], BF16)
    w2q_h = scratch("w2q_h", [E, F1], BF16)
    w2qT_h = scratch("w2qT_h", [F1, E], BF16)
    cdbg_h = scratch("cdbg_h", [1, 4 * H], F32)

    def sb(name, shape, dtype, st=stack):
        return st.enter_context(nc.sbuf_tensor(name, shape, dtype))

    def ps(name, shape, st):
        return st.enter_context(nc.psum_tensor(name, shape, F32))

    ones_r = sb("ones_r", [1, 512], F32)
    ones_b = sb("ones_b", [1, 512], BF16)
    cneg_s = sb("cneg_s", [128, H], F32)

    fence = sem("fence")
    FCT = [0]

    CHAIN_OPS = {
        "memset", "memzero", "tensor_copy", "tensor_tensor", "tensor_add",
        "tensor_sub", "tensor_mul", "tensor_max", "tensor_scalar",
        "tensor_single_scalar", "tensor_scalar_add", "tensor_scalar_sub",
        "tensor_scalar_mul", "tensor_scalar_max", "tensor_scalar_min",
        "tensor_relu", "tensor_reduce", "reduce_max", "reduce_sum",
        "scalar_tensor_tensor", "reciprocal", "select", "activation",
        "square", "sqrt", "copy", "add", "mul", "sign",
    }
    chain_state = {"scalar": 0, "vector": 0, "gpsimd": 0}

    ext_counts = {}
    chain_ext = {"scalar": None, "vector": None, "gpsimd": None}

    class ChainEng:

        def __init__(self, eng, name):
            self._eng = eng
            self._name = name
            self._sem = sem(f"ch_{name}")
            self._pending_sig = None

        def signal_next(self, signame):
            self._pending_sig = signame
            return self

        def __getattr__(self, attr):
            fn = getattr(self._eng, attr)
            if attr not in CHAIN_OPS:
                return fn

            def wrapped(*a, **k):
                c = chain_state[self._name]
                if c > 0:
                    self._eng.wait_ge(self._sem, c)
                ext = chain_ext[self._name]
                if ext is not None:
                    self._eng.wait_ge(sem(ext[0]), ext[1])
                    chain_ext[self._name] = None
                ins = fn(*a, **k)
                if self._pending_sig is not None:
                    nm = self._pending_sig
                    self._pending_sig = None
                    val = ext_counts.get(nm, 0) + 1
                    ext_counts[nm] = val
                    ins.then_inc(sem(nm), 1)
                    chain_ext[self._name] = (nm, val)
                else:
                    ins.then_inc(self._sem, 1)
                    chain_state[self._name] = c + 1
                return ins
            return wrapped

        def drain(self):
            return self._eng.drain()

    @contextlib.contextmanager
    def phase(name, run=True):
        class Engs:
            pass
        if not run:
            yield None
            return
        with nc.Block() as blk:
            bodies = {}

            class Reg:
                def __getattr__(self, eng):
                    def reg(fn):
                        bodies[eng] = fn
                        return fn
                    return reg
            yield Reg()
            base = FCT[0]
            for eng_name in ("gpsimd", "sync", "scalar", "vector", "tensor"):
                body = bodies.get(eng_name)
                dec = getattr(blk, eng_name)

                def make(body_fn, engn):
                    def _(eng):
                        eng.wait_ge(fence, base)
                        if body_fn is not None:
                            if engn in ("scalar", "vector", "gpsimd"):
                                body_fn(ChainEng(eng, engn))
                            else:
                                body_fn(eng)
                        eng.drain().then_inc(fence, 1)
                    return _
                dec(make(body, eng_name))
            FCT[0] = base + NENG

    def emit_quant2(v, src_ap, out_ap, nblk, am, scl, inv, xdv, y16, sig=None):
        fd = nblk * 32
        v.tensor_reduce(am[:, :nblk], r3(src_ap, 32), AX.X, ALU.max,
                        apply_absolute_value=True)
        v.tensor_scalar_max(am[:, :nblk], am[:, :nblk], QCLAMP)
        v.tensor_single_scalar(scl[:, :nblk].bitcast(I32),
                               am[:, :nblk].bitcast(I32), EXPMASK,
                               ALU.bitwise_and)
        v.tensor_single_scalar(scl[:, :nblk].bitcast(I32),
                               scl[:, :nblk].bitcast(I32), SIXSHIFT,
                               ALU.subtract)
        v.tensor_scalar(inv[:, :nblk].bitcast(I32), scl[:, :nblk].bitcast(I32),
                        -1, INVBASE, ALU.mult, ALU.add)
        v.tensor_tensor(r3(xdv[:, :fd], 32), r3(src_ap, 32),
                        bc(inv[:, :nblk], nblk, 32), ALU.mult)
        v.tensor_scalar(xdv[:, :fd], xdv[:, :fd], PRECLIP, MAGIC,
                        ALU.min, ALU.add)
        if sig is not None:
            v.signal_next(sig)
        ins = v.scalar_tensor_tensor(r3(out_ap, 32), r3(xdv[:, :fd], 32), MAGIC,
                                     bc(scl[:, :nblk], nblk, 32),
                                     ALU.subtract, ALU.mult)
        return ins

    with phase("ph0") as engs:
        @engs.vector
        def _(v):
            v.memset(ones_r[:], 1.0)
            v.memset(ones_b[:], 1.0)

    NT1 = S // 128
    ph1_st = contextlib.ExitStack()
    xbuf = sb("xbuf", [128, 2, E], F32, ph1_st)
    jnk = sb("jnk", [128, 2, E], F32, ph1_st)
    xnbuf = sb("xnbuf", [128, 2, E], BF16, ph1_st)
    stat = sb("stat", [128, 2, 8], F32, ph1_st)

    rec1 = {}
    with phase("ph1", run=(upto >= 1)) as engs:
        @engs.sync
        def _(sy):
            for i in range(NT1):
                if i >= 2:
                    sy.wait_ge(sem("s_af"), i - 1)
                rec1["x", i] = R_SY.issue(sy, xbuf[:, i % 2, :],
                                          xs[i * 128:(i + 1) * 128, :])

        @engs.scalar
        def _(sc):
            for i in range(NT1):
                pa = i % 2
                R_SY.wait(sc, rec1["x", i])
                xt = xbuf[:, pa, :]
                sc.activation(jnk[:, 0, :], xt, AF.Copy,
                              accum_out=stat[:, pa, 0:1])
                sc.signal_next("s_st").activation(jnk[:, 1, :], xt, AF.Square,
                              accum_out=stat[:, pa, 1:2])
                sc.wait_ge(sem("v_t"), i + 1)
                sc.signal_next("s_sq").activation(stat[:, pa, 7:8],
                                                     stat[:, pa, 6:7], AF.Sqrt)
                sc.wait_ge(sem("v_st"), i + 1)
                if i >= 2:
                    R_GS.wait(sc, rec1["xn", i - 2])
                sc.signal_next("s_af").activation(xnbuf[:, pa, :], xt, AF.Identity,
                              scale=stat[:, pa, 4:5], bias=stat[:, pa, 5:6])

        @engs.vector
        def _(v):
            inv_n = 1.0 / E
            for i in range(NT1):
                pa = i % 2
                v.wait_ge(sem("s_st"), i + 1)
                st_ = stat[:, pa]
                v.tensor_scalar_mul(st_[:, 2:3], st_[:, 0:1], inv_n)
                v.tensor_scalar_mul(st_[:, 3:4], st_[:, 1:2], inv_n)
                v.scalar_tensor_tensor(st_[:, 6:7], st_[:, 2:3], -1.0, st_[:, 2:3],
                                       ALU.mult, ALU.mult)
                v.scalar_tensor_tensor(st_[:, 6:7], st_[:, 3:4], 1.0, st_[:, 6:7],
                                       ALU.mult, ALU.add)
                v.signal_next("v_t").tensor_scalar_add(st_[:, 6:7], st_[:, 6:7],
                                                          EPS)
                v.wait_ge(sem("s_sq"), i + 1)
                v.reciprocal(st_[:, 4:5], st_[:, 7:8])
                for _ in range(2):
                    v.scalar_tensor_tensor(st_[:, 7:8], st_[:, 4:5], 1.0, st_[:, 4:5],
                                           ALU.mult, ALU.mult)
                    v.scalar_tensor_tensor(st_[:, 7:8], st_[:, 7:8], -0.5, st_[:, 6:7],
                                           ALU.mult, ALU.mult)
                    v.tensor_scalar_add(st_[:, 7:8], st_[:, 7:8], 1.5)
                    v.scalar_tensor_tensor(st_[:, 4:5], st_[:, 7:8], 1.0, st_[:, 4:5],
                                           ALU.mult, ALU.mult)
                v.signal_next("v_st").scalar_tensor_tensor(
                    st_[:, 5:6], st_[:, 2:3], -1.0, st_[:, 4:5],
                    ALU.mult, ALU.mult)

        @engs.gpsimd
        def _(g):
            for i in range(NT1):
                g.wait_ge(sem("s_af"), i + 1)
                rec1["xn", i] = R_GS.issue(g, xn_h[i * 128:(i + 1) * 128, :],
                                           xnbuf[:, i % 2, :])
            R_GS.wait_all(g)

    ph1_st.close()

    ph2_st = contextlib.ExitStack()
    wbuf = sb("wbuf", [128, NEC, E], BF16, ph2_st)
    xnt = sb("xnt", [128, 2, NEC, 512], BF16, ph2_st)
    t1b = sb("t1b", [128, E], F32, ph2_st)
    t2b = sb("t2b", [128, E], F32, ph2_st)
    nqt = sb("nqt", [128, H], F32, ph2_st)
    nqacc_q = sb("nqacc_q", [128, H], F32, ph2_st)
    nqacc_k = sb("nqacc_k", [128, H], F32, ph2_st)
    vtile = sb("vtile", [128, 2, E], BF16, ph2_st)
    cos_tm = sb("cos_tm", [128, 32, D], BF16, ph2_st)
    nsin_tm = sb("nsin_tm", [128, 32, D], BF16, ph2_st)
    bq_r = sb("bq_r", [1, E], BF16, ph2_st)
    bk_r = sb("bk_r", [1, E], BF16, ph2_st)
    bv_r = sb("bv_r", [1, E], BF16, ph2_st)
    crow_s = sb("crow_s", [1, H], F32, ph2_st)
    nrmt_s = sb("nrmt_s", [16, 2, 128], F32, ph2_st)
    mx_s = sb("mx_s", [16, 4], F32, ph2_st)

    vps = ps("vps", [128, 2, E], ph2_st)

    def hview(ap2d):
        return ap2d.rearrange("p (h d) -> p h d", d=D)

    def hbc(ap1, off, num):
        return bass.AP(ap1.tensor, ap1.offset + off, [ap1.ap[0], [0, H], [1, num]])

    def emit_wload(g, ring, w_hbm, rec, dstbuf=None):
        dstbuf = wbuf if dstbuf is None else dstbuf
        rec["wl"] = []
        for c4 in range(4):
            srcw = w_hbm[c4 * 512:(c4 + 1) * 512, :].rearrange(
                "(c p) f -> p c f", p=128)
            rec["wl"].append(ring.issue(g, dstbuf[:, c4 * 4:(c4 + 1) * 4, :], srcw))

    def proj_tm(tag, w_hbm, b_row, nslabs, kind, run):
        rec = {}
        NTC = nslabs * 4
        dst_tm = {"q": q_tm, "k": k_tm, "v": None}[kind]
        nqacc = {"q": nqacc_q, "k": nqacc_k, "v": None}[kind]

        with phase(f"ph2{tag}", run=run) as engs:
            if engs is None:
                return

            @engs.gpsimd
            def _(g):
                if kind == "q":
                    rec["cos"] = R_GL.issue(g, cos_tm[:],
                                            cos_t.rearrange("(c p) d -> p c d", p=128))
                    rec["nsin"] = R_GL.issue(g, nsin_tm[:],
                                             nsin_t.rearrange("(c p) d -> p c d", p=128))
                    bidx = [R_GL.issue(g, dst[:], srcb[None, :])
                            for dst, srcb in ((bq_r, bq), (bk_r, bkk), (bv_r, bv))]
                    for i in bidx:
                        R_GL.wait(g, i)
                    g.nop().then_inc(sem("bi_done"), 1)
                emit_wload(g, R_GL, w_hbm, rec)
                for tcix in range(NTC):
                    g.wait_ge(sem(f"v_ev{tag}"), tcix + 1)
                    rows = slice(tcix * 128, (tcix + 1) * 128)
                    if kind == "v":
                        for hh in range(H):
                            rec["st", tcix, hh] = R_GS.issue(
                                g, v_h[hh, rows, :],
                                vtile[:, tcix % 2, hh * D:(hh + 1) * D])
                    else:
                        rec["st", tcix] = R_GS.issue(g, dst_tm[rows, :],
                                                     vtile[:, tcix % 2, :])
                R_GS.wait_all(g)

            @engs.sync
            def _(sy):
                for j in range(nslabs):
                    if j >= 2:
                        sy.wait_ge(sem(f"t_sl{tag}"), j - 1)
                    for ec in range(NEC):
                        rec["xt", j, ec] = R_SY.issue(
                            sy, xnt[:, j % 2, ec, :],
                            xn_h[j * 512:(j + 1) * 512, ec * 128:(ec + 1) * 128],
                            transpose=True)

            @engs.tensor
            def _(t):
                for i in rec["wl"]:
                    R_GL.wait(t, i)
                if kind == "q":
                    t.wait_ge(sem("bi_done"), 1)
                for j in range(nslabs):
                    for ec in range(NEC):
                        R_SY.wait(t, rec["xt", j, ec])
                    for tc4 in range(4):
                        tcix = j * 4 + tc4
                        if tcix >= 2:
                            t.wait_ge(sem(f"v_ev{tag}"), tcix - 1)
                        pt = vps[:, tcix % 2, :]
                        tsl = slice(tc4 * 128, (tc4 + 1) * 128)
                        for fs in range(4):
                            po = pt[:, fs * 512:(fs + 1) * 512]
                            for ec in range(NEC):
                                t.matmul(po, xnt[:, j % 2, ec, tsl],
                                         wbuf[:, ec, fs * 512:(fs + 1) * 512],
                                         start=(ec == 0), stop=False)
                            mm = t.matmul(po, ones_b[0:1, 0:128],
                                          b_row[0:1, fs * 512:(fs + 1) * 512],
                                          start=False, stop=True)
                        mm.then_inc(sem(f"t_mm{tag}"), 1)
                    t.drain().then_inc(sem(f"t_sl{tag}"), 1)

            @engs.vector
            def _(v):
                if kind != "v":
                    v.memset(nqacc[:], 0.0)
                    R_GL.wait(v, rec["cos"] if "cos" in rec else 0)
                    R_GL.wait(v, rec["nsin"] if "nsin" in rec else 0)
                for tcix in range(NTC):
                    pa = tcix % 2
                    v.wait_ge(sem(f"t_mm{tag}"), tcix + 1)
                    if tcix >= 2:
                        if kind == "v":
                            for hh in range(H):
                                R_GS.wait(v, rec["st", tcix - 2, hh])
                        else:
                            R_GS.wait(v, rec["st", tcix - 2])
                    pt = vps[:, pa, :]
                    if kind == "v":
                        for fs in range(4):
                            e2 = v.signal_next(f"v_ev{tag}") if fs == 3 else v
                            e2.tensor_copy(vtile[:, pa, fs * 512:(fs + 1) * 512],
                                           pt[:, fs * 512:(fs + 1) * 512])
                    else:
                        cosc = cos_tm[:, tcix, :]
                        nsinc = nsin_tm[:, tcix, :]
                        pt3 = hview(pt)
                        v.tensor_tensor(hview(t1b[:]), pt3, hbc(cosc, 0, D),
                                        ALU.mult)
                        ob = hview(t2b[:])
                        oa = bass.AP(ob.tensor, ob.offset, [ob.ap[0], [D, H], [1, 64]])
                        ia = bass.AP(pt3.tensor, pt3.offset + 64,
                                     [pt3.ap[0], [D, H], [1, 64]])
                        v.tensor_tensor(oa, ia, hbc(nsinc, 0, 64), ALU.mult)
                        ob2 = bass.AP(ob.tensor, ob.offset + 64,
                                      [ob.ap[0], [D, H], [1, 64]])
                        ib2 = bass.AP(pt3.tensor, pt3.offset,
                                      [pt3.ap[0], [D, H], [1, 64]])
                        v.tensor_tensor(ob2, ib2, hbc(nsinc, 64, 64), ALU.mult)
                        ro = vtile[:, pa, :]
                        v.tensor_add(ro, t1b[:], t2b[:])
                        v.tensor_mul(t1b[:], ro, ro)
                        v.tensor_reduce(nqt[:], hview(t1b[:]), AX.X, ALU.add)
                        v.signal_next(f"v_ev{tag}").tensor_max(
                            nqacc[:], nqacc[:], nqt[:])

    proj_tm("q", wqt, bq_r, TOWN // 512, "q", run=(upto >= 2))
    proj_tm("k", wkt, bk_r, S // 512, "k", run=(upto >= 2))
    proj_tm("v", wvt, bv_r, S // 512, "v", run=(upto >= 2))

    with phase("ph2d1", run=(upto >= 2)) as engs:
        if engs is not None:
            @engs.gpsimd
            def _(g):
                i1 = R_GS.issue(g, nrm_h[0], nqacc_q[:])
                i2 = R_GS.issue(g, nrm_h[1], nqacc_k[:])
                R_GS.wait(g, i1)
                R_GS.wait(g, i2)

    rec2d = {}
    with phase("ph2d2", run=(upto >= 2)) as engs:
        if engs is not None:
            @engs.sync
            def _(sy):
                for i in range(2):
                    rec2d["nl", i] = R_SY.issue(
                        sy, nrmt_s[:, i, :], nrm_h[i].rearrange("t h -> h t"),
                        allow_slow_non_contiguous=True)

            @engs.vector
            def _(v):
                R_SY.wait(v, rec2d["nl", 0])
                R_SY.wait(v, rec2d["nl", 1])
                v.tensor_reduce(mx_s[:, 0:1], nrmt_s[:, 0, :], AX.X, ALU.max)
                v.tensor_reduce(mx_s[:, 1:2], nrmt_s[:, 1, :], AX.X, ALU.max)
                v.signal_next("cd1").tensor_mul(mx_s[:, 2:3], mx_s[:, 0:1],
                                                 mx_s[:, 1:2])
                v.wait_ge(sem("cd2"), 1)
                v.signal_next("cd3").tensor_scalar_mul(mx_s[:, 3:4], mx_s[:, 3:4],
                                                        -INVSQ)

            @engs.scalar
            def _(sc):
                sc.wait_ge(sem("cd1"), 1)
                sc.signal_next("cd2").activation(mx_s[:, 3:4], mx_s[:, 2:3],
                                                  AF.Sqrt)

            @engs.gpsimd
            def _(g):
                g.wait_ge(sem("cd3"), 1)
                R_GS.wait(g, R_GS.issue(g, crow_h, mx_s[:, 3:4]))

    with phase("ph2d3", run=(upto >= 2)) as engs:
        if engs is not None:
            @engs.sync
            def _(sy):
                rec2d["cl"] = R_SY.issue(sy, crow_s[:],
                                         crow_h.rearrange("h one -> one h"),
                                         allow_slow_non_contiguous=True)

            @engs.tensor
            def _(t):
                R_SY.wait(t, rec2d["cl"])
                t.matmul(vps[:, 0, 0:H], ones_r[0:1, 0:128], crow_s[:],
                         start=True, stop=True).then_inc(sem("cd4"), 1)

            @engs.vector
            def _(v):
                v.wait_ge(sem("cd4"), 1)
                v.tensor_copy(cneg_s[:], vps[:, 0, 0:H])

    ph2_st.close()

    qw_chunks = []
    for r in range(F1 // 128):
        for c in range(2):
            qw_chunks.append((w1[r * 128:(r + 1) * 128, c * 1024:(c + 1) * 1024],
                              w1q_h[r * 128:(r + 1) * 128, c * 1024:(c + 1) * 1024]))
    for r in range(E // 128):
        for c in range(8):
            qw_chunks.append((w2[r * 128:(r + 1) * 128, c * 1024:(c + 1) * 1024],
                              w2q_h[r * 128:(r + 1) * 128, c * 1024:(c + 1) * 1024]))
    NQWC = len(qw_chunks)

    ph34_st = contextlib.ExitStack()
    av_all = sb("av_all", [128, H, TOWN], BF16, ph34_st)
    wo_s = sb("wo_s", [128, NEC, E], BF16, ph34_st)
    rec34 = {}

    ph3_st = contextlib.ExitStack()
    qw_in = sb("qw_in", [128, 4, 1024], F32, ph3_st)
    qxdv = sb("qxdv", [128, 1024], F32, ph3_st)
    qout = sb("qout", [128, 2, 1024], BF16, ph3_st)
    q3am = sb("q3am", [128, 32], F32, ph3_st)
    q3sc = sb("q3sc", [128, 32], F32, ph3_st)
    q3iv = sb("q3iv", [128, 32], F32, ph3_st)
    kth = sb("kth", [128, 2, S], BF16, ph3_st)
    qth = sb("qth", [128, 2, TOWN], BF16, ph3_st)
    vh3 = sb("vh3", [128, 2, 32, D], BF16, ph3_st)
    p_t = sb("p_t", [128, 4, 512], BF16, ph3_st)
    onec = sb("onec", [128, 1], BF16, ph3_st)
    rsum_s = sb("rsum_s", [1, 512], F32, ph3_st)
    rsb = sb("rsb", [128, 512], F32, ph3_st)
    sps = ps("sps", [128, 3, 512], ph3_st)
    avp = ps("avp", [128, 2, 512], ph3_st)
    ssum = ps("ssum", [128, 2, 512], ph3_st)
    rps = ps("rps", [128, 512], ph3_st)

    NKC = S // 128
    NKP = NKC // 2
    NQB = TOWN // 512

    with phase("ph3", run=(upto >= 3)) as engs:
        if engs is not None:
            @engs.gpsimd
            def _(g):
                g.signal_next("g_one").memset(onec[:], 1.0)
                emit_wload(g, R_GL, wot, rec34, dstbuf=wo_s)
                for i in range(NQWC):
                    g.wait_ge(sem("v_qw"), i + 1)
                    rec34["qwo", i] = R_GS.issue(g, qw_chunks[i][1],
                                                 qout[:, i % 2, :])
                R_GS.wait_all(g)

            @engs.sync
            def _(sy):
                for hh in range(H):
                    if hh >= 2:
                        sy.wait_ge(sem("t_h"), hh - 1)
                    pa = hh % 2
                    ld = [R_SY.issue(sy, kth[:, pa, :],
                                     k_tm[:, hh * 128:(hh + 1) * 128],
                                     transpose=True),
                          R_SY.issue(sy, qth[:, pa, :],
                                     q_tm[:, hh * 128:(hh + 1) * 128],
                                     transpose=True)]
                    for c8 in range(4):
                        ld.append(R_SY.issue(
                            sy, vh3[:, pa, c8 * 8:(c8 + 1) * 8, :],
                            v_h[hh, c8 * 1024:(c8 + 1) * 1024, :]
                            .rearrange("(c p) d -> p c d", p=128)))
                    rec34["ld", hh] = ld
                    for c4 in range(16):
                        i = hh * 16 + c4
                        if i < NQWC:
                            if i >= 4:
                                sy.wait_ge(sem("v_qw"), i - 3)
                            rec34["qwi", i] = R_SY.issue(sy, qw_in[:, i % 4, :],
                                                         qw_chunks[i][0])

            def av_one(t, hh, qb, kc, sums_first=False):
                gqi = hh * NQB + qb
                kcg = gqi * NKC + kc
                pslice = p_t[:, kcg % 4, :]
                if sums_first:
                    t.matmul(ssum[0:1, gqi % 2, :], onec[:], pslice,
                             start=(kc == 0), stop=(kc == NKC - 1))
                    t.matmul(avp[:, gqi % 2, :], vh3[:, hh % 2, kc, :], pslice,
                             start=(kc == 0), stop=(kc == NKC - 1)) \
                        .then_inc(sem("t_av"), 1)
                else:
                    t.matmul(avp[:, gqi % 2, :], vh3[:, hh % 2, kc, :], pslice,
                             start=(kc == 0), stop=(kc == NKC - 1))
                    t.matmul(ssum[0:1, gqi % 2, :], onec[:], pslice,
                             start=(kc == 0), stop=(kc == NKC - 1)) \
                        .then_inc(sem("t_av"), 1)

            @engs.tensor
            def _(t):
                t.wait_ge(sem("g_one"), 1)
                NGQ = H * NQB

                def prev(gqi):
                    return (gqi - 1) // NQB, (gqi - 1) % NQB

                for hh in range(H):
                    for i in rec34["ld", hh]:
                        R_SY.wait(t, i)
                    for qb in range(NQB):
                        gqi = hh * NQB + qb
                        for kc in range(NKC):
                            kcg = gqi * NKC + kc
                            if kcg >= 3:
                                t.wait_ge(sem("s_p"), kcg - 2)
                            t.matmul(sps[:, kcg % 3, :],
                                     kth[:, hh % 2, kc * 128:(kc + 1) * 128],
                                     qth[:, hh % 2, qb * 512:(qb + 1) * 512],
                                     start=True, stop=True).then_inc(sem("t_s"), 1)
                            if kc == 0:
                                if gqi >= 1:
                                    ph, pq = prev(gqi)
                                    t.wait_ge(sem("s_p"), gqi * NKC - 1)
                                    av_one(t, ph, pq, NKC - 2)
                            elif kc == 1:
                                if gqi >= 1:
                                    ph, pq = prev(gqi)
                                    t.wait_ge(sem("s_p"), gqi * NKC)
                                    av_one(t, ph, pq, NKC - 1)
                                    if qb == 0:
                                        t.drain().then_inc(sem("t_h"), 1)
                                    t.wait_ge(sem("v_rcp"), gqi)
                                    t.matmul(rps[:], ones_r[0:1, 0:128], rsum_s[:],
                                             start=True, stop=True) \
                                        .then_inc(sem("t_r"), 1)
                            else:
                                if kc == 2 and gqi >= 1:
                                    t.wait_ge(sem("v_av"), gqi - 1)
                                t.wait_ge(sem("s_p"), kcg - 1)
                                av_one(t, hh, qb, kc - 2)
                t.wait_ge(sem("s_p"), NGQ * NKC - 1)
                av_one(t, H - 1, NQB - 1, NKC - 2)
                t.wait_ge(sem("s_p"), NGQ * NKC)
                av_one(t, H - 1, NQB - 1, NKC - 1)
                t.wait_ge(sem("v_rcp"), NGQ)
                t.matmul(rps[:], ones_r[0:1, 0:128], rsum_s[:],
                         start=True, stop=True).then_inc(sem("t_r"), 1)

            @engs.scalar
            def _(sc):
                for gqi in range(H * NQB):
                    hh = gqi // NQB
                    for kc in range(NKC):
                        kcg = gqi * NKC + kc
                        sc.wait_ge(sem("t_s"), kcg + 1)
                        if kcg >= 4:
                            sc.wait_ge(sem("t_av"), kcg - 3)
                        sc.signal_next("s_p").activation(
                            p_t[:, kcg % 4, :], sps[:, kcg % 3, :],
                            AF.Exp, bias=cneg_s[:, hh:hh + 1], scale=INVSQ)

            @engs.vector
            def _(v):
                for gqi in range(H * NQB):
                    hh, qb = gqi // NQB, gqi % NQB
                    v.wait_ge(sem("t_av"), (gqi + 1) * NKC)
                    v.signal_next("v_rcp").tensor_copy(rsum_s[:],
                                                       ssum[0:1, gqi % 2, :])
                    v.wait_ge(sem("t_r"), gqi + 1)
                    v.reciprocal(rsb[:], rps[:])
                    v.signal_next("v_av").tensor_tensor(
                        av_all[:, hh, qb * 512:(qb + 1) * 512],
                        avp[:, gqi % 2, :], rsb[:], ALU.mult)
                    for c4 in range(4):
                        i = gqi * 4 + c4
                        if i < NQWC:
                            R_SY.wait(v, rec34["qwi", i])
                            if i >= 2:
                                R_GS.wait(v, rec34["qwo", i - 2])
                            emit_quant2(v, qw_in[:, i % 4, :], qout[:, i % 2, :],
                                        32, q3am, q3sc, q3iv, qxdv, None,
                                        sig="v_qw")

    ph3_st.close()

    ph4_st = contextlib.ExitStack()
    aot = sb("aot", [128, 2, E], BF16, ph4_st)
    bo_r = sb("bo_r", [1, E], F32, ph4_st)
    aops = ps("aops", [128, 2, E], ph4_st)

    with phase("ph4", run=(upto >= 4)) as engs:
        if engs is not None:
            @engs.gpsimd
            def _(g):
                for tcg in range(TOWN // 128):
                    g.wait_ge(sem("v_eo"), tcg + 1)
                    rec34["aostore", tcg] = R_GS.issue(
                        g, ao_h[tcg * 128:(tcg + 1) * 128, :], aot[:, tcg % 2, :])
                R_GS.wait_all(g)

            @engs.sync
            def _(sy):
                rec34["bo"] = R_SY.issue(sy, bo_r[:], bo[None, :])

            @engs.tensor
            def _(t):
                for i in rec34["wl"]:
                    R_GL.wait(t, i)
                R_SY.wait(t, rec34["bo"])
                for tcg in range(TOWN // 128):
                    if tcg >= 2:
                        t.wait_ge(sem("v_eo"), tcg - 1)
                    pt = aops[:, tcg % 2, :]
                    tsl = slice((tcg % 4) * 128 + (tcg // 4) * 512,
                                (tcg % 4) * 128 + (tcg // 4) * 512 + 128)
                    for os_ in range(4):
                        po = pt[:, os_ * 512:(os_ + 1) * 512]
                        for hh in range(H):
                            t.matmul(po, av_all[:, hh, tcg * 128:(tcg + 1) * 128],
                                     wo_s[:, hh, os_ * 512:(os_ + 1) * 512],
                                     start=(hh == 0), stop=False)
                        mm = t.matmul(po, ones_r[0:1, 0:128],
                                      bo_r[0:1, os_ * 512:(os_ + 1) * 512],
                                      start=False, stop=True)
                    mm.then_inc(sem("t_mo"), 1)

            @engs.vector
            def _(v):
                for tcg in range(TOWN // 128):
                    v.wait_ge(sem("t_mo"), tcg + 1)
                    if tcg >= 2:
                        R_GS.wait(v, rec34["aostore", tcg - 2])
                    for fs in range(4):
                        e2 = v.signal_next("v_eo") if fs == 3 else v
                        e2.tensor_copy(aot[:, tcg % 2, fs * 512:(fs + 1) * 512],
                                       aops[:, tcg % 2, fs * 512:(fs + 1) * 512])

    ph4_st.close()
    ph34_st.close()

    rec5b = {}
    ph5b_outer = contextlib.ExitStack()
    xlqt = sb("xlqt", [128, NEC, TOWN], BF16, ph5b_outer)
    ph5a_st = contextlib.ExitStack()
    xft = sb("xft", [128, 2, E], BF16, ph5a_st)
    jnk5 = sb("jnk5", [128, 2, E], F32, ph5a_st)
    xlnp = sb("xlnp", [128, E], F32, ph5a_st)
    xlnw = sb("xlnw", [128, E], F32, ph5a_st)
    xln5 = sb("xln5", [128, E], BF16, ph5a_st)
    stat5 = sb("stat5", [128, 2, 8], F32, ph5a_st)
    lnw_s = sb("lnw_s", [128, E], F32, ph5a_st)
    lnb_s = sb("lnb_s", [128, E], F32, ph5a_st)
    q5am = sb("q5am", [128, 64], F32, ph5a_st)
    q5sc = sb("q5sc", [128, 64], F32, ph5a_st)
    q5iv = sb("q5iv", [128, 64], F32, ph5a_st)
    q5dv = sb("q5dv", [128, E], F32, ph5a_st)
    q5y = sb("q5y", [128, E], F16, ph5a_st)
    xlq5 = sb("xlq5", [128, 2, E], BF16, ph5a_st)


    rec5 = {}
    NT5 = TOWN // 128

    with phase("ph5a", run=(upto >= 5)) as engs:
        if engs is not None:
            @engs.gpsimd
            def _(g):
                for i in range(NT5):
                    g.wait_ge(sem("v_x5"), i + 1)
                    rec5["xq", i] = R_GS.issue(
                        g, xlnq_h[i * 128:(i + 1) * 128, :], xlq5[:, i % 2, :])
                R_GS.wait_all(g)

            @engs.sync
            def _(sy):
                rec5["lnw"] = R_SY.issue(sy, lnw_s[:], lnw_t)
                rec5["lnb"] = R_SY.issue(sy, lnb_s[:], lnb_t)
                for i in range(NT5):
                    if i >= 2:
                        sy.wait_ge(sem("v_x5"), i - 1)
                    rec5["xf", i] = R_SY.issue(sy, xft[:, i % 2, :],
                                               xn_h[i * 128:(i + 1) * 128, :])
                    if i >= 4:
                        j = i - 4
                        R_GS.wait(sy, rec5["xq", j])
                        for ec in range(NEC):
                            rec5b["xlt", ec, j] = R_SY.issue(
                                sy, xlqt[:, ec, j * 128:(j + 1) * 128],
                                xlnq_h[j * 128:(j + 1) * 128,
                                       ec * 128:(ec + 1) * 128],
                                transpose=True)
                for j in range(NT5 - 4, NT5):
                    R_GS.wait(sy, rec5["xq", j])
                    for ec in range(NEC):
                        rec5b["xlt", ec, j] = R_SY.issue(
                            sy, xlqt[:, ec, j * 128:(j + 1) * 128],
                            xlnq_h[j * 128:(j + 1) * 128,
                                   ec * 128:(ec + 1) * 128],
                            transpose=True)

            @engs.scalar
            def _(sc):
                for i in range(NT5):
                    pa = i % 2
                    R_SY.wait(sc, rec5["xf", i])
                    xt = xft[:, pa, :]
                    sc.activation(jnk5[:, 0, :], xt, AF.Copy,
                                  accum_out=stat5[:, pa, 0:1])
                    sc.signal_next("s_st5").activation(
                        jnk5[:, 1, :], xt, AF.Square, accum_out=stat5[:, pa, 1:2])
                    sc.wait_ge(sem("v_t5"), i + 1)
                    sc.signal_next("s_sq5").activation(stat5[:, pa, 7:8],
                                                        stat5[:, pa, 6:7], AF.Sqrt)
                    sc.wait_ge(sem("v_st5"), i + 1)
                    sc.signal_next("s_af5").activation(
                        xlnp[:], xt, AF.Identity,
                        scale=stat5[:, pa, 4:5], bias=stat5[:, pa, 5:6])

            @engs.vector
            def _(v):
                R_SY.wait(v, rec5["lnw"])
                R_SY.wait(v, rec5["lnb"])
                inv_n = 1.0 / E
                for i in range(NT5):
                    pa = i % 2
                    st_ = stat5[:, pa]
                    v.wait_ge(sem("s_st5"), i + 1)
                    v.tensor_scalar_mul(st_[:, 2:3], st_[:, 0:1], inv_n)
                    v.tensor_scalar_mul(st_[:, 3:4], st_[:, 1:2], inv_n)
                    v.scalar_tensor_tensor(st_[:, 6:7], st_[:, 2:3], -1.0,
                                           st_[:, 2:3], ALU.mult, ALU.mult)
                    v.scalar_tensor_tensor(st_[:, 6:7], st_[:, 3:4], 1.0,
                                           st_[:, 6:7], ALU.mult, ALU.add)
                    v.signal_next("v_t5").tensor_scalar_add(st_[:, 6:7],
                                                             st_[:, 6:7], EPS)
                    v.wait_ge(sem("s_sq5"), i + 1)
                    v.reciprocal(st_[:, 4:5], st_[:, 7:8])
                    for _ in range(2):
                        v.scalar_tensor_tensor(st_[:, 7:8], st_[:, 4:5], 1.0,
                                               st_[:, 4:5], ALU.mult, ALU.mult)
                        v.scalar_tensor_tensor(st_[:, 7:8], st_[:, 7:8], -0.5,
                                               st_[:, 6:7], ALU.mult, ALU.mult)
                        v.tensor_scalar_add(st_[:, 7:8], st_[:, 7:8], 1.5)
                        v.scalar_tensor_tensor(st_[:, 4:5], st_[:, 7:8], 1.0,
                                               st_[:, 4:5], ALU.mult, ALU.mult)
                    v.signal_next("v_st5").scalar_tensor_tensor(
                        st_[:, 5:6], st_[:, 2:3], -1.0, st_[:, 4:5],
                        ALU.mult, ALU.mult)
                    v.wait_ge(sem("s_af5"), i + 1)
                    v.tensor_mul(xlnw[:], xlnp[:], lnw_s[:])
                    v.tensor_add(xln5[:], xlnw[:], lnb_s[:])
                    if i >= 2:
                        R_GS.wait(v, rec5["xq", i - 2])
                    emit_quant2(v, xln5[:], xlq5[:, pa, :], 64,
                                q5am, q5sc, q5iv, q5dv, q5y, sig="v_x5")

    ph5a_st.close()

    ph5b_st = contextlib.ExitStack()
    w1t = sb("w1t", [128, 2, NEC, 512], BF16, ph5b_st)
    g1b = sb("g1b", [128, 2, 512], BF16, ph5b_st)
    g2b = sb("g2b", [128, 2, 512], BF16, ph5b_st)
    gqb = sb("gqb", [128, 2, 512], BF16, ph5b_st)
    b1s = sb("b1s", [128, F1], F32, ph5b_st)
    wbt = sb("wbt", [128, 4, 512], BF16, ph5b_st)
    q6am = sb("q6am", [128, 2, 16], F32, ph5b_st)
    q6sc = sb("q6sc", [128, 2, 16], F32, ph5b_st)
    q6iv = sb("q6iv", [128, 2, 16], F32, ph5b_st)
    q6dv = sb("q6dv", [128, 2, 512], F32, ph5b_st)
    h1ps = ps("h1ps", [128, 2, 512], ph5b_st)

    wbl_ring = Ring("wbl", 4)
    NFS = F1 // 512
    NTC5 = TOWN // 128
    NG5 = NFS * NTC5
    NWB = (E * F1 * 2) // (512 * 128 * 2)

    with phase("ph5b", run=(upto >= 5)) as engs:
        if engs is not None:
            @engs.gpsimd
            def _(g):
                for i in range(NG5):
                    g.wait_ge(sem("v_gq"), i + 1)
                    fs, tcb = i // NTC5, i % NTC5
                    rows = slice(tcb * 128, (tcb + 1) * 128)
                    rec5b["gq", i] = R_GS.issue(
                        g, gq_h[rows, fs * 512:(fs + 1) * 512], gqb[:, i % 2, :])
                    wbl_ring.wait(g, i)
                    os_, fc = i // 64, i % 64
                    rec5b["wbo", i] = R_GS.issue(
                        g, w2qT_h[fc * 128:(fc + 1) * 128,
                                  os_ * 512:(os_ + 1) * 512], wbt[:, i % 4, :])
                R_GS.wait_all(g)

            @engs.sync
            def _(sy):
                rec5b["b1"] = R_SY.issue(sy, b1s[:], b1_t)
                for fs in range(NFS):
                    if fs >= 2:
                        sy.wait_ge(sem("t_fs"), fs - 1)
                    for ec in range(NEC):
                        rec5b["w1", fs, ec] = R_SY.issue(
                            sy, w1t[:, fs % 2, ec, :],
                            w1q_h[fs * 512:(fs + 1) * 512,
                                  ec * 128:(ec + 1) * 128],
                            transpose=True)
                    for k in range(16):
                        i = fs * 16 + k
                        os_, fc = i // 64, i % 64
                        if i >= 4:
                            R_GS.wait(sy, rec5b["wbo", i - 4])
                        wbl_ring.issue(sy, wbt[:, i % 4, :],
                                       w2q_h[os_ * 512:(os_ + 1) * 512,
                                             fc * 128:(fc + 1) * 128],
                                       transpose=True)


            @engs.tensor
            def _(t):
                for ec in range(NEC):
                    for i in range(NT5):
                        R_SY.wait(t, rec5b["xlt", ec, i])
                for fs in range(NFS):
                    for ec in range(NEC):
                        R_SY.wait(t, rec5b["w1", fs, ec])
                    for tcb in range(NTC5):
                        gidx = fs * NTC5 + tcb
                        if gidx >= 2:
                            t.wait_ge(sem("v_e1"), gidx - 1)
                        po = h1ps[:, gidx % 2, :]
                        for ec in range(NEC):
                            mm1 = t.matmul(po, xlqt[:, ec, tcb * 128:(tcb + 1) * 128],
                                           w1t[:, fs % 2, ec, :],
                                           start=(ec == 0), stop=(ec == NEC - 1))
                        mm1.then_inc(sem("t_m1"), 1)
                    t.drain().then_inc(sem("t_fs"), 1)

            @engs.scalar
            def _(sc):
                for gidx in range(NG5):
                    pa = gidx % 2
                    sc.wait_ge(sem("v_e1"), gidx + 1)
                    if gidx >= 2:
                        sc.wait_ge(sem("v_gq"), gidx - 1)
                    sc.signal_next("s_g1").activation(g2b[:, pa, :], g1b[:, pa, :],
                                                       AF.Gelu_apprx_tanh)

            @engs.vector
            def _(v):
                R_SY.wait(v, rec5b["b1"])
                for gidx in range(NG5):
                    pa = gidx % 2
                    v.wait_ge(sem("t_m1"), gidx + 1)
                    if gidx >= 2:
                        v.wait_ge(sem("s_g1"), gidx - 1)
                    v.signal_next("v_e1").tensor_tensor(
                        g1b[:, pa, :], h1ps[:, pa, :],
                        b1s[:, (gidx // NTC5) * 512:(gidx // NTC5 + 1) * 512],
                        ALU.add)
                    if gidx >= 1:
                        v.wait_ge(sem("s_g1"), gidx)
                        if gidx >= 3:
                            R_GS.wait(v, rec5b["gq", gidx - 3])
                        emit_quant2(v, g2b[:, (gidx - 1) % 2, :],
                                    gqb[:, (gidx - 1) % 2, :], 16,
                                    q6am[:, (gidx - 1) % 2, :],
                                    q6sc[:, (gidx - 1) % 2, :],
                                    q6iv[:, (gidx - 1) % 2, :],
                                    q6dv[:, (gidx - 1) % 2, :], None, sig="v_gq")
                g_last = NG5 - 1
                v.wait_ge(sem("s_g1"), g_last + 1)
                R_GS.wait(v, rec5b["gq", g_last - 2])
                emit_quant2(v, g2b[:, g_last % 2, :],
                            gqb[:, g_last % 2, :], 16,
                            q6am[:, g_last % 2, :], q6sc[:, g_last % 2, :],
                            q6iv[:, g_last % 2, :], q6dv[:, g_last % 2, :], None,
                            sig="v_gq")

    ph5b_st.close()
    ph5b_outer.close()

    ph5c_st = contextlib.ExitStack()
    gqt = sb("gqt", [128, 64, 512], BF16, ph5c_st)
    rec5c = {}
    w2t = sb("w2t", [128, 2, 32, 512], BF16, ph5c_st)
    h2s = sb("h2s", [128, 4, E], BF16, ph5c_st)
    xf6 = sb("xf6", [128, 2, E], BF16, ph5c_st)
    xr6 = sb("xr6", [128, 2, E], F32, ph5c_st)
    ao6 = sb("ao6", [128, 2, E], BF16, ph5c_st)
    ob6 = sb("ob6", [128, 2, E], F32, ph5c_st)
    fo6 = sb("fo6", [128, E], BF16, ph5c_st)
    b2_r = sb("b2_r", [1, E], F32, ph5c_st)
    h2ps = ps("h2ps", [128, 4, 512], ph5c_st)

    with phase("ph5c", run=(upto >= 5)) as engs:
        if engs is not None:
            @engs.gpsimd
            def _(g):
                for gt in range(16):
                    g.wait_ge(sem("v_fin"), gt + 1)
                    rec5c["out", gt] = R_GS.issue(
                        g, out[gt * 128:(gt + 1) * 128, :], ob6[:, gt % 2, :])
                R_GS.wait_all(g)

            @engs.sync
            def _(sy):
                rec5c["b2"] = R_SY.issue(sy, b2_r[:], b2[None, :])
                for tsb in range(4):
                    if tsb >= 1:
                        sy.wait_ge(sem("t_tsb"), tsb)
                    for fc in range(64):
                        rec5c["gqt", tsb, fc] = R_SY.issue(
                            sy, gqt[:, fc, :],
                            gq_h[tsb * 512:(tsb + 1) * 512,
                                 fc * 128:(fc + 1) * 128],
                            transpose=True)

                    for os_ in range(4):
                        gos = tsb * 4 + os_
                        for hf in range(0, 2, 2):
                            pass
                        hf = 0
                        if gos >= 1:
                            sy.wait_ge(sem("t_hf"), (gos - 1) * 2 + hf + 1)
                        for fc in range(32):
                            rec5c["w2", gos, hf, fc] = R_SY.issue(
                                sy, w2t[:, hf, fc, :],
                                w2qT_h[(hf * 32 + fc) * 128:
                                       (hf * 32 + fc + 1) * 128,
                                       os_ * 512:(os_ + 1) * 512])

            @engs.scalar
            def _(sc):
                for tsb in range(4):
                    for os_ in range(4):
                        gos = tsb * 4 + os_
                        hf = 1
                        if gos >= 1:
                            sc.wait_ge(sem("t_hf"), (gos - 1) * 2 + hf + 1)
                        for fc in range(32):
                            rec5c["w2", gos, hf, fc] = R_SC.issue(
                                sc, w2t[:, hf, fc, :],
                                w2qT_h[(hf * 32 + fc) * 128:
                                       (hf * 32 + fc + 1) * 128,
                                       os_ * 512:(os_ + 1) * 512])
                    for tt_ in range(4):
                        gt = tsb * 4 + tt_
                        if gt >= 2:
                            sc.wait_ge(sem("v_fin"), gt - 1)
                        rows = slice(gt * 128, (gt + 1) * 128)
                        rec5c["xr", gt] = R_SC.issue(sc, xr6[:, gt % 2, :], xs[rows, :])
                        rec5c["aor", gt] = R_SC.issue(sc, ao6[:, gt % 2, :], ao_h[rows, :])
                        rec5c["xfr", gt] = R_SC.issue(sc, xf6[:, gt % 2, :], xn_h[rows, :])

            @engs.tensor
            def _(t):
                R_SY.wait(t, rec5c["b2"])
                for tsb in range(4):
                    for fc in range(64):
                        R_SY.wait(t, rec5c["gqt", tsb, fc])
                    for os_ in range(4):
                        gos = tsb * 4 + os_
                        for hf in range(2):
                            w2r = R_SY if hf == 0 else R_SC
                            for ii in range(32):
                                w2r.wait(t, rec5c["w2", gos, hf, ii])
                            for tt_ in range(4):
                                gidx = gos * 4 + tt_
                                if hf == 0 and gos >= 1:
                                    t.wait_ge(sem("v_e2"), (gos - 1) * 4 + tt_ + 1)
                                po = h2ps[:, tt_, :]
                                for fc in range(32):
                                    t.matmul(po, gqt[:, hf * 32 + fc,
                                                     tt_ * 128:(tt_ + 1) * 128],
                                             w2t[:, hf, fc, :],
                                             start=(hf == 0 and fc == 0),
                                             stop=False)
                            t.drain().then_inc(sem("t_hf"), 1)
                        for tt_ in range(4):
                            t.matmul(h2ps[:, tt_, :], ones_r[0:1, 0:128],
                                     b2_r[0:1, os_ * 512:(os_ + 1) * 512],
                                     start=False, stop=True) \
                                .then_inc(sem("t_m2"), 1)
                    t.drain().then_inc(sem("t_tsb"), 1)

            @engs.vector
            def _(v):
                for tsb in range(4):
                    for os_ in range(4):
                        for tt_ in range(4):
                            gidx = (tsb * 4 + os_) * 4 + tt_
                            v.wait_ge(sem("t_m2"), gidx + 1)
                            v.signal_next("v_e2").tensor_copy(
                                h2s[:, tt_, os_ * 512:(os_ + 1) * 512],
                                h2ps[:, tt_, :])
                    for tt_ in range(4):
                        gt = tsb * 4 + tt_
                        R_SC.wait(v, rec5c["xr", gt])
                        R_SC.wait(v, rec5c["aor", gt])
                        R_SC.wait(v, rec5c["xfr", gt])
                        if gt >= 2:
                            R_GS.wait(v, rec5c["out", gt - 2])
                        v.tensor_add(fo6[:], xf6[:, gt % 2, :], h2s[:, tt_, :])
                        v.scalar_tensor_tensor(ob6[:, gt % 2, :], ao6[:, gt % 2, :],
                                               0.0, xr6[:, gt % 2, :],
                                               ALU.bypass, ALU.add)
                        v.signal_next("v_fin").tensor_add(ob6[:, gt % 2, :],
                                                           ob6[:, gt % 2, :], fo6[:])


    ph5c_st.close()

    stack.close()
    return nc




_ROPE_BASE = 10000.0


def _rope_tables():
    inv_freq = 1.0 / _ROPE_BASE ** (np.arange(0, D, 2, dtype=np.float32) / D)
    pos = np.arange(S, dtype=np.float32)
    freqs = np.outer(pos, inv_freq)
    emb = np.concatenate([freqs, freqs], axis=-1)
    cos = np.cos(emb).astype(np.float32)
    sin = np.sin(emb).astype(np.float32)
    nsin = np.concatenate([-sin[:, :64], sin[:, 64:]], axis=1)
    return cos, nsin


def _stage(inputs, core, shared, cos, nsin):
    b, h = core // 2, core % 2
    x = inputs["L0_attn_x"]
    perm = np.concatenate([np.arange(h * TOWN, (h + 1) * TOWN),
                           np.arange((1 - h) * TOWN, (2 - h) * TOWN)])
    d = dict(shared)
    d["xs"] = np.ascontiguousarray(x[b][perm])
    d["cos_t"] = np.ascontiguousarray(cos[perm])
    d["nsin_t"] = np.ascontiguousarray(nsin[perm])
    return d


def kernel(**inputs):
    from concourse.bass_utils import run_bass_kernel_spmd
    inputs = {k: np.ascontiguousarray(np.asarray(v, dtype=np.float32))
              for k, v in inputs.items()}
    shared = {
        "wqt": np.ascontiguousarray(inputs["L0_attn_Wq"].T),
        "wkt": np.ascontiguousarray(inputs["L0_attn_Wk"].T),
        "wvt": np.ascontiguousarray(inputs["L0_attn_Wv"].T),
        "wot": np.ascontiguousarray(inputs["L0_attn_Wo"].T),
        "w1": inputs["L0_ffn_W1"],
        "w2": inputs["L0_ffn_W2"],
        "bq": inputs["L0_attn_bq"],
        "bk": inputs["L0_attn_bk"],
        "bv": inputs["L0_attn_bv"],
        "bo": inputs["L0_attn_bo"],
        "b1": inputs["L0_ffn_b1"],
        "b2": inputs["L0_ffn_b2"],
        "b1_t": np.ascontiguousarray(
            np.broadcast_to(inputs["L0_ffn_b1"], (128, F1))),
        "lnw_t": np.ascontiguousarray(
            np.broadcast_to(inputs["L0_ffn_ln_w"], (128, E))),
        "lnb_t": np.ascontiguousarray(
            np.broadcast_to(inputs["L0_ffn_ln_b"], (128, E))),
    }
    cos, nsin = _rope_tables()
    in_maps = [_stage(inputs, c, shared, cos, nsin) for c in range(8)]
    nc = build()
    res = run_bass_kernel_spmd(nc, in_maps, core_ids=list(range(8)))
    outp = np.empty((B, S, E), dtype=np.float32)
    for c in range(8):
        b, h = c // 2, c % 2
        outp[b, h * TOWN:(h + 1) * TOWN, :] = res.results[c]["out"]
    return outp


# version 17
# speedup vs baseline: 1.1016x; 1.0300x over previous
import math
import contextlib
import numpy as np
import concourse.bass as bass
import concourse.mybir as mybir

F32 = mybir.dt.float32
BF16 = mybir.dt.bfloat16
F16 = mybir.dt.float16
I32 = mybir.dt.int32
AF = mybir.ActivationFunctionType
ALU = mybir.AluOpType
AX = mybir.AxisListType

B, S, E, H, D = 4, 4096, 2048, 16, 128
F1 = 8192
TOWN = 2048
NEC = E // 128
INVSQ = 1.0 / math.sqrt(D)
EPS = 1e-5

QCLAMP = 2.0 ** -40
EXPMASK = 0x7F800000
SIXSHIFT = 6 << 23
INVBASE = 254 << 23
MAGIC = 12582912.0
PRECLIP = 127.4

NENG = 5


def bc(ap2d, nblk, bk):
    return bass.AP(ap2d.tensor, ap2d.offset, [ap2d.ap[0], [ap2d.ap[1][0], nblk], [0, bk]])


def r3(ap2d, bk):
    return ap2d.rearrange("p (b k) -> p b k", k=bk)


def build(upto=99, dbg=()):
    nc = bass.Bass("TRN2", target_bir_lowering=False, debug=False)
    stack = contextlib.ExitStack()
    sems = {}

    def sem(name):
        if name not in sems:
            sems[name] = stack.enter_context(nc.semaphore(name))
        return sems[name]

    class Ring:

        def __init__(self, name, n):
            self.name, self.n = name, n
            self.issued = 0

        def issue(self, eng, out_ap, in_ap, **kw):
            i = self.issued
            slot, rnd = i % self.n, i // self.n
            s = sem(f"{self.name}{slot}")
            if rnd > 0:
                eng.wait_ge(s, 16 * rnd)
            eng.dma_start(out_ap, in_ap, **kw).then_inc(s, 16)
            self.issued += 1
            return i

        def wait(self, eng, i):
            eng.wait_ge(sem(f"{self.name}{i % self.n}"), 16 * (i // self.n + 1))

        def wait_all(self, eng):
            for slot in range(min(self.n, self.issued)):
                cnt = (self.issued - 1 - slot) // self.n + 1
                eng.wait_ge(sem(f"{self.name}{slot}"), 16 * cnt)

    R_SY = Ring("rsy", 16)
    R_GS = Ring("rgs", 8)
    R_GL = Ring("rgl", 4)
    R_SC = Ring("rsc", 6)

    def dram(name, shape, dtype, kind):
        return nc.dram_tensor(name, shape, dtype, kind=kind).ap()

    xs = dram("xs", [S, E], F32, "ExternalInput")
    wqt = dram("wqt", [E, E], F32, "ExternalInput")
    wkt = dram("wkt", [E, E], F32, "ExternalInput")
    wvt = dram("wvt", [E, E], F32, "ExternalInput")
    wot = dram("wot", [E, E], F32, "ExternalInput")
    w1 = dram("w1", [F1, E], F32, "ExternalInput")
    w2 = dram("w2", [E, F1], F32, "ExternalInput")
    bq = dram("bq", [E], F32, "ExternalInput")
    bkk = dram("bk", [E], F32, "ExternalInput")
    bv = dram("bv", [E], F32, "ExternalInput")
    bo = dram("bo", [E], F32, "ExternalInput")
    b1 = dram("b1", [F1], F32, "ExternalInput")
    b2 = dram("b2", [E], F32, "ExternalInput")
    b1_t = dram("b1_t", [128, F1], F32, "ExternalInput")
    lnw_t = dram("lnw_t", [128, E], F32, "ExternalInput")
    lnb_t = dram("lnb_t", [128, E], F32, "ExternalInput")
    cos_t = dram("cos_t", [S, D], F32, "ExternalInput")
    nsin_t = dram("nsin_t", [S, D], F32, "ExternalInput")

    out = dram("out", [TOWN, E], F32, "ExternalOutput")

    def scratch(name, shape, dtype):
        kind = "ExternalOutput" if name in dbg else "Internal"
        return dram(name, shape, dtype, kind)

    xn_h = scratch("xn_h", [S, E], BF16)
    q_tm = scratch("q_tm", [TOWN, E], BF16)
    k_tm = scratch("k_tm", [S, E], BF16)
    v_h = scratch("v_h", [H, S, D], BF16)
    nrm_h = scratch("nrm_h", [2, 128, H], F32)
    crow_h = scratch("crow_h", [H, 1], F32)
    av_h = scratch("av_h", [H, TOWN, D], BF16)
    ao_h = scratch("ao_h", [TOWN, E], BF16)
    xlnq_h = scratch("xlnq_h", [TOWN, E], BF16)
    gq_h = scratch("gq_h", [TOWN, F1], BF16)
    w1q_h = scratch("w1q_h", [F1, E], BF16)
    w2q_h = scratch("w2q_h", [E, F1], BF16)
    w2qT_h = scratch("w2qT_h", [F1, E], BF16)
    cdbg_h = scratch("cdbg_h", [1, 4 * H], F32)

    def sb(name, shape, dtype, st=stack):
        return st.enter_context(nc.sbuf_tensor(name, shape, dtype))

    def ps(name, shape, st):
        return st.enter_context(nc.psum_tensor(name, shape, F32))

    ones_r = sb("ones_r", [1, 512], F32)
    ones_b = sb("ones_b", [1, 512], BF16)
    cneg_s = sb("cneg_s", [128, H], F32)

    fence = sem("fence")
    FCT = [0]

    CHAIN_OPS = {
        "memset", "memzero", "tensor_copy", "tensor_tensor", "tensor_add",
        "tensor_sub", "tensor_mul", "tensor_max", "tensor_scalar",
        "tensor_single_scalar", "tensor_scalar_add", "tensor_scalar_sub",
        "tensor_scalar_mul", "tensor_scalar_max", "tensor_scalar_min",
        "tensor_relu", "tensor_reduce", "reduce_max", "reduce_sum",
        "scalar_tensor_tensor", "reciprocal", "select", "activation",
        "square", "sqrt", "copy", "add", "mul", "sign",
    }
    chain_state = {"scalar": 0, "vector": 0, "gpsimd": 0}

    ext_counts = {}
    chain_ext = {"scalar": None, "vector": None, "gpsimd": None}

    class ChainEng:

        def __init__(self, eng, name):
            self._eng = eng
            self._name = name
            self._sem = sem(f"ch_{name}")
            self._pending_sig = None

        def signal_next(self, signame):
            self._pending_sig = signame
            return self

        def __getattr__(self, attr):
            fn = getattr(self._eng, attr)
            if attr not in CHAIN_OPS:
                return fn

            def wrapped(*a, **k):
                c = chain_state[self._name]
                if c > 0:
                    self._eng.wait_ge(self._sem, c)
                ext = chain_ext[self._name]
                if ext is not None:
                    self._eng.wait_ge(sem(ext[0]), ext[1])
                    chain_ext[self._name] = None
                ins = fn(*a, **k)
                if self._pending_sig is not None:
                    nm = self._pending_sig
                    self._pending_sig = None
                    val = ext_counts.get(nm, 0) + 1
                    ext_counts[nm] = val
                    ins.then_inc(sem(nm), 1)
                    chain_ext[self._name] = (nm, val)
                else:
                    ins.then_inc(self._sem, 1)
                    chain_state[self._name] = c + 1
                return ins
            return wrapped

        def drain(self):
            return self._eng.drain()

    @contextlib.contextmanager
    def phase(name, run=True):
        class Engs:
            pass
        if not run:
            yield None
            return
        with nc.Block() as blk:
            bodies = {}

            class Reg:
                def __getattr__(self, eng):
                    def reg(fn):
                        bodies[eng] = fn
                        return fn
                    return reg
            yield Reg()
            base = FCT[0]
            for eng_name in ("gpsimd", "sync", "scalar", "vector", "tensor"):
                body = bodies.get(eng_name)
                dec = getattr(blk, eng_name)

                def make(body_fn, engn):
                    def _(eng):
                        eng.wait_ge(fence, base)
                        if body_fn is not None:
                            if engn in ("scalar", "vector", "gpsimd"):
                                body_fn(ChainEng(eng, engn))
                            else:
                                body_fn(eng)
                        eng.drain().then_inc(fence, 1)
                    return _
                dec(make(body, eng_name))
            FCT[0] = base + NENG

    def emit_quant2(v, src_ap, out_ap, nblk, am, scl, inv, xdv, y16, sig=None):
        fd = nblk * 32
        v.tensor_reduce(am[:, :nblk], r3(src_ap, 32), AX.X, ALU.max,
                        apply_absolute_value=True)
        v.tensor_scalar_max(am[:, :nblk], am[:, :nblk], QCLAMP)
        v.tensor_single_scalar(scl[:, :nblk].bitcast(I32),
                               am[:, :nblk].bitcast(I32), EXPMASK,
                               ALU.bitwise_and)
        v.tensor_single_scalar(scl[:, :nblk].bitcast(I32),
                               scl[:, :nblk].bitcast(I32), SIXSHIFT,
                               ALU.subtract)
        v.tensor_scalar(inv[:, :nblk].bitcast(I32), scl[:, :nblk].bitcast(I32),
                        -1, INVBASE, ALU.mult, ALU.add)
        v.tensor_tensor(r3(xdv[:, :fd], 32), r3(src_ap, 32),
                        bc(inv[:, :nblk], nblk, 32), ALU.mult)
        v.tensor_scalar(xdv[:, :fd], xdv[:, :fd], PRECLIP, MAGIC,
                        ALU.min, ALU.add)
        if sig is not None:
            v.signal_next(sig)
        ins = v.scalar_tensor_tensor(r3(out_ap, 32), r3(xdv[:, :fd], 32), MAGIC,
                                     bc(scl[:, :nblk], nblk, 32),
                                     ALU.subtract, ALU.mult)
        return ins

    with phase("ph0") as engs:
        @engs.vector
        def _(v):
            v.memset(ones_r[:], 1.0)
            v.memset(ones_b[:], 1.0)

    NT1 = S // 128
    ph1_st = contextlib.ExitStack()
    xbuf = sb("xbuf", [128, 2, E], F32, ph1_st)
    jnk = sb("jnk", [128, 2, E], F32, ph1_st)
    xnbuf = sb("xnbuf", [128, 2, E], BF16, ph1_st)
    stat = sb("stat", [128, 2, 8], F32, ph1_st)

    rec1 = {}

    def ph1_sync_chunk(sy, lo, hi):
        for i in range(lo, min(hi, NT1)):
            if i >= 2:
                sy.wait_ge(sem("s_af"), i - 1)
            rec1["x", i] = R_SY.issue(sy, xbuf[:, i % 2, :],
                                      xs[i * 128:(i + 1) * 128, :])

    def ph1_scalar(sc):
        for i in range(NT1):
            pa = i % 2
            R_SY.wait(sc, rec1["x", i])
            xt = xbuf[:, pa, :]
            sc.activation(jnk[:, 0, :], xt, AF.Copy,
                          accum_out=stat[:, pa, 0:1])
            sc.signal_next("s_st").activation(jnk[:, 1, :], xt, AF.Square,
                          accum_out=stat[:, pa, 1:2])
            sc.wait_ge(sem("v_t"), i + 1)
            sc.signal_next("s_sq").activation(stat[:, pa, 7:8],
                                              stat[:, pa, 6:7], AF.Sqrt)
            sc.wait_ge(sem("v_st"), i + 1)
            if i >= 2:
                R_GS.wait(sc, rec1["xn", i - 2])
            sc.signal_next("s_af").activation(xnbuf[:, pa, :], xt, AF.Identity,
                          scale=stat[:, pa, 4:5], bias=stat[:, pa, 5:6])

    def ph1_vec_pump(v, upto_i):
        inv_n = 1.0 / E
        while ph1_vec_pump.done < min(upto_i, NT1):
            i = ph1_vec_pump.done
            pa = i % 2
            st_ = stat[:, pa]
            v.wait_ge(sem("s_st"), i + 1)
            v.tensor_scalar_mul(st_[:, 2:3], st_[:, 0:1], inv_n)
            v.tensor_scalar_mul(st_[:, 3:4], st_[:, 1:2], inv_n)
            v.scalar_tensor_tensor(st_[:, 6:7], st_[:, 2:3], -1.0, st_[:, 2:3],
                                   ALU.mult, ALU.mult)
            v.scalar_tensor_tensor(st_[:, 6:7], st_[:, 3:4], 1.0, st_[:, 6:7],
                                   ALU.mult, ALU.add)
            v.signal_next("v_t").tensor_scalar_add(st_[:, 6:7], st_[:, 6:7], EPS)
            v.wait_ge(sem("s_sq"), i + 1)
            v.reciprocal(st_[:, 4:5], st_[:, 7:8])
            for _ in range(2):
                v.scalar_tensor_tensor(st_[:, 7:8], st_[:, 4:5], 1.0, st_[:, 4:5],
                                       ALU.mult, ALU.mult)
                v.scalar_tensor_tensor(st_[:, 7:8], st_[:, 7:8], -0.5, st_[:, 6:7],
                                       ALU.mult, ALU.mult)
                v.tensor_scalar_add(st_[:, 7:8], st_[:, 7:8], 1.5)
                v.scalar_tensor_tensor(st_[:, 4:5], st_[:, 7:8], 1.0, st_[:, 4:5],
                                       ALU.mult, ALU.mult)
            v.signal_next("v_st").scalar_tensor_tensor(
                st_[:, 5:6], st_[:, 2:3], -1.0, st_[:, 4:5],
                ALU.mult, ALU.mult)
            ph1_vec_pump.done = i + 1
    ph1_vec_pump.done = 0

    def ph1_gp_pump(g, upto_xn):
        while ph1_gp_pump.done < min(upto_xn, NT1):
            i = ph1_gp_pump.done
            g.wait_ge(sem("s_af"), i + 1)
            rec1["xn", i] = R_GS.issue(g, xn_h[i * 128:(i + 1) * 128, :],
                                       xnbuf[:, i % 2, :])
            ph1_gp_pump.done = i + 1
    ph1_gp_pump.done = 0


    ph2_st = contextlib.ExitStack()
    wbuf = sb("wbuf", [128, NEC, E], BF16, ph2_st)
    xnt = sb("xnt", [128, 2, NEC, 512], BF16, ph2_st)
    t1b = sb("t1b", [128, E], F32, ph2_st)
    t2b = sb("t2b", [128, E], F32, ph2_st)
    nqt = sb("nqt", [128, H], F32, ph2_st)
    nqacc_q = sb("nqacc_q", [128, H], F32, ph2_st)
    nqacc_k = sb("nqacc_k", [128, H], F32, ph2_st)
    vtile = sb("vtile", [128, 2, E], BF16, ph2_st)
    cos_tm = sb("cos_tm", [128, 32, D], BF16, ph2_st)
    nsin_tm = sb("nsin_tm", [128, 32, D], BF16, ph2_st)
    bq_r = sb("bq_r", [1, E], BF16, ph2_st)
    bk_r = sb("bk_r", [1, E], BF16, ph2_st)
    bv_r = sb("bv_r", [1, E], BF16, ph2_st)
    crow_s = sb("crow_s", [1, H], F32, ph2_st)
    nrmt_s = sb("nrmt_s", [16, 2, 128], F32, ph2_st)
    mx_s = sb("mx_s", [16, 4], F32, ph2_st)

    vps = ps("vps", [128, 2, E], ph2_st)

    def hview(ap2d):
        return ap2d.rearrange("p (h d) -> p h d", d=D)

    def hbc(ap1, off, num):
        return bass.AP(ap1.tensor, ap1.offset + off, [ap1.ap[0], [0, H], [1, num]])

    def emit_wload(g, ring, w_hbm, rec, dstbuf=None):
        dstbuf = wbuf if dstbuf is None else dstbuf
        rec["wl"] = []
        for c4 in range(4):
            srcw = w_hbm[c4 * 512:(c4 + 1) * 512, :].rearrange(
                "(c p) f -> p c f", p=128)
            rec["wl"].append(ring.issue(g, dstbuf[:, c4 * 4:(c4 + 1) * 4, :], srcw))

    def proj_tm(tag, w_hbm, b_row, nslabs, kind, run):
        rec = {}
        NTC = nslabs * 4
        dst_tm = {"q": q_tm, "k": k_tm, "v": None}[kind]
        nqacc = {"q": nqacc_q, "k": nqacc_k, "v": None}[kind]

        with phase(f"ph2{tag}", run=run) as engs:
            if engs is None:
                return

            @engs.gpsimd
            def _(g):
                if kind == "q":
                    rec["cos"] = R_GL.issue(g, cos_tm[:],
                                            cos_t.rearrange("(c p) d -> p c d", p=128))
                    rec["nsin"] = R_GL.issue(g, nsin_tm[:],
                                             nsin_t.rearrange("(c p) d -> p c d", p=128))
                    bidx = [R_GL.issue(g, dst[:], srcb[None, :])
                            for dst, srcb in ((bq_r, bq), (bk_r, bkk), (bv_r, bv))]
                    for i in bidx:
                        R_GL.wait(g, i)
                    g.nop().then_inc(sem("bi_done"), 1)
                emit_wload(g, R_GL, w_hbm, rec)
                for tcix in range(NTC):
                    if kind == "q":
                        ph1_gp_pump(g, 2 * tcix + 4)
                    g.wait_ge(sem(f"v_ev{tag}"), tcix + 1)
                    rows = slice(tcix * 128, (tcix + 1) * 128)
                    if kind == "v":
                        for hh in range(H):
                            rec["st", tcix, hh] = R_GS.issue(
                                g, v_h[hh, rows, :],
                                vtile[:, tcix % 2, hh * D:(hh + 1) * D])
                    else:
                        rec["st", tcix] = R_GS.issue(g, dst_tm[rows, :],
                                                     vtile[:, tcix % 2, :])
                if kind == "q":
                    ph1_gp_pump(g, NT1)
                R_GS.wait_all(g)

            @engs.sync
            def _(sy):
                if kind == "q":
                    ph1_sync_chunk(sy, 0, 6)
                if kind == "k":
                    for i in range(NT1 - 8, NT1):
                        R_GS.wait(sy, rec1["xn", i])
                for j in range(nslabs):
                    if j >= 2:
                        sy.wait_ge(sem(f"t_sl{tag}"), j - 1)
                    if kind == "q":
                        for i in range(4 * j, 4 * j + 4):
                            R_GS.wait(sy, rec1["xn", i])
                    for ec in range(NEC):
                        rec["xt", j, ec] = R_SY.issue(
                            sy, xnt[:, j % 2, ec, :],
                            xn_h[j * 512:(j + 1) * 512, ec * 128:(ec + 1) * 128],
                            transpose=True)
                    if kind == "q":
                        ph1_sync_chunk(sy, 6 + 6 * j, 6 + 6 * (j + 1))
                if kind == "q":
                    ph1_sync_chunk(sy, 6 + 6 * nslabs, NT1)

            @engs.tensor
            def _(t):
                for i in rec["wl"]:
                    R_GL.wait(t, i)
                if kind == "q":
                    t.wait_ge(sem("bi_done"), 1)
                for j in range(nslabs):
                    for ec in range(NEC):
                        R_SY.wait(t, rec["xt", j, ec])
                    for tc4 in range(4):
                        tcix = j * 4 + tc4
                        if tcix >= 2:
                            t.wait_ge(sem(f"v_ev{tag}"), tcix - 1)
                        pt = vps[:, tcix % 2, :]
                        tsl = slice(tc4 * 128, (tc4 + 1) * 128)
                        for fs in range(4):
                            po = pt[:, fs * 512:(fs + 1) * 512]
                            for ec in range(NEC):
                                t.matmul(po, xnt[:, j % 2, ec, tsl],
                                         wbuf[:, ec, fs * 512:(fs + 1) * 512],
                                         start=(ec == 0), stop=False)
                            mm = t.matmul(po, ones_b[0:1, 0:128],
                                          b_row[0:1, fs * 512:(fs + 1) * 512],
                                          start=False, stop=True)
                        mm.then_inc(sem(f"t_mm{tag}"), 1)
                    t.drain().then_inc(sem(f"t_sl{tag}"), 1)

            @engs.scalar
            def _(sc):
                if kind == "q":
                    ph1_scalar(sc)

            @engs.vector
            def _(v):
                if kind != "v":
                    v.memset(nqacc[:], 0.0)
                    R_GL.wait(v, rec["cos"] if "cos" in rec else 0)
                    R_GL.wait(v, rec["nsin"] if "nsin" in rec else 0)
                for tcix in range(NTC):
                    pa = tcix % 2
                    if kind == "q":
                        ph1_vec_pump(v, 2 * tcix + 6)
                    v.wait_ge(sem(f"t_mm{tag}"), tcix + 1)
                    if tcix >= 2:
                        if kind == "v":
                            for hh in range(H):
                                R_GS.wait(v, rec["st", tcix - 2, hh])
                        else:
                            R_GS.wait(v, rec["st", tcix - 2])
                    pt = vps[:, pa, :]
                    if kind == "v":
                        for fs in range(4):
                            e2 = v.signal_next(f"v_ev{tag}") if fs == 3 else v
                            e2.tensor_copy(vtile[:, pa, fs * 512:(fs + 1) * 512],
                                           pt[:, fs * 512:(fs + 1) * 512])
                    else:
                        cosc = cos_tm[:, tcix, :]
                        nsinc = nsin_tm[:, tcix, :]
                        pt3 = hview(pt)
                        v.tensor_tensor(hview(t1b[:]), pt3, hbc(cosc, 0, D),
                                        ALU.mult)
                        ob = hview(t2b[:])
                        oa = bass.AP(ob.tensor, ob.offset, [ob.ap[0], [D, H], [1, 64]])
                        ia = bass.AP(pt3.tensor, pt3.offset + 64,
                                     [pt3.ap[0], [D, H], [1, 64]])
                        v.tensor_tensor(oa, ia, hbc(nsinc, 0, 64), ALU.mult)
                        ob2 = bass.AP(ob.tensor, ob.offset + 64,
                                      [ob.ap[0], [D, H], [1, 64]])
                        ib2 = bass.AP(pt3.tensor, pt3.offset,
                                      [pt3.ap[0], [D, H], [1, 64]])
                        v.tensor_tensor(ob2, ib2, hbc(nsinc, 64, 64), ALU.mult)
                        ro = vtile[:, pa, :]
                        v.tensor_add(ro, t1b[:], t2b[:])
                        v.tensor_mul(t1b[:], ro, ro)
                        v.tensor_reduce(nqt[:], hview(t1b[:]), AX.X, ALU.add)
                        v.signal_next(f"v_ev{tag}").tensor_max(
                            nqacc[:], nqacc[:], nqt[:])
                if kind == "q":
                    ph1_vec_pump(v, NT1)

    proj_tm("q", wqt, bq_r, TOWN // 512, "q", run=(upto >= 2))
    proj_tm("k", wkt, bk_r, S // 512, "k", run=(upto >= 2))
    proj_tm("v", wvt, bv_r, S // 512, "v", run=(upto >= 2))

    with phase("ph2d1", run=(upto >= 2)) as engs:
        if engs is not None:
            @engs.gpsimd
            def _(g):
                i1 = R_GS.issue(g, nrm_h[0], nqacc_q[:])
                i2 = R_GS.issue(g, nrm_h[1], nqacc_k[:])
                R_GS.wait(g, i1)
                R_GS.wait(g, i2)

    rec2d = {}
    with phase("ph2d2", run=(upto >= 2)) as engs:
        if engs is not None:
            @engs.sync
            def _(sy):
                for i in range(2):
                    rec2d["nl", i] = R_SY.issue(
                        sy, nrmt_s[:, i, :], nrm_h[i].rearrange("t h -> h t"),
                        allow_slow_non_contiguous=True)

            @engs.vector
            def _(v):
                R_SY.wait(v, rec2d["nl", 0])
                R_SY.wait(v, rec2d["nl", 1])
                v.tensor_reduce(mx_s[:, 0:1], nrmt_s[:, 0, :], AX.X, ALU.max)
                v.tensor_reduce(mx_s[:, 1:2], nrmt_s[:, 1, :], AX.X, ALU.max)
                v.signal_next("cd1").tensor_mul(mx_s[:, 2:3], mx_s[:, 0:1],
                                                 mx_s[:, 1:2])
                v.wait_ge(sem("cd2"), 1)
                v.signal_next("cd3").tensor_scalar_mul(mx_s[:, 3:4], mx_s[:, 3:4],
                                                        -INVSQ)

            @engs.scalar
            def _(sc):
                sc.wait_ge(sem("cd1"), 1)
                sc.signal_next("cd2").activation(mx_s[:, 3:4], mx_s[:, 2:3],
                                                  AF.Sqrt)

            @engs.gpsimd
            def _(g):
                g.wait_ge(sem("cd3"), 1)
                R_GS.wait(g, R_GS.issue(g, crow_h, mx_s[:, 3:4]))

    with phase("ph2d3", run=(upto >= 2)) as engs:
        if engs is not None:
            @engs.sync
            def _(sy):
                rec2d["cl"] = R_SY.issue(sy, crow_s[:],
                                         crow_h.rearrange("h one -> one h"),
                                         allow_slow_non_contiguous=True)

            @engs.tensor
            def _(t):
                R_SY.wait(t, rec2d["cl"])
                t.matmul(vps[:, 0, 0:H], ones_r[0:1, 0:128], crow_s[:],
                         start=True, stop=True).then_inc(sem("cd4"), 1)

            @engs.vector
            def _(v):
                v.wait_ge(sem("cd4"), 1)
                v.tensor_copy(cneg_s[:], vps[:, 0, 0:H])

    ph2_st.close()
    ph1_st.close()

    qw_chunks = []
    for r in range(F1 // 128):
        for c in range(2):
            qw_chunks.append((w1[r * 128:(r + 1) * 128, c * 1024:(c + 1) * 1024],
                              w1q_h[r * 128:(r + 1) * 128, c * 1024:(c + 1) * 1024]))
    for r in range(E // 128):
        for c in range(8):
            qw_chunks.append((w2[r * 128:(r + 1) * 128, c * 1024:(c + 1) * 1024],
                              w2q_h[r * 128:(r + 1) * 128, c * 1024:(c + 1) * 1024]))
    NQWC = len(qw_chunks)

    ph34_st = contextlib.ExitStack()
    av_all = sb("av_all", [128, H, TOWN], BF16, ph34_st)
    wo_s = sb("wo_s", [128, NEC, E], BF16, ph34_st)
    rec34 = {}

    ph3_st = contextlib.ExitStack()
    qw_in = sb("qw_in", [128, 4, 1024], F32, ph3_st)
    qxdv = sb("qxdv", [128, 1024], F32, ph3_st)
    qout = sb("qout", [128, 2, 1024], BF16, ph3_st)
    q3am = sb("q3am", [128, 32], F32, ph3_st)
    q3sc = sb("q3sc", [128, 32], F32, ph3_st)
    q3iv = sb("q3iv", [128, 32], F32, ph3_st)
    kth = sb("kth", [128, 2, S], BF16, ph3_st)
    qth = sb("qth", [128, 2, TOWN], BF16, ph3_st)
    vh3 = sb("vh3", [128, 2, 32, D], BF16, ph3_st)
    p_t = sb("p_t", [128, 4, 512], BF16, ph3_st)
    onec = sb("onec", [128, 1], BF16, ph3_st)
    rsum_s = sb("rsum_s", [1, 512], F32, ph3_st)
    rsb = sb("rsb", [128, 512], F32, ph3_st)
    sps = ps("sps", [128, 3, 512], ph3_st)
    avp = ps("avp", [128, 2, 512], ph3_st)
    ssum = ps("ssum", [128, 2, 512], ph3_st)
    rps = ps("rps", [128, 512], ph3_st)

    NKC = S // 128
    NKP = NKC // 2
    NQB = TOWN // 512

    with phase("ph3", run=(upto >= 3)) as engs:
        if engs is not None:
            @engs.gpsimd
            def _(g):
                g.signal_next("g_one").memset(onec[:], 1.0)
                emit_wload(g, R_GL, wot, rec34, dstbuf=wo_s)
                for i in range(NQWC):
                    g.wait_ge(sem("v_qw"), i + 1)
                    rec34["qwo", i] = R_GS.issue(g, qw_chunks[i][1],
                                                 qout[:, i % 2, :])
                R_GS.wait_all(g)

            @engs.sync
            def _(sy):
                for hh in range(H):
                    if hh >= 2:
                        sy.wait_ge(sem("t_h"), hh - 1)
                    pa = hh % 2
                    ld = [R_SY.issue(sy, kth[:, pa, :],
                                     k_tm[:, hh * 128:(hh + 1) * 128],
                                     transpose=True),
                          R_SY.issue(sy, qth[:, pa, :],
                                     q_tm[:, hh * 128:(hh + 1) * 128],
                                     transpose=True)]
                    for c8 in range(4):
                        ld.append(R_SY.issue(
                            sy, vh3[:, pa, c8 * 8:(c8 + 1) * 8, :],
                            v_h[hh, c8 * 1024:(c8 + 1) * 1024, :]
                            .rearrange("(c p) d -> p c d", p=128)))
                    rec34["ld", hh] = ld
                    for c4 in range(16):
                        i = hh * 16 + c4
                        if i < NQWC:
                            if i >= 4:
                                sy.wait_ge(sem("v_qw"), i - 3)
                            rec34["qwi", i] = R_SY.issue(sy, qw_in[:, i % 4, :],
                                                         qw_chunks[i][0])

            def av_one(t, hh, qb, kc, sums_first=False):
                gqi = hh * NQB + qb
                kcg = gqi * NKC + kc
                pslice = p_t[:, kcg % 4, :]
                if sums_first:
                    t.matmul(ssum[0:1, gqi % 2, :], onec[:], pslice,
                             start=(kc == 0), stop=(kc == NKC - 1))
                    t.matmul(avp[:, gqi % 2, :], vh3[:, hh % 2, kc, :], pslice,
                             start=(kc == 0), stop=(kc == NKC - 1)) \
                        .then_inc(sem("t_av"), 1)
                else:
                    t.matmul(avp[:, gqi % 2, :], vh3[:, hh % 2, kc, :], pslice,
                             start=(kc == 0), stop=(kc == NKC - 1))
                    t.matmul(ssum[0:1, gqi % 2, :], onec[:], pslice,
                             start=(kc == 0), stop=(kc == NKC - 1)) \
                        .then_inc(sem("t_av"), 1)

            @engs.tensor
            def _(t):
                t.wait_ge(sem("g_one"), 1)
                NGQ = H * NQB

                def prev(gqi):
                    return (gqi - 1) // NQB, (gqi - 1) % NQB

                for hh in range(H):
                    for i in rec34["ld", hh]:
                        R_SY.wait(t, i)
                    for qb in range(NQB):
                        gqi = hh * NQB + qb
                        for kc in range(NKC):
                            kcg = gqi * NKC + kc
                            if kcg >= 3:
                                t.wait_ge(sem("s_p"), kcg - 2)
                            t.matmul(sps[:, kcg % 3, :],
                                     kth[:, hh % 2, kc * 128:(kc + 1) * 128],
                                     qth[:, hh % 2, qb * 512:(qb + 1) * 512],
                                     start=True, stop=True).then_inc(sem("t_s"), 1)
                            if kc == 0:
                                if gqi >= 1:
                                    ph, pq = prev(gqi)
                                    t.wait_ge(sem("s_p"), gqi * NKC - 1)
                                    av_one(t, ph, pq, NKC - 2)
                            elif kc == 1:
                                if gqi >= 1:
                                    ph, pq = prev(gqi)
                                    t.wait_ge(sem("s_p"), gqi * NKC)
                                    av_one(t, ph, pq, NKC - 1)
                                    if qb == 0:
                                        t.drain().then_inc(sem("t_h"), 1)
                                    t.wait_ge(sem("v_rcp"), gqi)
                                    t.matmul(rps[:], ones_r[0:1, 0:128], rsum_s[:],
                                             start=True, stop=True) \
                                        .then_inc(sem("t_r"), 1)
                            else:
                                if kc == 2 and gqi >= 1:
                                    t.wait_ge(sem("v_av"), gqi - 1)
                                t.wait_ge(sem("s_p"), kcg - 1)
                                av_one(t, hh, qb, kc - 2)
                t.wait_ge(sem("s_p"), NGQ * NKC - 1)
                av_one(t, H - 1, NQB - 1, NKC - 2)
                t.wait_ge(sem("s_p"), NGQ * NKC)
                av_one(t, H - 1, NQB - 1, NKC - 1)
                t.wait_ge(sem("v_rcp"), NGQ)
                t.matmul(rps[:], ones_r[0:1, 0:128], rsum_s[:],
                         start=True, stop=True).then_inc(sem("t_r"), 1)

            @engs.scalar
            def _(sc):
                for gqi in range(H * NQB):
                    hh = gqi // NQB
                    for kc in range(NKC):
                        kcg = gqi * NKC + kc
                        sc.wait_ge(sem("t_s"), kcg + 1)
                        if kcg >= 4:
                            sc.wait_ge(sem("t_av"), kcg - 3)
                        sc.signal_next("s_p").activation(
                            p_t[:, kcg % 4, :], sps[:, kcg % 3, :],
                            AF.Exp, bias=cneg_s[:, hh:hh + 1], scale=INVSQ)

            @engs.vector
            def _(v):
                for gqi in range(H * NQB):
                    hh, qb = gqi // NQB, gqi % NQB
                    v.wait_ge(sem("t_av"), (gqi + 1) * NKC)
                    v.signal_next("v_rcp").tensor_copy(rsum_s[:],
                                                       ssum[0:1, gqi % 2, :])
                    v.wait_ge(sem("t_r"), gqi + 1)
                    v.reciprocal(rsb[:], rps[:])
                    v.signal_next("v_av").tensor_tensor(
                        av_all[:, hh, qb * 512:(qb + 1) * 512],
                        avp[:, gqi % 2, :], rsb[:], ALU.mult)
                    for c4 in range(4):
                        i = gqi * 4 + c4
                        if i < NQWC:
                            R_SY.wait(v, rec34["qwi", i])
                            if i >= 2:
                                R_GS.wait(v, rec34["qwo", i - 2])
                            emit_quant2(v, qw_in[:, i % 4, :], qout[:, i % 2, :],
                                        32, q3am, q3sc, q3iv, qxdv, None,
                                        sig="v_qw")

    ph3_st.close()

    ph4_st = contextlib.ExitStack()
    aot = sb("aot", [128, 2, E], BF16, ph4_st)
    bo_r = sb("bo_r", [1, E], F32, ph4_st)
    aops = ps("aops", [128, 2, E], ph4_st)

    with phase("ph4", run=(upto >= 4)) as engs:
        if engs is not None:
            @engs.gpsimd
            def _(g):
                for tcg in range(TOWN // 128):
                    g.wait_ge(sem("v_eo"), tcg + 1)
                    rec34["aostore", tcg] = R_GS.issue(
                        g, ao_h[tcg * 128:(tcg + 1) * 128, :], aot[:, tcg % 2, :])
                R_GS.wait_all(g)

            @engs.sync
            def _(sy):
                rec34["bo"] = R_SY.issue(sy, bo_r[:], bo[None, :])

            @engs.tensor
            def _(t):
                for i in rec34["wl"]:
                    R_GL.wait(t, i)
                R_SY.wait(t, rec34["bo"])
                for tcg in range(TOWN // 128):
                    if tcg >= 2:
                        t.wait_ge(sem("v_eo"), tcg - 1)
                    pt = aops[:, tcg % 2, :]
                    tsl = slice((tcg % 4) * 128 + (tcg // 4) * 512,
                                (tcg % 4) * 128 + (tcg // 4) * 512 + 128)
                    for os_ in range(4):
                        po = pt[:, os_ * 512:(os_ + 1) * 512]
                        for hh in range(H):
                            t.matmul(po, av_all[:, hh, tcg * 128:(tcg + 1) * 128],
                                     wo_s[:, hh, os_ * 512:(os_ + 1) * 512],
                                     start=(hh == 0), stop=False)
                        mm = t.matmul(po, ones_r[0:1, 0:128],
                                      bo_r[0:1, os_ * 512:(os_ + 1) * 512],
                                      start=False, stop=True)
                    mm.then_inc(sem("t_mo"), 1)

            @engs.vector
            def _(v):
                for tcg in range(TOWN // 128):
                    v.wait_ge(sem("t_mo"), tcg + 1)
                    if tcg >= 2:
                        R_GS.wait(v, rec34["aostore", tcg - 2])
                    for fs in range(4):
                        e2 = v.signal_next("v_eo") if fs == 3 else v
                        e2.tensor_copy(aot[:, tcg % 2, fs * 512:(fs + 1) * 512],
                                       aops[:, tcg % 2, fs * 512:(fs + 1) * 512])

    ph4_st.close()
    ph34_st.close()

    rec5b = {}
    ph5b_outer = contextlib.ExitStack()
    xlqt = sb("xlqt", [128, NEC, TOWN], BF16, ph5b_outer)
    ph5a_st = contextlib.ExitStack()
    xft = sb("xft", [128, 2, E], BF16, ph5a_st)
    jnk5 = sb("jnk5", [128, 2, E], F32, ph5a_st)
    xlnp = sb("xlnp", [128, E], F32, ph5a_st)
    xlnw = sb("xlnw", [128, E], F32, ph5a_st)
    xln5 = sb("xln5", [128, E], BF16, ph5a_st)
    stat5 = sb("stat5", [128, 2, 8], F32, ph5a_st)
    lnw_s = sb("lnw_s", [128, E], F32, ph5a_st)
    lnb_s = sb("lnb_s", [128, E], F32, ph5a_st)
    q5am = sb("q5am", [128, 64], F32, ph5a_st)
    q5sc = sb("q5sc", [128, 64], F32, ph5a_st)
    q5iv = sb("q5iv", [128, 64], F32, ph5a_st)
    q5dv = sb("q5dv", [128, E], F32, ph5a_st)
    q5y = sb("q5y", [128, E], F16, ph5a_st)
    xlq5 = sb("xlq5", [128, 2, E], BF16, ph5a_st)


    rec5 = {}
    NT5 = TOWN // 128

    with phase("ph5a", run=(upto >= 5)) as engs:
        if engs is not None:
            @engs.gpsimd
            def _(g):
                for i in range(NT5):
                    g.wait_ge(sem("v_x5"), i + 1)
                    rec5["xq", i] = R_GS.issue(
                        g, xlnq_h[i * 128:(i + 1) * 128, :], xlq5[:, i % 2, :])
                R_GS.wait_all(g)

            @engs.sync
            def _(sy):
                rec5["lnw"] = R_SY.issue(sy, lnw_s[:], lnw_t)
                rec5["lnb"] = R_SY.issue(sy, lnb_s[:], lnb_t)
                for i in range(NT5):
                    if i >= 2:
                        sy.wait_ge(sem("v_x5"), i - 1)
                    rec5["xf", i] = R_SY.issue(sy, xft[:, i % 2, :],
                                               xn_h[i * 128:(i + 1) * 128, :])
                    if i >= 4:
                        j = i - 4
                        R_GS.wait(sy, rec5["xq", j])
                        for ec in range(NEC):
                            rec5b["xlt", ec, j] = R_SY.issue(
                                sy, xlqt[:, ec, j * 128:(j + 1) * 128],
                                xlnq_h[j * 128:(j + 1) * 128,
                                       ec * 128:(ec + 1) * 128],
                                transpose=True)
                for j in range(NT5 - 4, NT5):
                    R_GS.wait(sy, rec5["xq", j])
                    for ec in range(NEC):
                        rec5b["xlt", ec, j] = R_SY.issue(
                            sy, xlqt[:, ec, j * 128:(j + 1) * 128],
                            xlnq_h[j * 128:(j + 1) * 128,
                                   ec * 128:(ec + 1) * 128],
                            transpose=True)

            @engs.scalar
            def _(sc):
                for i in range(NT5):
                    pa = i % 2
                    R_SY.wait(sc, rec5["xf", i])
                    xt = xft[:, pa, :]
                    sc.activation(jnk5[:, 0, :], xt, AF.Copy,
                                  accum_out=stat5[:, pa, 0:1])
                    sc.signal_next("s_st5").activation(
                        jnk5[:, 1, :], xt, AF.Square, accum_out=stat5[:, pa, 1:2])
                    sc.wait_ge(sem("v_t5"), i + 1)
                    sc.signal_next("s_sq5").activation(stat5[:, pa, 7:8],
                                                        stat5[:, pa, 6:7], AF.Sqrt)
                    sc.wait_ge(sem("v_st5"), i + 1)
                    sc.signal_next("s_af5").activation(
                        xlnp[:], xt, AF.Identity,
                        scale=stat5[:, pa, 4:5], bias=stat5[:, pa, 5:6])

            @engs.vector
            def _(v):
                R_SY.wait(v, rec5["lnw"])
                R_SY.wait(v, rec5["lnb"])
                inv_n = 1.0 / E
                for i in range(NT5):
                    pa = i % 2
                    st_ = stat5[:, pa]
                    v.wait_ge(sem("s_st5"), i + 1)
                    v.tensor_scalar_mul(st_[:, 2:3], st_[:, 0:1], inv_n)
                    v.tensor_scalar_mul(st_[:, 3:4], st_[:, 1:2], inv_n)
                    v.scalar_tensor_tensor(st_[:, 6:7], st_[:, 2:3], -1.0,
                                           st_[:, 2:3], ALU.mult, ALU.mult)
                    v.scalar_tensor_tensor(st_[:, 6:7], st_[:, 3:4], 1.0,
                                           st_[:, 6:7], ALU.mult, ALU.add)
                    v.signal_next("v_t5").tensor_scalar_add(st_[:, 6:7],
                                                             st_[:, 6:7], EPS)
                    v.wait_ge(sem("s_sq5"), i + 1)
                    v.reciprocal(st_[:, 4:5], st_[:, 7:8])
                    for _ in range(2):
                        v.scalar_tensor_tensor(st_[:, 7:8], st_[:, 4:5], 1.0,
                                               st_[:, 4:5], ALU.mult, ALU.mult)
                        v.scalar_tensor_tensor(st_[:, 7:8], st_[:, 7:8], -0.5,
                                               st_[:, 6:7], ALU.mult, ALU.mult)
                        v.tensor_scalar_add(st_[:, 7:8], st_[:, 7:8], 1.5)
                        v.scalar_tensor_tensor(st_[:, 4:5], st_[:, 7:8], 1.0,
                                               st_[:, 4:5], ALU.mult, ALU.mult)
                    v.signal_next("v_st5").scalar_tensor_tensor(
                        st_[:, 5:6], st_[:, 2:3], -1.0, st_[:, 4:5],
                        ALU.mult, ALU.mult)
                    v.wait_ge(sem("s_af5"), i + 1)
                    v.tensor_mul(xlnw[:], xlnp[:], lnw_s[:])
                    v.tensor_add(xln5[:], xlnw[:], lnb_s[:])
                    if i >= 2:
                        R_GS.wait(v, rec5["xq", i - 2])
                    emit_quant2(v, xln5[:], xlq5[:, pa, :], 64,
                                q5am, q5sc, q5iv, q5dv, q5y, sig="v_x5")

    ph5a_st.close()

    ph5b_st = contextlib.ExitStack()
    w1t = sb("w1t", [128, 2, NEC, 512], BF16, ph5b_st)
    g1b = sb("g1b", [128, 4, 512], BF16, ph5b_st)
    g2b = sb("g2b", [128, 4, 512], BF16, ph5b_st)
    gqb = sb("gqb", [128, 4, 512], BF16, ph5b_st)
    b1s = sb("b1s", [128, F1], F32, ph5b_st)
    wbt = sb("wbt", [128, 4, 512], BF16, ph5b_st)
    q6am = sb("q6am", [128, 4, 16], F32, ph5b_st)
    q6sc = sb("q6sc", [128, 4, 16], F32, ph5b_st)
    q6iv = sb("q6iv", [128, 4, 16], F32, ph5b_st)
    q6dv = sb("q6dv", [128, 4, 512], F32, ph5b_st)
    h1ps = ps("h1ps", [128, 4, 512], ph5b_st)

    wbl_ring = Ring("wbl", 4)
    NFS = F1 // 512
    NTC5 = TOWN // 128
    NG5 = NFS * NTC5
    NWB = (E * F1 * 2) // (512 * 128 * 2)

    with phase("ph5b", run=(upto >= 5)) as engs:
        if engs is not None:
            @engs.gpsimd
            def _(g):
                for i in range(NG5):
                    g.wait_ge(sem("v_gq"), i + 1)
                    fs, tcb = i // NTC5, i % NTC5
                    rows = slice(tcb * 128, (tcb + 1) * 128)
                    rec5b["gq", i] = R_GS.issue(
                        g, gq_h[rows, fs * 512:(fs + 1) * 512], gqb[:, i % 4, :])
                    wbl_ring.wait(g, i)
                    os_, fc = i // 64, i % 64
                    rec5b["wbo", i] = R_GS.issue(
                        g, w2qT_h[fc * 128:(fc + 1) * 128,
                                  os_ * 512:(os_ + 1) * 512], wbt[:, i % 4, :])
                R_GS.wait_all(g)

            @engs.sync
            def _(sy):
                rec5b["b1"] = R_SY.issue(sy, b1s[:], b1_t)
                for fs in range(NFS):
                    if fs >= 2:
                        sy.wait_ge(sem("t_fs"), fs - 1)
                    for ec in range(NEC):
                        rec5b["w1", fs, ec] = R_SY.issue(
                            sy, w1t[:, fs % 2, ec, :],
                            w1q_h[fs * 512:(fs + 1) * 512,
                                  ec * 128:(ec + 1) * 128],
                            transpose=True)
                    for k in range(16):
                        i = fs * 16 + k
                        os_, fc = i // 64, i % 64
                        if i >= 4:
                            R_GS.wait(sy, rec5b["wbo", i - 4])
                        wbl_ring.issue(sy, wbt[:, i % 4, :],
                                       w2q_h[os_ * 512:(os_ + 1) * 512,
                                             fc * 128:(fc + 1) * 128],
                                       transpose=True)


            @engs.tensor
            def _(t):
                for ec in range(NEC):
                    for i in range(NT5):
                        R_SY.wait(t, rec5b["xlt", ec, i])
                for fs in range(NFS):
                    for ec in range(NEC):
                        R_SY.wait(t, rec5b["w1", fs, ec])
                    for tcb in range(NTC5):
                        gidx = fs * NTC5 + tcb
                        if gidx >= 4:
                            t.wait_ge(sem("v_e1"), gidx - 3)
                        po = h1ps[:, gidx % 4, :]
                        for ec in range(NEC):
                            mm1 = t.matmul(po, xlqt[:, ec, tcb * 128:(tcb + 1) * 128],
                                           w1t[:, fs % 2, ec, :],
                                           start=(ec == 0), stop=(ec == NEC - 1))
                        mm1.then_inc(sem("t_m1"), 1)
                    t.drain().then_inc(sem("t_fs"), 1)

            @engs.scalar
            def _(sc):
                for gidx in range(NG5):
                    pa = gidx % 4
                    sc.wait_ge(sem("v_e1"), gidx + 1)
                    if gidx >= 4:
                        sc.wait_ge(sem("v_gq"), gidx - 3)
                    sc.signal_next("s_g1").activation(g2b[:, pa, :], g1b[:, pa, :],
                                                       AF.Gelu_apprx_tanh)

            @engs.vector
            def _(v):
                R_SY.wait(v, rec5b["b1"])
                for gidx in range(NG5):
                    pa = gidx % 4
                    v.wait_ge(sem("t_m1"), gidx + 1)
                    if gidx >= 4:
                        v.wait_ge(sem("s_g1"), gidx - 3)
                    v.signal_next("v_e1").tensor_tensor(
                        g1b[:, pa, :], h1ps[:, pa, :],
                        b1s[:, (gidx // NTC5) * 512:(gidx // NTC5 + 1) * 512],
                        ALU.add)
                    if gidx >= 1:
                        v.wait_ge(sem("s_g1"), gidx)
                        if gidx >= 5:
                            R_GS.wait(v, rec5b["gq", gidx - 5])
                        emit_quant2(v, g2b[:, (gidx - 1) % 4, :],
                                    gqb[:, (gidx - 1) % 4, :], 16,
                                    q6am[:, (gidx - 1) % 4, :],
                                    q6sc[:, (gidx - 1) % 4, :],
                                    q6iv[:, (gidx - 1) % 4, :],
                                    q6dv[:, (gidx - 1) % 4, :], None, sig="v_gq")
                g_last = NG5 - 1
                v.wait_ge(sem("s_g1"), g_last + 1)
                R_GS.wait(v, rec5b["gq", g_last - 4])
                emit_quant2(v, g2b[:, g_last % 4, :],
                            gqb[:, g_last % 4, :], 16,
                            q6am[:, g_last % 4, :], q6sc[:, g_last % 4, :],
                            q6iv[:, g_last % 4, :], q6dv[:, g_last % 4, :], None,
                            sig="v_gq")

    ph5b_st.close()
    ph5b_outer.close()

    ph5c_st = contextlib.ExitStack()
    gqt = sb("gqt", [128, 64, 512], BF16, ph5c_st)
    rec5c = {}
    w2t = sb("w2t", [128, 2, 32, 512], BF16, ph5c_st)
    h2s = sb("h2s", [128, 4, E], BF16, ph5c_st)
    xf6 = sb("xf6", [128, 2, E], BF16, ph5c_st)
    xr6 = sb("xr6", [128, 2, E], F32, ph5c_st)
    ao6 = sb("ao6", [128, 2, E], BF16, ph5c_st)
    ob6 = sb("ob6", [128, 2, E], F32, ph5c_st)
    fo6 = sb("fo6", [128, E], BF16, ph5c_st)
    b2_r = sb("b2_r", [1, E], F32, ph5c_st)
    h2ps = ps("h2ps", [128, 4, 512], ph5c_st)

    with phase("ph5c", run=(upto >= 5)) as engs:
        if engs is not None:
            @engs.gpsimd
            def _(g):
                for gt in range(16):
                    g.wait_ge(sem("v_fin"), gt + 1)
                    rec5c["out", gt] = R_GS.issue(
                        g, out[gt * 128:(gt + 1) * 128, :], ob6[:, gt % 2, :])
                R_GS.wait_all(g)

            @engs.sync
            def _(sy):
                rec5c["b2"] = R_SY.issue(sy, b2_r[:], b2[None, :])

                def w2half(os_, hf):
                    gos_ = w2half.tsb * 4 + os_
                    if gos_ >= 1:
                        sy.wait_ge(sem("t_hf"), (gos_ - 1) * 2 + hf + 1)
                    for fc in range(32):
                        rec5c["w2", gos_, hf, fc] = R_SY.issue(
                            sy, w2t[:, hf, fc, :],
                            w2qT_h[(hf * 32 + fc) * 128:
                                   (hf * 32 + fc + 1) * 128,
                                   os_ * 512:(os_ + 1) * 512])

                for tsb in range(4):
                    w2half.tsb = tsb
                    w2half(0, 0)
                    if tsb >= 1:
                        sy.wait_ge(sem("t_tsb"), tsb)
                    for fc in range(64):
                        rec5c["gqt", tsb, fc] = R_SY.issue(
                            sy, gqt[:, fc, :],
                            gq_h[tsb * 512:(tsb + 1) * 512,
                                 fc * 128:(fc + 1) * 128],
                            transpose=True)
                    for os_ in range(1, 4):
                        w2half(os_, 0)

            @engs.scalar
            def _(sc):
                for tsb in range(4):
                    for os_ in range(4):
                        gos = tsb * 4 + os_
                        hf = 1
                        if gos >= 1:
                            sc.wait_ge(sem("t_hf"), (gos - 1) * 2 + hf + 1)
                        for fc in range(32):
                            rec5c["w2", gos, hf, fc] = R_SC.issue(
                                sc, w2t[:, hf, fc, :],
                                w2qT_h[(hf * 32 + fc) * 128:
                                       (hf * 32 + fc + 1) * 128,
                                       os_ * 512:(os_ + 1) * 512])
                    for tt_ in range(4):
                        gt = tsb * 4 + tt_
                        if gt >= 2:
                            sc.wait_ge(sem("v_fin"), gt - 1)
                        rows = slice(gt * 128, (gt + 1) * 128)
                        rec5c["xr", gt] = R_SC.issue(sc, xr6[:, gt % 2, :], xs[rows, :])
                        rec5c["aor", gt] = R_SC.issue(sc, ao6[:, gt % 2, :], ao_h[rows, :])
                        rec5c["xfr", gt] = R_SC.issue(sc, xf6[:, gt % 2, :], xn_h[rows, :])

            @engs.tensor
            def _(t):
                R_SY.wait(t, rec5c["b2"])
                for tsb in range(4):
                    for os_ in range(4):
                        gos = tsb * 4 + os_
                        for hf in range(2):
                            w2r = R_SY if hf == 0 else R_SC
                            for ii in range(32):
                                w2r.wait(t, rec5c["w2", gos, hf, ii])
                            if hf == 0 and gos >= 1:
                                for tt_ in range(4):
                                    t.wait_ge(sem("v_e2"),
                                              (gos - 1) * 4 + tt_ + 1)
                            for fc in range(32):
                                if os_ == 0:
                                    R_SY.wait(t, rec5c["gqt", tsb,
                                                       hf * 32 + fc])
                                for tt_ in range(4):
                                    t.matmul(h2ps[:, tt_, :],
                                             gqt[:, hf * 32 + fc,
                                                 tt_ * 128:(tt_ + 1) * 128],
                                             w2t[:, hf, fc, :],
                                             start=(hf == 0 and fc == 0),
                                             stop=False)
                            t.drain().then_inc(sem("t_hf"), 1)
                        for tt_ in range(4):
                            t.matmul(h2ps[:, tt_, :], ones_r[0:1, 0:128],
                                     b2_r[0:1, os_ * 512:(os_ + 1) * 512],
                                     start=False, stop=True) \
                                .then_inc(sem("t_m2"), 1)
                    t.drain().then_inc(sem("t_tsb"), 1)

            @engs.vector
            def _(v):
                for tsb in range(4):
                    for os_ in range(4):
                        for tt_ in range(4):
                            gidx = (tsb * 4 + os_) * 4 + tt_
                            v.wait_ge(sem("t_m2"), gidx + 1)
                            v.signal_next("v_e2").tensor_copy(
                                h2s[:, tt_, os_ * 512:(os_ + 1) * 512],
                                h2ps[:, tt_, :])
                    for tt_ in range(4):
                        gt = tsb * 4 + tt_
                        R_SC.wait(v, rec5c["xr", gt])
                        R_SC.wait(v, rec5c["aor", gt])
                        R_SC.wait(v, rec5c["xfr", gt])
                        if gt >= 2:
                            R_GS.wait(v, rec5c["out", gt - 2])
                        v.tensor_add(fo6[:], xf6[:, gt % 2, :], h2s[:, tt_, :])
                        v.scalar_tensor_tensor(ob6[:, gt % 2, :], ao6[:, gt % 2, :],
                                               0.0, xr6[:, gt % 2, :],
                                               ALU.bypass, ALU.add)
                        v.signal_next("v_fin").tensor_add(ob6[:, gt % 2, :],
                                                           ob6[:, gt % 2, :], fo6[:])


    ph5c_st.close()

    stack.close()
    return nc




_ROPE_BASE = 10000.0


def _rope_tables():
    inv_freq = 1.0 / _ROPE_BASE ** (np.arange(0, D, 2, dtype=np.float32) / D)
    pos = np.arange(S, dtype=np.float32)
    freqs = np.outer(pos, inv_freq)
    emb = np.concatenate([freqs, freqs], axis=-1)
    cos = np.cos(emb).astype(np.float32)
    sin = np.sin(emb).astype(np.float32)
    nsin = np.concatenate([-sin[:, :64], sin[:, 64:]], axis=1)
    return cos, nsin


def _stage(inputs, core, shared, cos, nsin):
    b, h = core // 2, core % 2
    x = inputs["L0_attn_x"]
    perm = np.concatenate([np.arange(h * TOWN, (h + 1) * TOWN),
                           np.arange((1 - h) * TOWN, (2 - h) * TOWN)])
    d = dict(shared)
    d["xs"] = np.ascontiguousarray(x[b][perm])
    d["cos_t"] = np.ascontiguousarray(cos[perm])
    d["nsin_t"] = np.ascontiguousarray(nsin[perm])
    return d


def kernel(**inputs):
    from concourse.bass_utils import run_bass_kernel_spmd
    inputs = {k: np.ascontiguousarray(np.asarray(v, dtype=np.float32))
              for k, v in inputs.items()}
    shared = {
        "wqt": np.ascontiguousarray(inputs["L0_attn_Wq"].T),
        "wkt": np.ascontiguousarray(inputs["L0_attn_Wk"].T),
        "wvt": np.ascontiguousarray(inputs["L0_attn_Wv"].T),
        "wot": np.ascontiguousarray(inputs["L0_attn_Wo"].T),
        "w1": inputs["L0_ffn_W1"],
        "w2": inputs["L0_ffn_W2"],
        "bq": inputs["L0_attn_bq"],
        "bk": inputs["L0_attn_bk"],
        "bv": inputs["L0_attn_bv"],
        "bo": inputs["L0_attn_bo"],
        "b1": inputs["L0_ffn_b1"],
        "b2": inputs["L0_ffn_b2"],
        "b1_t": np.ascontiguousarray(
            np.broadcast_to(inputs["L0_ffn_b1"], (128, F1))),
        "lnw_t": np.ascontiguousarray(
            np.broadcast_to(inputs["L0_ffn_ln_w"], (128, E))),
        "lnb_t": np.ascontiguousarray(
            np.broadcast_to(inputs["L0_ffn_ln_b"], (128, E))),
    }
    cos, nsin = _rope_tables()
    in_maps = [_stage(inputs, c, shared, cos, nsin) for c in range(8)]
    nc = build()
    res = run_bass_kernel_spmd(nc, in_maps, core_ids=list(range(8)))
    outp = np.empty((B, S, E), dtype=np.float32)
    for c in range(8):
        b, h = c // 2, c % 2
        outp[b, h * TOWN:(h + 1) * TOWN, :] = res.results[c]["out"]
    return outp


# version 19
# speedup vs baseline: 1.1412x; 1.0360x over previous
import math
import contextlib
import numpy as np
import concourse.bass as bass
import concourse.mybir as mybir

F32 = mybir.dt.float32
BF16 = mybir.dt.bfloat16
F16 = mybir.dt.float16
I32 = mybir.dt.int32
AF = mybir.ActivationFunctionType
ALU = mybir.AluOpType
AX = mybir.AxisListType

B, S, E, H, D = 4, 4096, 2048, 16, 128
F1 = 8192
TOWN = 2048
NEC = E // 128
INVSQ = 1.0 / math.sqrt(D)
EPS = 1e-5

QCLAMP = 2.0 ** -40
EXPMASK = 0x7F800000
SIXSHIFT = 6 << 23
INVBASE = 254 << 23
MAGIC = 12582912.0
PRECLIP = 127.4

NENG = 5


def bc(ap2d, nblk, bk):
    return bass.AP(ap2d.tensor, ap2d.offset, [ap2d.ap[0], [ap2d.ap[1][0], nblk], [0, bk]])


def r3(ap2d, bk):
    return ap2d.rearrange("p (b k) -> p b k", k=bk)


def build(upto=99, dbg=()):
    nc = bass.Bass("TRN2", target_bir_lowering=False, debug=False)
    stack = contextlib.ExitStack()
    sems = {}

    def sem(name):
        if name not in sems:
            sems[name] = stack.enter_context(nc.semaphore(name))
        return sems[name]

    class Ring:

        def __init__(self, name, n):
            self.name, self.n = name, n
            self.issued = 0

        def issue(self, eng, out_ap, in_ap, **kw):
            i = self.issued
            slot, rnd = i % self.n, i // self.n
            s = sem(f"{self.name}{slot}")
            if rnd > 0:
                eng.wait_ge(s, 16 * rnd)
            eng.dma_start(out_ap, in_ap, **kw).then_inc(s, 16)
            self.issued += 1
            return i

        def wait(self, eng, i):
            eng.wait_ge(sem(f"{self.name}{i % self.n}"), 16 * (i // self.n + 1))

        def wait_all(self, eng):
            for slot in range(min(self.n, self.issued)):
                cnt = (self.issued - 1 - slot) // self.n + 1
                eng.wait_ge(sem(f"{self.name}{slot}"), 16 * cnt)

    R_SY = Ring("rsy", 16)
    R_GS = Ring("rgs", 8)
    R_GL = Ring("rgl", 4)
    R_SC = Ring("rsc", 6)

    def dram(name, shape, dtype, kind):
        return nc.dram_tensor(name, shape, dtype, kind=kind).ap()

    xs = dram("xs", [S, E], F32, "ExternalInput")
    wqt = dram("wqt", [E, E], F32, "ExternalInput")
    wkt = dram("wkt", [E, E], F32, "ExternalInput")
    wvt = dram("wvt", [E, E], F32, "ExternalInput")
    wot = dram("wot", [E, E], F32, "ExternalInput")
    w1 = dram("w1", [F1, E], F32, "ExternalInput")
    w2 = dram("w2", [E, F1], F32, "ExternalInput")
    bq = dram("bq", [E], F32, "ExternalInput")
    bkk = dram("bk", [E], F32, "ExternalInput")
    bv = dram("bv", [E], F32, "ExternalInput")
    bo = dram("bo", [E], F32, "ExternalInput")
    b1 = dram("b1", [F1], F32, "ExternalInput")
    b2 = dram("b2", [E], F32, "ExternalInput")
    b1_t = dram("b1_t", [128, F1], F32, "ExternalInput")
    lnw_t = dram("lnw_t", [128, E], F32, "ExternalInput")
    lnb_t = dram("lnb_t", [128, E], F32, "ExternalInput")
    cos_t = dram("cos_t", [S, D], F32, "ExternalInput")
    nsin_t = dram("nsin_t", [S, D], F32, "ExternalInput")

    out = dram("out", [TOWN, E], F32, "ExternalOutput")

    def scratch(name, shape, dtype):
        kind = "ExternalOutput" if name in dbg else "Internal"
        return dram(name, shape, dtype, kind)

    xn_h = scratch("xn_h", [S, E], BF16)
    q_tm = scratch("q_tm", [TOWN, E], BF16)
    k_tm = scratch("k_tm", [S, E], BF16)
    v_h = scratch("v_h", [H, S, D], BF16)
    nrm_h = scratch("nrm_h", [2, 128, H], F32)
    crow_h = scratch("crow_h", [H, 1], F32)
    av_h = scratch("av_h", [H, TOWN, D], BF16)
    ao_h = scratch("ao_h", [TOWN, E], BF16)
    xlnq_h = scratch("xlnq_h", [TOWN, E], BF16)
    gq_h = scratch("gq_h", [TOWN, F1], BF16)
    w1q_h = scratch("w1q_h", [F1, E], BF16)
    w2q_h = scratch("w2q_h", [E, F1], BF16)
    w2qT_h = scratch("w2qT_h", [F1, E], BF16)
    cdbg_h = scratch("cdbg_h", [1, 4 * H], F32)

    def sb(name, shape, dtype, st=stack):
        return st.enter_context(nc.sbuf_tensor(name, shape, dtype))

    def ps(name, shape, st):
        return st.enter_context(nc.psum_tensor(name, shape, F32))

    ones_r = sb("ones_r", [1, 512], F32)
    ones_b = sb("ones_b", [1, 512], BF16)
    cneg_s = sb("cneg_s", [128, H], F32)

    fence = sem("fence")
    FCT = [0]

    CHAIN_OPS = {
        "memset", "memzero", "tensor_copy", "tensor_tensor", "tensor_add",
        "tensor_sub", "tensor_mul", "tensor_max", "tensor_scalar",
        "tensor_single_scalar", "tensor_scalar_add", "tensor_scalar_sub",
        "tensor_scalar_mul", "tensor_scalar_max", "tensor_scalar_min",
        "tensor_relu", "tensor_reduce", "reduce_max", "reduce_sum",
        "scalar_tensor_tensor", "reciprocal", "select", "activation",
        "square", "sqrt", "copy", "add", "mul", "sign",
    }
    chain_state = {"scalar": 0, "vector": 0, "gpsimd": 0}

    ext_counts = {}
    chain_ext = {"scalar": None, "vector": None, "gpsimd": None}

    class ChainEng:

        def __init__(self, eng, name):
            self._eng = eng
            self._name = name
            self._sem = sem(f"ch_{name}")
            self._pending_sig = None

        def signal_next(self, signame):
            self._pending_sig = signame
            return self

        def __getattr__(self, attr):
            fn = getattr(self._eng, attr)
            if attr not in CHAIN_OPS:
                return fn

            def wrapped(*a, **k):
                c = chain_state[self._name]
                if c > 0:
                    self._eng.wait_ge(self._sem, c)
                ext = chain_ext[self._name]
                if ext is not None:
                    self._eng.wait_ge(sem(ext[0]), ext[1])
                    chain_ext[self._name] = None
                ins = fn(*a, **k)
                if self._pending_sig is not None:
                    nm = self._pending_sig
                    self._pending_sig = None
                    val = ext_counts.get(nm, 0) + 1
                    ext_counts[nm] = val
                    ins.then_inc(sem(nm), 1)
                    chain_ext[self._name] = (nm, val)
                else:
                    ins.then_inc(self._sem, 1)
                    chain_state[self._name] = c + 1
                return ins
            return wrapped

        def drain(self):
            return self._eng.drain()

    @contextlib.contextmanager
    def phase(name, run=True):
        class Engs:
            pass
        if not run:
            yield None
            return
        with nc.Block() as blk:
            bodies = {}

            class Reg:
                def __getattr__(self, eng):
                    def reg(fn):
                        bodies[eng] = fn
                        return fn
                    return reg
            yield Reg()
            base = FCT[0]
            for eng_name in ("gpsimd", "sync", "scalar", "vector", "tensor"):
                body = bodies.get(eng_name)
                dec = getattr(blk, eng_name)

                def make(body_fn, engn):
                    def _(eng):
                        eng.wait_ge(fence, base)
                        if body_fn is not None:
                            if engn in ("scalar", "vector", "gpsimd"):
                                body_fn(ChainEng(eng, engn))
                            else:
                                body_fn(eng)
                        eng.drain().then_inc(fence, 1)
                    return _
                dec(make(body, eng_name))
            FCT[0] = base + NENG

    def emit_quant2(v, src_ap, out_ap, nblk, am, scl, inv, xdv, y16, sig=None):
        fd = nblk * 32
        v.tensor_reduce(am[:, :nblk], r3(src_ap, 32), AX.X, ALU.max,
                        apply_absolute_value=True)
        v.tensor_scalar_max(am[:, :nblk], am[:, :nblk], QCLAMP)
        v.tensor_single_scalar(scl[:, :nblk].bitcast(I32),
                               am[:, :nblk].bitcast(I32), EXPMASK,
                               ALU.bitwise_and)
        v.tensor_single_scalar(scl[:, :nblk].bitcast(I32),
                               scl[:, :nblk].bitcast(I32), SIXSHIFT,
                               ALU.subtract)
        v.tensor_scalar(inv[:, :nblk].bitcast(I32), scl[:, :nblk].bitcast(I32),
                        -1, INVBASE, ALU.mult, ALU.add)
        v.tensor_tensor(r3(xdv[:, :fd], 32), r3(src_ap, 32),
                        bc(inv[:, :nblk], nblk, 32), ALU.mult)
        v.tensor_scalar(xdv[:, :fd], xdv[:, :fd], PRECLIP, MAGIC,
                        ALU.min, ALU.add)
        if sig is not None:
            v.signal_next(sig)
        ins = v.scalar_tensor_tensor(r3(out_ap, 32), r3(xdv[:, :fd], 32), MAGIC,
                                     bc(scl[:, :nblk], nblk, 32),
                                     ALU.subtract, ALU.mult)
        return ins

    with phase("ph0") as engs:
        @engs.vector
        def _(v):
            v.memset(ones_r[:], 1.0)
            v.memset(ones_b[:], 1.0)

    NT1 = S // 128
    ph1_st = contextlib.ExitStack()
    xbuf = sb("xbuf", [128, 2, E], F32, ph1_st)
    jnk = sb("jnk", [128, 2, E], F32, ph1_st)
    xnbuf = sb("xnbuf", [128, 2, E], BF16, ph1_st)
    stat = sb("stat", [128, 2, 8], F32, ph1_st)

    rec1 = {}

    def ph1_sync_chunk(sy, lo, hi):
        for i in range(lo, min(hi, NT1)):
            if i >= 2:
                sy.wait_ge(sem("s_af"), i - 1)
            rec1["x", i] = R_SY.issue(sy, xbuf[:, i % 2, :],
                                      xs[i * 128:(i + 1) * 128, :])

    def ph1_scalar(sc):
        for i in range(NT1):
            pa = i % 2
            R_SY.wait(sc, rec1["x", i])
            xt = xbuf[:, pa, :]
            sc.activation(jnk[:, 0, :], xt, AF.Copy,
                          accum_out=stat[:, pa, 0:1])
            sc.signal_next("s_st").activation(jnk[:, 1, :], xt, AF.Square,
                          accum_out=stat[:, pa, 1:2])
            sc.wait_ge(sem("v_t"), i + 1)
            sc.signal_next("s_sq").activation(stat[:, pa, 7:8],
                                              stat[:, pa, 6:7], AF.Sqrt)
            sc.wait_ge(sem("v_st"), i + 1)
            if i >= 2:
                R_GS.wait(sc, rec1["xn", i - 2])
            sc.signal_next("s_af").activation(xnbuf[:, pa, :], xt, AF.Identity,
                          scale=stat[:, pa, 4:5], bias=stat[:, pa, 5:6])

    def ph1_vec_pump(v, upto_i):
        inv_n = 1.0 / E
        while ph1_vec_pump.done < min(upto_i, NT1):
            i = ph1_vec_pump.done
            pa = i % 2
            st_ = stat[:, pa]
            v.wait_ge(sem("s_st"), i + 1)
            v.tensor_scalar_mul(st_[:, 2:3], st_[:, 0:1], inv_n)
            v.tensor_scalar_mul(st_[:, 3:4], st_[:, 1:2], inv_n)
            v.scalar_tensor_tensor(st_[:, 6:7], st_[:, 2:3], -1.0, st_[:, 2:3],
                                   ALU.mult, ALU.mult)
            v.scalar_tensor_tensor(st_[:, 6:7], st_[:, 3:4], 1.0, st_[:, 6:7],
                                   ALU.mult, ALU.add)
            v.signal_next("v_t").tensor_scalar_add(st_[:, 6:7], st_[:, 6:7], EPS)
            v.wait_ge(sem("s_sq"), i + 1)
            v.reciprocal(st_[:, 4:5], st_[:, 7:8])
            for _ in range(2):
                v.scalar_tensor_tensor(st_[:, 7:8], st_[:, 4:5], 1.0, st_[:, 4:5],
                                       ALU.mult, ALU.mult)
                v.scalar_tensor_tensor(st_[:, 7:8], st_[:, 7:8], -0.5, st_[:, 6:7],
                                       ALU.mult, ALU.mult)
                v.tensor_scalar_add(st_[:, 7:8], st_[:, 7:8], 1.5)
                v.scalar_tensor_tensor(st_[:, 4:5], st_[:, 7:8], 1.0, st_[:, 4:5],
                                       ALU.mult, ALU.mult)
            v.signal_next("v_st").scalar_tensor_tensor(
                st_[:, 5:6], st_[:, 2:3], -1.0, st_[:, 4:5],
                ALU.mult, ALU.mult)
            ph1_vec_pump.done = i + 1
    ph1_vec_pump.done = 0

    def ph1_gp_pump(g, upto_xn):
        while ph1_gp_pump.done < min(upto_xn, NT1):
            i = ph1_gp_pump.done
            g.wait_ge(sem("s_af"), i + 1)
            rec1["xn", i] = R_GS.issue(g, xn_h[i * 128:(i + 1) * 128, :],
                                       xnbuf[:, i % 2, :])
            ph1_gp_pump.done = i + 1
    ph1_gp_pump.done = 0


    ph2_st = contextlib.ExitStack()
    wbuf = sb("wbuf", [128, NEC, E], BF16, ph2_st)
    xnt = sb("xnt", [128, 2, NEC, 512], BF16, ph2_st)
    t1b = sb("t1b", [128, E], F32, ph2_st)
    t2b = sb("t2b", [128, E], F32, ph2_st)
    nqt = sb("nqt", [128, H], F32, ph2_st)
    nqacc_q = sb("nqacc_q", [128, H], F32, ph2_st)
    nqacc_k = sb("nqacc_k", [128, H], F32, ph2_st)
    vtile = sb("vtile", [128, 2, E], BF16, ph2_st)
    cos_tm = sb("cos_tm", [128, 32, D], BF16, ph2_st)
    nsin_tm = sb("nsin_tm", [128, 32, D], BF16, ph2_st)
    bq_r = sb("bq_r", [1, E], BF16, ph2_st)
    bk_r = sb("bk_r", [1, E], BF16, ph2_st)
    bv_r = sb("bv_r", [1, E], BF16, ph2_st)
    crow_s = sb("crow_s", [1, H], F32, ph2_st)
    nrmt_s = sb("nrmt_s", [16, 2, 128], F32, ph2_st)
    mx_s = sb("mx_s", [16, 4], F32, ph2_st)

    vps = ps("vps", [128, 2, E], ph2_st)

    def hview(ap2d):
        return ap2d.rearrange("p (h d) -> p h d", d=D)

    def hbc(ap1, off, num):
        return bass.AP(ap1.tensor, ap1.offset + off, [ap1.ap[0], [0, H], [1, num]])

    def emit_wload(g, ring, w_hbm, rec, dstbuf=None):
        dstbuf = wbuf if dstbuf is None else dstbuf
        rec["wl"] = []
        for c4 in range(4):
            srcw = w_hbm[c4 * 512:(c4 + 1) * 512, :].rearrange(
                "(c p) f -> p c f", p=128)
            rec["wl"].append(ring.issue(g, dstbuf[:, c4 * 4:(c4 + 1) * 4, :], srcw))

    def proj_tm(tag, w_hbm, b_row, nslabs, kind, run):
        rec = {}
        NTC = nslabs * 4
        dst_tm = {"q": q_tm, "k": k_tm, "v": None}[kind]
        nqacc = {"q": nqacc_q, "k": nqacc_k, "v": None}[kind]

        with phase(f"ph2{tag}", run=run) as engs:
            if engs is None:
                return

            @engs.gpsimd
            def _(g):
                if kind == "q":
                    rec["cos"] = R_GL.issue(g, cos_tm[:],
                                            cos_t.rearrange("(c p) d -> p c d", p=128))
                    rec["nsin"] = R_GL.issue(g, nsin_tm[:],
                                             nsin_t.rearrange("(c p) d -> p c d", p=128))
                    bidx = [R_GL.issue(g, dst[:], srcb[None, :])
                            for dst, srcb in ((bq_r, bq), (bk_r, bkk), (bv_r, bv))]
                    for i in bidx:
                        R_GL.wait(g, i)
                    g.nop().then_inc(sem("bi_done"), 1)
                emit_wload(g, R_GL, w_hbm, rec)
                for tcix in range(NTC):
                    if kind == "q":
                        ph1_gp_pump(g, 2 * tcix + 4)
                    g.wait_ge(sem(f"v_ev{tag}"), tcix + 1)
                    rows = slice(tcix * 128, (tcix + 1) * 128)
                    if kind == "v":
                        for hh in range(H):
                            rec["st", tcix, hh] = R_GS.issue(
                                g, v_h[hh, rows, :],
                                vtile[:, tcix % 2, hh * D:(hh + 1) * D])
                    else:
                        rec["st", tcix] = R_GS.issue(g, dst_tm[rows, :],
                                                     vtile[:, tcix % 2, :])
                if kind == "q":
                    ph1_gp_pump(g, NT1)
                R_GS.wait_all(g)

            @engs.sync
            def _(sy):
                if kind == "q":
                    ph1_sync_chunk(sy, 0, 6)
                if kind == "k":
                    for i in range(NT1 - 8, NT1):
                        R_GS.wait(sy, rec1["xn", i])
                for j in range(nslabs):
                    if j >= 2:
                        sy.wait_ge(sem(f"t_sl{tag}"), j - 1)
                    if kind == "q":
                        for i in range(4 * j, 4 * j + 4):
                            R_GS.wait(sy, rec1["xn", i])
                    for ec in range(NEC):
                        rec["xt", j, ec] = R_SY.issue(
                            sy, xnt[:, j % 2, ec, :],
                            xn_h[j * 512:(j + 1) * 512, ec * 128:(ec + 1) * 128],
                            transpose=True)
                    if kind == "q":
                        ph1_sync_chunk(sy, 6 + 6 * j, 6 + 6 * (j + 1))
                if kind == "q":
                    ph1_sync_chunk(sy, 6 + 6 * nslabs, NT1)

            @engs.tensor
            def _(t):
                for i in rec["wl"]:
                    R_GL.wait(t, i)
                if kind == "q":
                    t.wait_ge(sem("bi_done"), 1)
                for j in range(nslabs):
                    for ec in range(NEC):
                        R_SY.wait(t, rec["xt", j, ec])
                    for tc4 in range(4):
                        tcix = j * 4 + tc4
                        if tcix >= 2:
                            t.wait_ge(sem(f"v_ev{tag}"), tcix - 1)
                        pt = vps[:, tcix % 2, :]
                        tsl = slice(tc4 * 128, (tc4 + 1) * 128)
                        for fs in range(4):
                            po = pt[:, fs * 512:(fs + 1) * 512]
                            for ec in range(NEC):
                                t.matmul(po, xnt[:, j % 2, ec, tsl],
                                         wbuf[:, ec, fs * 512:(fs + 1) * 512],
                                         start=(ec == 0), stop=False)
                            mm = t.matmul(po, ones_b[0:1, 0:128],
                                          b_row[0:1, fs * 512:(fs + 1) * 512],
                                          start=False, stop=True)
                        mm.then_inc(sem(f"t_mm{tag}"), 1)
                    t.drain().then_inc(sem(f"t_sl{tag}"), 1)

            @engs.scalar
            def _(sc):
                if kind == "q":
                    ph1_scalar(sc)

            @engs.vector
            def _(v):
                if kind != "v":
                    v.memset(nqacc[:], 0.0)
                    R_GL.wait(v, rec["cos"] if "cos" in rec else 0)
                    R_GL.wait(v, rec["nsin"] if "nsin" in rec else 0)
                for tcix in range(NTC):
                    pa = tcix % 2
                    if kind == "q":
                        ph1_vec_pump(v, 2 * tcix + 6)
                    v.wait_ge(sem(f"t_mm{tag}"), tcix + 1)
                    if tcix >= 2:
                        if kind == "v":
                            for hh in range(H):
                                R_GS.wait(v, rec["st", tcix - 2, hh])
                        else:
                            R_GS.wait(v, rec["st", tcix - 2])
                    pt = vps[:, pa, :]
                    if kind == "v":
                        for fs in range(4):
                            e2 = v.signal_next(f"v_ev{tag}") if fs == 3 else v
                            e2.tensor_copy(vtile[:, pa, fs * 512:(fs + 1) * 512],
                                           pt[:, fs * 512:(fs + 1) * 512])
                    else:
                        cosc = cos_tm[:, tcix, :]
                        nsinc = nsin_tm[:, tcix, :]
                        pt3 = hview(pt)
                        v.tensor_tensor(hview(t1b[:]), pt3, hbc(cosc, 0, D),
                                        ALU.mult)
                        ob = hview(t2b[:])
                        oa = bass.AP(ob.tensor, ob.offset, [ob.ap[0], [D, H], [1, 64]])
                        ia = bass.AP(pt3.tensor, pt3.offset + 64,
                                     [pt3.ap[0], [D, H], [1, 64]])
                        v.tensor_tensor(oa, ia, hbc(nsinc, 0, 64), ALU.mult)
                        ob2 = bass.AP(ob.tensor, ob.offset + 64,
                                      [ob.ap[0], [D, H], [1, 64]])
                        ib2 = bass.AP(pt3.tensor, pt3.offset,
                                      [pt3.ap[0], [D, H], [1, 64]])
                        v.tensor_tensor(ob2, ib2, hbc(nsinc, 64, 64), ALU.mult)
                        ro = vtile[:, pa, :]
                        v.tensor_add(ro, t1b[:], t2b[:])
                        v.tensor_mul(t1b[:], ro, ro)
                        v.tensor_reduce(nqt[:], hview(t1b[:]), AX.X, ALU.add)
                        v.signal_next(f"v_ev{tag}").tensor_max(
                            nqacc[:], nqacc[:], nqt[:])
                if kind == "q":
                    ph1_vec_pump(v, NT1)

    proj_tm("q", wqt, bq_r, TOWN // 512, "q", run=(upto >= 2))
    proj_tm("k", wkt, bk_r, S // 512, "k", run=(upto >= 2))
    proj_tm("v", wvt, bv_r, S // 512, "v", run=(upto >= 2))

    with phase("ph2d1", run=(upto >= 2)) as engs:
        if engs is not None:
            @engs.gpsimd
            def _(g):
                i1 = R_GS.issue(g, nrm_h[0], nqacc_q[:])
                i2 = R_GS.issue(g, nrm_h[1], nqacc_k[:])
                R_GS.wait(g, i1)
                R_GS.wait(g, i2)

    rec2d = {}
    with phase("ph2d2", run=(upto >= 2)) as engs:
        if engs is not None:
            @engs.sync
            def _(sy):
                for i in range(2):
                    rec2d["nl", i] = R_SY.issue(
                        sy, nrmt_s[:, i, :], nrm_h[i].rearrange("t h -> h t"),
                        allow_slow_non_contiguous=True)

            @engs.vector
            def _(v):
                R_SY.wait(v, rec2d["nl", 0])
                R_SY.wait(v, rec2d["nl", 1])
                v.tensor_reduce(mx_s[:, 0:1], nrmt_s[:, 0, :], AX.X, ALU.max)
                v.tensor_reduce(mx_s[:, 1:2], nrmt_s[:, 1, :], AX.X, ALU.max)
                v.signal_next("cd1").tensor_mul(mx_s[:, 2:3], mx_s[:, 0:1],
                                                 mx_s[:, 1:2])
                v.wait_ge(sem("cd2"), 1)
                v.signal_next("cd3").tensor_scalar_mul(mx_s[:, 3:4], mx_s[:, 3:4],
                                                        -INVSQ)

            @engs.scalar
            def _(sc):
                sc.wait_ge(sem("cd1"), 1)
                sc.signal_next("cd2").activation(mx_s[:, 3:4], mx_s[:, 2:3],
                                                  AF.Sqrt)

            @engs.gpsimd
            def _(g):
                g.wait_ge(sem("cd3"), 1)
                R_GS.wait(g, R_GS.issue(g, crow_h, mx_s[:, 3:4]))

    with phase("ph2d3", run=(upto >= 2)) as engs:
        if engs is not None:
            @engs.sync
            def _(sy):
                rec2d["cl"] = R_SY.issue(sy, crow_s[:],
                                         crow_h.rearrange("h one -> one h"),
                                         allow_slow_non_contiguous=True)

            @engs.tensor
            def _(t):
                R_SY.wait(t, rec2d["cl"])
                t.matmul(vps[:, 0, 0:H], ones_r[0:1, 0:128], crow_s[:],
                         start=True, stop=True).then_inc(sem("cd4"), 1)

            @engs.vector
            def _(v):
                v.wait_ge(sem("cd4"), 1)
                v.tensor_copy(cneg_s[:], vps[:, 0, 0:H])

    ph2_st.close()
    ph1_st.close()

    qw_chunks = []
    for r in range(F1 // 128):
        for c in range(2):
            qw_chunks.append((w1[r * 128:(r + 1) * 128, c * 1024:(c + 1) * 1024],
                              w1q_h[r * 128:(r + 1) * 128, c * 1024:(c + 1) * 1024]))
    for r in range(E // 128):
        for c in range(8):
            qw_chunks.append((w2[r * 128:(r + 1) * 128, c * 1024:(c + 1) * 1024],
                              w2q_h[r * 128:(r + 1) * 128, c * 1024:(c + 1) * 1024]))
    NQWC = len(qw_chunks)

    ph34_st = contextlib.ExitStack()
    av_all = sb("av_all", [128, H, TOWN], BF16, ph34_st)
    wo_s = sb("wo_s", [128, NEC, E], BF16, ph34_st)
    rec34 = {}

    ph3_st = contextlib.ExitStack()
    qw_in = sb("qw_in", [128, 4, 1024], F32, ph3_st)
    qxdv = sb("qxdv", [128, 1024], F32, ph3_st)
    qout = sb("qout", [128, 2, 1024], BF16, ph3_st)
    q3am = sb("q3am", [128, 32], F32, ph3_st)
    q3sc = sb("q3sc", [128, 32], F32, ph3_st)
    q3iv = sb("q3iv", [128, 32], F32, ph3_st)
    kth = sb("kth", [128, 2, S], BF16, ph3_st)
    qth = sb("qth", [128, 2, TOWN], BF16, ph3_st)
    vh3 = sb("vh3", [128, 2, 32, D], BF16, ph3_st)
    p_t = sb("p_t", [128, 4, 512], BF16, ph3_st)
    onec = sb("onec", [128, 1], BF16, ph3_st)
    rsum_s = sb("rsum_s", [1, 512], F32, ph3_st)
    rsb = sb("rsb", [128, 512], F32, ph3_st)
    sps = ps("sps", [128, 3, 512], ph3_st)
    avp = ps("avp", [128, 2, 512], ph3_st)
    ssum = ps("ssum", [128, 2, 512], ph3_st)
    rps = ps("rps", [128, 512], ph3_st)

    NKC = S // 128
    NKP = NKC // 2
    NQB = TOWN // 512

    with phase("ph3", run=(upto >= 3)) as engs:
        if engs is not None:
            @engs.gpsimd
            def _(g):
                g.signal_next("g_one").memset(onec[:], 1.0)
                emit_wload(g, R_GL, wot, rec34, dstbuf=wo_s)
                for i in range(NQWC):
                    g.wait_ge(sem("v_qw"), i + 1)
                    rec34["qwo", i] = R_GS.issue(g, qw_chunks[i][1],
                                                 qout[:, i % 2, :])
                R_GS.wait_all(g)

            @engs.sync
            def _(sy):
                for hh in range(H):
                    if hh >= 2:
                        sy.wait_ge(sem("t_h"), hh - 1)
                    pa = hh % 2
                    ld = [R_SY.issue(sy, kth[:, pa, :],
                                     k_tm[:, hh * 128:(hh + 1) * 128],
                                     transpose=True),
                          R_SY.issue(sy, qth[:, pa, :],
                                     q_tm[:, hh * 128:(hh + 1) * 128],
                                     transpose=True)]
                    for c8 in range(4):
                        ld.append(R_SY.issue(
                            sy, vh3[:, pa, c8 * 8:(c8 + 1) * 8, :],
                            v_h[hh, c8 * 1024:(c8 + 1) * 1024, :]
                            .rearrange("(c p) d -> p c d", p=128)))
                    rec34["ld", hh] = ld
                    for c4 in range(16):
                        i = hh * 16 + c4
                        if i < NQWC:
                            if i >= 4:
                                sy.wait_ge(sem("v_qw"), i - 3)
                            rec34["qwi", i] = R_SY.issue(sy, qw_in[:, i % 4, :],
                                                         qw_chunks[i][0])

            def av_one(t, hh, qb, kc, sums_first=False):
                gqi = hh * NQB + qb
                kcg = gqi * NKC + kc
                pslice = p_t[:, kcg % 4, :]
                if sums_first:
                    t.matmul(ssum[0:1, gqi % 2, :], onec[:], pslice,
                             start=(kc == 0), stop=(kc == NKC - 1))
                    t.matmul(avp[:, gqi % 2, :], vh3[:, hh % 2, kc, :], pslice,
                             start=(kc == 0), stop=(kc == NKC - 1)) \
                        .then_inc(sem("t_av"), 1)
                else:
                    t.matmul(avp[:, gqi % 2, :], vh3[:, hh % 2, kc, :], pslice,
                             start=(kc == 0), stop=(kc == NKC - 1))
                    t.matmul(ssum[0:1, gqi % 2, :], onec[:], pslice,
                             start=(kc == 0), stop=(kc == NKC - 1)) \
                        .then_inc(sem("t_av"), 1)

            @engs.tensor
            def _(t):
                t.wait_ge(sem("g_one"), 1)
                NGQ = H * NQB

                def prev(gqi):
                    return (gqi - 1) // NQB, (gqi - 1) % NQB

                for hh in range(H):
                    for i in rec34["ld", hh]:
                        R_SY.wait(t, i)
                    for qb in range(NQB):
                        gqi = hh * NQB + qb
                        for kc in range(NKC):
                            kcg = gqi * NKC + kc
                            if kcg >= 3:
                                t.wait_ge(sem("s_p"), kcg - 2)
                            t.matmul(sps[:, kcg % 3, :],
                                     kth[:, hh % 2, kc * 128:(kc + 1) * 128],
                                     qth[:, hh % 2, qb * 512:(qb + 1) * 512],
                                     start=True, stop=True).then_inc(sem("t_s"), 1)
                            if kc == 0:
                                if gqi >= 1:
                                    ph, pq = prev(gqi)
                                    t.wait_ge(sem("s_p"), gqi * NKC - 1)
                                    av_one(t, ph, pq, NKC - 2)
                            elif kc == 1:
                                if gqi >= 1:
                                    ph, pq = prev(gqi)
                                    t.wait_ge(sem("s_p"), gqi * NKC)
                                    av_one(t, ph, pq, NKC - 1)
                                    if qb == 0:
                                        t.drain().then_inc(sem("t_h"), 1)
                                    t.wait_ge(sem("v_rcp"), gqi)
                                    t.matmul(rps[:], ones_r[0:1, 0:128], rsum_s[:],
                                             start=True, stop=True) \
                                        .then_inc(sem("t_r"), 1)
                            else:
                                if kc == 2 and gqi >= 1:
                                    t.wait_ge(sem("v_av"), gqi - 1)
                                t.wait_ge(sem("s_p"), kcg - 1)
                                av_one(t, hh, qb, kc - 2)
                t.wait_ge(sem("s_p"), NGQ * NKC - 1)
                av_one(t, H - 1, NQB - 1, NKC - 2)
                t.wait_ge(sem("s_p"), NGQ * NKC)
                av_one(t, H - 1, NQB - 1, NKC - 1)
                t.wait_ge(sem("v_rcp"), NGQ)
                t.matmul(rps[:], ones_r[0:1, 0:128], rsum_s[:],
                         start=True, stop=True).then_inc(sem("t_r"), 1)

            @engs.scalar
            def _(sc):
                for gqi in range(H * NQB):
                    hh = gqi // NQB
                    for kc in range(NKC):
                        kcg = gqi * NKC + kc
                        sc.wait_ge(sem("t_s"), kcg + 1)
                        if kcg >= 4:
                            sc.wait_ge(sem("t_av"), kcg - 3)
                        sc.signal_next("s_p").activation(
                            p_t[:, kcg % 4, :], sps[:, kcg % 3, :],
                            AF.Exp, bias=cneg_s[:, hh:hh + 1], scale=INVSQ)

            @engs.vector
            def _(v):
                for gqi in range(H * NQB):
                    hh, qb = gqi // NQB, gqi % NQB
                    v.wait_ge(sem("t_av"), (gqi + 1) * NKC)
                    v.signal_next("v_rcp").tensor_copy(rsum_s[:],
                                                       ssum[0:1, gqi % 2, :])
                    v.wait_ge(sem("t_r"), gqi + 1)
                    v.reciprocal(rsb[:], rps[:])
                    v.signal_next("v_av").tensor_tensor(
                        av_all[:, hh, qb * 512:(qb + 1) * 512],
                        avp[:, gqi % 2, :], rsb[:], ALU.mult)
                    for c4 in range(4):
                        i = gqi * 4 + c4
                        if i < NQWC:
                            R_SY.wait(v, rec34["qwi", i])
                            if i >= 2:
                                R_GS.wait(v, rec34["qwo", i - 2])
                            emit_quant2(v, qw_in[:, i % 4, :], qout[:, i % 2, :],
                                        32, q3am, q3sc, q3iv, qxdv, None,
                                        sig="v_qw")

    ph3_st.close()

    ph4_st = contextlib.ExitStack()
    aot = sb("aot", [128, 2, E], BF16, ph4_st)
    bo_r = sb("bo_r", [1, E], F32, ph4_st)
    aops = ps("aops", [128, 2, E], ph4_st)
    xft = sb("xft", [128, 2, E], BF16, ph4_st)
    jnk5 = sb("jnk5", [128, E], BF16, ph4_st)
    xlnp = sb("xlnp", [128, E], F32, ph4_st)
    xln5 = sb("xln5", [128, E], BF16, ph4_st)
    stat5 = sb("stat5", [128, 2, 8], F32, ph4_st)
    lnw_s = sb("lnw_s", [128, E], F32, ph4_st)
    lnb_s = sb("lnb_s", [128, E], F32, ph4_st)
    q5am = sb("q5am", [128, 64], F32, ph4_st)
    q5sc = sb("q5sc", [128, 64], F32, ph4_st)
    q5iv = sb("q5iv", [128, 64], F32, ph4_st)
    q5dv = sb("q5dv", [128, E], F32, ph4_st)
    xlq5 = sb("xlq5", [128, 2, E], BF16, ph4_st)

    rec5 = {}
    rec5b = {}
    NT5 = TOWN // 128

    with phase("ph4", run=(upto >= 4)) as engs:
        if engs is not None:
            @engs.gpsimd
            def _(g):
                def xq_pump(upto_i):
                    while xq_pump.done < min(upto_i, NT5):
                        i = xq_pump.done
                        g.wait_ge(sem("v_x5"), i + 1)
                        rec5["xq", i] = R_GS.issue(
                            g, xlnq_h[i * 128:(i + 1) * 128, :],
                            xlq5[:, i % 2, :])
                        xq_pump.done = i + 1
                xq_pump.done = 0
                for tcg in range(TOWN // 128):
                    g.wait_ge(sem("v_eo"), tcg + 1)
                    rec34["aostore", tcg] = R_GS.issue(
                        g, ao_h[tcg * 128:(tcg + 1) * 128, :], aot[:, tcg % 2, :])
                    xq_pump(tcg)
                xq_pump(NT5)
                R_GS.wait_all(g)

            @engs.sync
            def _(sy):
                rec34["bo"] = R_SY.issue(sy, bo_r[:], bo[None, :])
                rec5["lnw"] = R_SY.issue(sy, lnw_s[:], lnw_t)
                rec5["lnb"] = R_SY.issue(sy, lnb_s[:], lnb_t)
                for i in range(NT5):
                    if i >= 2:
                        sy.wait_ge(sem("v_x5"), i - 1)
                    rec5["xf", i] = R_SY.issue(sy, xft[:, i % 2, :],
                                               xn_h[i * 128:(i + 1) * 128, :])

            @engs.tensor
            def _(t):
                for i in rec34["wl"]:
                    R_GL.wait(t, i)
                R_SY.wait(t, rec34["bo"])
                for tcg in range(TOWN // 128):
                    if tcg >= 2:
                        t.wait_ge(sem("v_eo"), tcg - 1)
                    pt = aops[:, tcg % 2, :]
                    for os_ in range(4):
                        po = pt[:, os_ * 512:(os_ + 1) * 512]
                        for hh in range(H):
                            t.matmul(po, av_all[:, hh, tcg * 128:(tcg + 1) * 128],
                                     wo_s[:, hh, os_ * 512:(os_ + 1) * 512],
                                     start=(hh == 0), stop=False)
                        mm = t.matmul(po, ones_r[0:1, 0:128],
                                      bo_r[0:1, os_ * 512:(os_ + 1) * 512],
                                      start=False, stop=True)
                    mm.then_inc(sem("t_mo"), 1)

            @engs.scalar
            def _(sc):
                for i in range(NT5):
                    pa = i % 2
                    R_SY.wait(sc, rec5["xf", i])
                    xt = xft[:, pa, :]
                    sc.activation(jnk5[:], xt, AF.Copy,
                                  accum_out=stat5[:, pa, 0:1])
                    sc.signal_next("s_st5").activation(
                        jnk5[:], xt, AF.Square, accum_out=stat5[:, pa, 1:2])
                    sc.wait_ge(sem("v_t5"), i + 1)
                    sc.signal_next("s_sq5").activation(stat5[:, pa, 7:8],
                                                        stat5[:, pa, 6:7], AF.Sqrt)
                    sc.wait_ge(sem("v_st5"), i + 1)
                    sc.signal_next("s_af5").activation(
                        xlnp[:], xt, AF.Identity,
                        scale=stat5[:, pa, 4:5], bias=stat5[:, pa, 5:6])

            @engs.vector
            def _(v):
                R_SY.wait(v, rec5["lnw"])
                R_SY.wait(v, rec5["lnb"])
                inv_n = 1.0 / E

                def ln5_chunk(i):
                    pa = i % 2
                    st_ = stat5[:, pa]
                    v.wait_ge(sem("s_st5"), i + 1)
                    v.tensor_scalar_mul(st_[:, 2:3], st_[:, 0:1], inv_n)
                    v.tensor_scalar_mul(st_[:, 3:4], st_[:, 1:2], inv_n)
                    v.scalar_tensor_tensor(st_[:, 6:7], st_[:, 2:3], -1.0,
                                           st_[:, 2:3], ALU.mult, ALU.mult)
                    v.scalar_tensor_tensor(st_[:, 6:7], st_[:, 3:4], 1.0,
                                           st_[:, 6:7], ALU.mult, ALU.add)
                    v.signal_next("v_t5").tensor_scalar_add(st_[:, 6:7],
                                                             st_[:, 6:7], EPS)
                    v.wait_ge(sem("s_sq5"), i + 1)
                    v.reciprocal(st_[:, 4:5], st_[:, 7:8])
                    for _ in range(2):
                        v.scalar_tensor_tensor(st_[:, 7:8], st_[:, 4:5], 1.0,
                                               st_[:, 4:5], ALU.mult, ALU.mult)
                        v.scalar_tensor_tensor(st_[:, 7:8], st_[:, 7:8], -0.5,
                                               st_[:, 6:7], ALU.mult, ALU.mult)
                        v.tensor_scalar_add(st_[:, 7:8], st_[:, 7:8], 1.5)
                        v.scalar_tensor_tensor(st_[:, 4:5], st_[:, 7:8], 1.0,
                                               st_[:, 4:5], ALU.mult, ALU.mult)
                    v.signal_next("v_st5").scalar_tensor_tensor(
                        st_[:, 5:6], st_[:, 2:3], -1.0, st_[:, 4:5],
                        ALU.mult, ALU.mult)
                    v.wait_ge(sem("s_af5"), i + 1)
                    v.tensor_mul(xlnp[:], xlnp[:], lnw_s[:])
                    v.tensor_add(xln5[:], xlnp[:], lnb_s[:])
                    if i >= 2:
                        R_GS.wait(v, rec5["xq", i - 2])
                    emit_quant2(v, xln5[:], xlq5[:, pa, :], 64,
                                q5am, q5sc, q5iv, q5dv, None, sig="v_x5")

                for tcg in range(TOWN // 128):
                    v.wait_ge(sem("t_mo"), tcg + 1)
                    if tcg >= 2:
                        R_GS.wait(v, rec34["aostore", tcg - 2])
                    for fs in range(4):
                        e2 = v.signal_next("v_eo") if fs == 3 else v
                        e2.tensor_copy(aot[:, tcg % 2, fs * 512:(fs + 1) * 512],
                                       aops[:, tcg % 2, fs * 512:(fs + 1) * 512])
                    ln5_chunk(tcg)

    ph4_st.close()
    ph34_st.close()

    ph5b_st = contextlib.ExitStack()
    xlqt = sb("xlqt", [128, NEC, TOWN], BF16, ph5b_st)
    w1t = sb("w1t", [128, 2, NEC, 512], BF16, ph5b_st)
    g1b = sb("g1b", [128, 4, 512], BF16, ph5b_st)
    g2b = sb("g2b", [128, 4, 512], BF16, ph5b_st)
    gqb = sb("gqb", [128, 4, 512], BF16, ph5b_st)
    b1r = sb("b1r", [1, F1], F32, ph5b_st)
    wbt = sb("wbt", [128, 4, 512], BF16, ph5b_st)
    q6am = sb("q6am", [128, 4, 16], F32, ph5b_st)
    q6sc = sb("q6sc", [128, 4, 16], F32, ph5b_st)
    q6iv = sb("q6iv", [128, 4, 16], F32, ph5b_st)
    q6dv = sb("q6dv", [128, 4, 512], F32, ph5b_st)
    h1ps = ps("h1ps", [128, 4, 512], ph5b_st)

    wbl_ring = Ring("wbl", 4)
    NFS = F1 // 512
    NTC5 = TOWN // 128
    NG5 = NFS * NTC5
    NWB = (E * F1 * 2) // (512 * 128 * 2)

    with phase("ph5b", run=(upto >= 5)) as engs:
        if engs is not None:
            @engs.gpsimd
            def _(g):
                for i in range(NG5):
                    g.wait_ge(sem("v_gq"), i + 1)
                    fs, tcb = i // NTC5, i % NTC5
                    rows = slice(tcb * 128, (tcb + 1) * 128)
                    rec5b["gq", i] = R_GS.issue(
                        g, gq_h[rows, fs * 512:(fs + 1) * 512], gqb[:, i % 4, :])
                    wbl_ring.wait(g, i)
                    os_, fc = i // 64, i % 64
                    rec5b["wbo", i] = R_GS.issue(
                        g, w2qT_h[fc * 128:(fc + 1) * 128,
                                  os_ * 512:(os_ + 1) * 512], wbt[:, i % 4, :])
                R_GS.wait_all(g)

            @engs.sync
            def _(sy):
                rec5b["b1"] = R_SY.issue(sy, b1r[:], b1[None, :])

                def w1fs(fs):
                    if fs >= 2:
                        sy.wait_ge(sem("t_fs"), fs - 1)
                    for ec in range(NEC):
                        rec5b["w1", fs, ec] = R_SY.issue(
                            sy, w1t[:, fs % 2, ec, :],
                            w1q_h[fs * 512:(fs + 1) * 512,
                                  ec * 128:(ec + 1) * 128],
                            transpose=True)

                def wbtk(fs):
                    for k in range(16):
                        i = fs * 16 + k
                        os_, fc = i // 64, i % 64
                        if i >= 4:
                            R_GS.wait(sy, rec5b["wbo", i - 4])
                        wbl_ring.issue(sy, wbt[:, i % 4, :],
                                       w2q_h[os_ * 512:(os_ + 1) * 512,
                                             fc * 128:(fc + 1) * 128],
                                       transpose=True)

                def xlt(j):
                    for ec in range(NEC):
                        rec5b["xlt", ec, j] = R_SY.issue(
                            sy, xlqt[:, ec, j * 128:(j + 1) * 128],
                            xlnq_h[j * 128:(j + 1) * 128,
                                   ec * 128:(ec + 1) * 128],
                            transpose=True)

                w1fs(0)
                for j in range(4):
                    xlt(j)
                w1fs(1)
                for j in range(4, NT5):
                    xlt(j)
                wbtk(0)
                wbtk(1)
                for fs in range(2, NFS):
                    w1fs(fs)
                    wbtk(fs)


            @engs.tensor
            def _(t):
                R_SY.wait(t, rec5b["b1"])
                for fs in range(NFS):
                    for ec in range(NEC):
                        R_SY.wait(t, rec5b["w1", fs, ec])
                    for tcb in range(NTC5):
                        gidx = fs * NTC5 + tcb
                        if gidx >= 4:
                            t.wait_ge(sem("s_e1"), gidx - 3)
                        po = h1ps[:, gidx % 4, :]
                        for ec in range(NEC):
                            if fs == 0:
                                R_SY.wait(t, rec5b["xlt", ec, tcb])
                            t.matmul(po, xlqt[:, ec, tcb * 128:(tcb + 1) * 128],
                                     w1t[:, fs % 2, ec, :],
                                     start=(ec == 0), stop=False)
                        mm1 = t.matmul(po, ones_r[0:1, 0:128],
                                       b1r[0:1, fs * 512:(fs + 1) * 512],
                                       start=False, stop=True)
                        mm1.then_inc(sem("t_m1"), 1)
                    t.drain().then_inc(sem("t_fs"), 1)

            @engs.scalar
            def _(sc):
                for gidx in range(NG5):
                    pa = gidx % 4
                    sc.wait_ge(sem("t_m1"), gidx + 1)
                    sc.signal_next("s_e1").activation(g1b[:, pa, :],
                                                      h1ps[:, pa, :], AF.Copy)
                    if gidx >= 4:
                        sc.wait_ge(sem("v_gq"), gidx - 3)
                    sc.signal_next("s_g1").activation(g2b[:, pa, :], g1b[:, pa, :],
                                                       AF.Gelu_apprx_tanh)

            @engs.vector
            def _(v):
                for gidx in range(NG5):
                    pa = gidx % 4
                    v.wait_ge(sem("s_g1"), gidx + 1)
                    if gidx >= 4:
                        R_GS.wait(v, rec5b["gq", gidx - 4])
                    emit_quant2(v, g2b[:, pa, :],
                                gqb[:, pa, :], 16,
                                q6am[:, pa, :], q6sc[:, pa, :],
                                q6iv[:, pa, :], q6dv[:, pa, :], None, sig="v_gq")

    ph5b_st.close()

    ph5c_st = contextlib.ExitStack()
    gqt = sb("gqt", [128, 64, 512], BF16, ph5c_st)
    rec5c = {}
    w2t = sb("w2t", [128, 2, 32, 512], BF16, ph5c_st)
    h2s = sb("h2s", [128, 4, E], BF16, ph5c_st)
    xf6 = sb("xf6", [128, 2, E], BF16, ph5c_st)
    xr6 = sb("xr6", [128, 2, E], F32, ph5c_st)
    ao6 = sb("ao6", [128, 2, E], BF16, ph5c_st)
    ob6 = sb("ob6", [128, 2, E], F32, ph5c_st)
    fo6 = sb("fo6", [128, E], BF16, ph5c_st)
    b2_r = sb("b2_r", [1, E], F32, ph5c_st)
    h2ps = ps("h2ps", [128, 4, 512], ph5c_st)

    with phase("ph5c", run=(upto >= 5)) as engs:
        if engs is not None:
            @engs.gpsimd
            def _(g):
                for gt in range(16):
                    g.wait_ge(sem("v_fin"), gt + 1)
                    rec5c["out", gt] = R_GS.issue(
                        g, out[gt * 128:(gt + 1) * 128, :], ob6[:, gt % 2, :])
                R_GS.wait_all(g)

            @engs.sync
            def _(sy):
                rec5c["b2"] = R_SY.issue(sy, b2_r[:], b2[None, :])

                def w2half(os_, hf):
                    gos_ = w2half.tsb * 4 + os_
                    if gos_ >= 1:
                        sy.wait_ge(sem("t_hf"), (gos_ - 1) * 2 + hf + 1)
                    for fc in range(32):
                        rec5c["w2", gos_, hf, fc] = R_SY.issue(
                            sy, w2t[:, hf, fc, :],
                            w2qT_h[(hf * 32 + fc) * 128:
                                   (hf * 32 + fc + 1) * 128,
                                   os_ * 512:(os_ + 1) * 512])

                for tsb in range(4):
                    w2half.tsb = tsb
                    w2half(0, 0)
                    if tsb >= 1:
                        sy.wait_ge(sem("t_tsb"), tsb)
                    for fc in range(64):
                        rec5c["gqt", tsb, fc] = R_SY.issue(
                            sy, gqt[:, fc, :],
                            gq_h[tsb * 512:(tsb + 1) * 512,
                                 fc * 128:(fc + 1) * 128],
                            transpose=True)
                    for os_ in range(1, 4):
                        w2half(os_, 0)

            @engs.scalar
            def _(sc):
                for tsb in range(4):
                    for os_ in range(4):
                        gos = tsb * 4 + os_
                        hf = 1
                        if gos >= 1:
                            sc.wait_ge(sem("t_hf"), (gos - 1) * 2 + hf + 1)
                        for fc in range(32):
                            rec5c["w2", gos, hf, fc] = R_SC.issue(
                                sc, w2t[:, hf, fc, :],
                                w2qT_h[(hf * 32 + fc) * 128:
                                       (hf * 32 + fc + 1) * 128,
                                       os_ * 512:(os_ + 1) * 512])
                    for tt_ in range(4):
                        gt = tsb * 4 + tt_
                        if gt >= 2:
                            sc.wait_ge(sem("v_fin"), gt - 1)
                        rows = slice(gt * 128, (gt + 1) * 128)
                        rec5c["xr", gt] = R_SC.issue(sc, xr6[:, gt % 2, :], xs[rows, :])
                        rec5c["aor", gt] = R_SC.issue(sc, ao6[:, gt % 2, :], ao_h[rows, :])
                        rec5c["xfr", gt] = R_SC.issue(sc, xf6[:, gt % 2, :], xn_h[rows, :])

            @engs.tensor
            def _(t):
                R_SY.wait(t, rec5c["b2"])
                for tsb in range(4):
                    for os_ in range(4):
                        gos = tsb * 4 + os_
                        for hf in range(2):
                            w2r = R_SY if hf == 0 else R_SC
                            for ii in range(32):
                                w2r.wait(t, rec5c["w2", gos, hf, ii])
                            if hf == 0 and gos >= 1:
                                for tt_ in range(4):
                                    t.wait_ge(sem("v_e2"),
                                              (gos - 1) * 4 + tt_ + 1)
                            for fc in range(32):
                                if os_ == 0:
                                    R_SY.wait(t, rec5c["gqt", tsb,
                                                       hf * 32 + fc])
                                for tt_ in range(4):
                                    t.matmul(h2ps[:, tt_, :],
                                             gqt[:, hf * 32 + fc,
                                                 tt_ * 128:(tt_ + 1) * 128],
                                             w2t[:, hf, fc, :],
                                             start=(hf == 0 and fc == 0),
                                             stop=False)
                            t.drain().then_inc(sem("t_hf"), 1)
                        for tt_ in range(4):
                            t.matmul(h2ps[:, tt_, :], ones_r[0:1, 0:128],
                                     b2_r[0:1, os_ * 512:(os_ + 1) * 512],
                                     start=False, stop=True) \
                                .then_inc(sem("t_m2"), 1)
                    t.drain().then_inc(sem("t_tsb"), 1)

            @engs.vector
            def _(v):
                for tsb in range(4):
                    for os_ in range(4):
                        for tt_ in range(4):
                            gidx = (tsb * 4 + os_) * 4 + tt_
                            v.wait_ge(sem("t_m2"), gidx + 1)
                            v.signal_next("v_e2").tensor_copy(
                                h2s[:, tt_, os_ * 512:(os_ + 1) * 512],
                                h2ps[:, tt_, :])
                    for tt_ in range(4):
                        gt = tsb * 4 + tt_
                        R_SC.wait(v, rec5c["xr", gt])
                        R_SC.wait(v, rec5c["aor", gt])
                        R_SC.wait(v, rec5c["xfr", gt])
                        if gt >= 2:
                            R_GS.wait(v, rec5c["out", gt - 2])
                        v.tensor_add(fo6[:], xf6[:, gt % 2, :], h2s[:, tt_, :])
                        v.scalar_tensor_tensor(ob6[:, gt % 2, :], ao6[:, gt % 2, :],
                                               0.0, xr6[:, gt % 2, :],
                                               ALU.bypass, ALU.add)
                        v.signal_next("v_fin").tensor_add(ob6[:, gt % 2, :],
                                                           ob6[:, gt % 2, :], fo6[:])


    ph5c_st.close()

    stack.close()
    return nc




_ROPE_BASE = 10000.0


def _rope_tables():
    inv_freq = 1.0 / _ROPE_BASE ** (np.arange(0, D, 2, dtype=np.float32) / D)
    pos = np.arange(S, dtype=np.float32)
    freqs = np.outer(pos, inv_freq)
    emb = np.concatenate([freqs, freqs], axis=-1)
    cos = np.cos(emb).astype(np.float32)
    sin = np.sin(emb).astype(np.float32)
    nsin = np.concatenate([-sin[:, :64], sin[:, 64:]], axis=1)
    return cos, nsin


def _stage(inputs, core, shared, cos, nsin):
    b, h = core // 2, core % 2
    x = inputs["L0_attn_x"]
    perm = np.concatenate([np.arange(h * TOWN, (h + 1) * TOWN),
                           np.arange((1 - h) * TOWN, (2 - h) * TOWN)])
    d = dict(shared)
    d["xs"] = np.ascontiguousarray(x[b][perm])
    d["cos_t"] = np.ascontiguousarray(cos[perm])
    d["nsin_t"] = np.ascontiguousarray(nsin[perm])
    return d


def kernel(**inputs):
    from concourse.bass_utils import run_bass_kernel_spmd
    inputs = {k: np.ascontiguousarray(np.asarray(v, dtype=np.float32))
              for k, v in inputs.items()}
    shared = {
        "wqt": np.ascontiguousarray(inputs["L0_attn_Wq"].T),
        "wkt": np.ascontiguousarray(inputs["L0_attn_Wk"].T),
        "wvt": np.ascontiguousarray(inputs["L0_attn_Wv"].T),
        "wot": np.ascontiguousarray(inputs["L0_attn_Wo"].T),
        "w1": inputs["L0_ffn_W1"],
        "w2": inputs["L0_ffn_W2"],
        "bq": inputs["L0_attn_bq"],
        "bk": inputs["L0_attn_bk"],
        "bv": inputs["L0_attn_bv"],
        "bo": inputs["L0_attn_bo"],
        "b1": inputs["L0_ffn_b1"],
        "b2": inputs["L0_ffn_b2"],
        "b1_t": np.ascontiguousarray(
            np.broadcast_to(inputs["L0_ffn_b1"], (128, F1))),
        "lnw_t": np.ascontiguousarray(
            np.broadcast_to(inputs["L0_ffn_ln_w"], (128, E))),
        "lnb_t": np.ascontiguousarray(
            np.broadcast_to(inputs["L0_ffn_ln_b"], (128, E))),
    }
    cos, nsin = _rope_tables()
    in_maps = [_stage(inputs, c, shared, cos, nsin) for c in range(8)]
    nc = build()
    res = run_bass_kernel_spmd(nc, in_maps, core_ids=list(range(8)))
    outp = np.empty((B, S, E), dtype=np.float32)
    for c in range(8):
        b, h = c // 2, c % 2
        outp[b, h * TOWN:(h + 1) * TOWN, :] = res.results[c]["out"]
    return outp


# version 20 (final)
# speedup vs baseline: 1.1564x; 1.0133x over previous
import math
import contextlib
import numpy as np
import concourse.bass as bass
import concourse.mybir as mybir

F32 = mybir.dt.float32
BF16 = mybir.dt.bfloat16
F16 = mybir.dt.float16
I32 = mybir.dt.int32
AF = mybir.ActivationFunctionType
ALU = mybir.AluOpType
AX = mybir.AxisListType

B, S, E, H, D = 4, 4096, 2048, 16, 128
F1 = 8192
TOWN = 2048
NEC = E // 128
INVSQ = 1.0 / math.sqrt(D)
EPS = 1e-5

QCLAMP = 2.0 ** -40
EXPMASK = 0x7F800000
SIXSHIFT = 6 << 23
INVBASE = 254 << 23
MAGIC = 12582912.0
PRECLIP = 127.4

NENG = 5


def bc(ap2d, nblk, bk):
    return bass.AP(ap2d.tensor, ap2d.offset, [ap2d.ap[0], [ap2d.ap[1][0], nblk], [0, bk]])


def r3(ap2d, bk):
    return ap2d.rearrange("p (b k) -> p b k", k=bk)


def build(upto=99, dbg=()):
    nc = bass.Bass("TRN2", target_bir_lowering=False, debug=False)
    stack = contextlib.ExitStack()
    sems = {}

    def sem(name):
        if name not in sems:
            sems[name] = stack.enter_context(nc.semaphore(name))
        return sems[name]

    class Ring:

        def __init__(self, name, n):
            self.name, self.n = name, n
            self.issued = 0

        def issue(self, eng, out_ap, in_ap, **kw):
            i = self.issued
            slot, rnd = i % self.n, i // self.n
            s = sem(f"{self.name}{slot}")
            if rnd > 0:
                eng.wait_ge(s, 16 * rnd)
            eng.dma_start(out_ap, in_ap, **kw).then_inc(s, 16)
            self.issued += 1
            return i

        def wait(self, eng, i):
            eng.wait_ge(sem(f"{self.name}{i % self.n}"), 16 * (i // self.n + 1))

        def wait_all(self, eng):
            for slot in range(min(self.n, self.issued)):
                cnt = (self.issued - 1 - slot) // self.n + 1
                eng.wait_ge(sem(f"{self.name}{slot}"), 16 * cnt)

    R_SY = Ring("rsy", 16)
    R_GS = Ring("rgs", 8)
    R_GL = Ring("rgl", 4)
    R_SC = Ring("rsc", 6)

    def dram(name, shape, dtype, kind):
        return nc.dram_tensor(name, shape, dtype, kind=kind).ap()

    xs = dram("xs", [S, E], F32, "ExternalInput")
    wqt = dram("wqt", [E, E], F32, "ExternalInput")
    wkt = dram("wkt", [E, E], F32, "ExternalInput")
    wvt = dram("wvt", [E, E], F32, "ExternalInput")
    wot = dram("wot", [E, E], F32, "ExternalInput")
    w1 = dram("w1", [F1, E], F32, "ExternalInput")
    w2 = dram("w2", [E, F1], F32, "ExternalInput")
    bq = dram("bq", [E], F32, "ExternalInput")
    bkk = dram("bk", [E], F32, "ExternalInput")
    bv = dram("bv", [E], F32, "ExternalInput")
    bo = dram("bo", [E], F32, "ExternalInput")
    b1 = dram("b1", [F1], F32, "ExternalInput")
    b2 = dram("b2", [E], F32, "ExternalInput")
    b1_t = dram("b1_t", [128, F1], F32, "ExternalInput")
    lnw_t = dram("lnw_t", [128, E], F32, "ExternalInput")
    lnb_t = dram("lnb_t", [128, E], F32, "ExternalInput")
    cos_t = dram("cos_t", [S, D], F32, "ExternalInput")
    nsin_t = dram("nsin_t", [S, D], F32, "ExternalInput")

    out = dram("out", [TOWN, E], F32, "ExternalOutput")

    def scratch(name, shape, dtype):
        kind = "ExternalOutput" if name in dbg else "Internal"
        return dram(name, shape, dtype, kind)

    xn_h = scratch("xn_h", [S, E], BF16)
    q_tm = scratch("q_tm", [TOWN, E], BF16)
    k_tm = scratch("k_tm", [S, E], BF16)
    v_h = scratch("v_h", [H, S, D], BF16)
    nrm_h = scratch("nrm_h", [2, 128, H], F32)
    crow_h = scratch("crow_h", [H, 1], F32)
    av_h = scratch("av_h", [H, TOWN, D], BF16)
    ao_h = scratch("ao_h", [TOWN, E], BF16)
    xlnq_h = scratch("xlnq_h", [TOWN, E], BF16)
    gq_h = scratch("gq_h", [TOWN, F1], BF16)
    w1q_h = scratch("w1q_h", [F1, E], BF16)
    w2q_h = scratch("w2q_h", [E, F1], BF16)
    w2qT_h = scratch("w2qT_h", [F1, E], BF16)
    cdbg_h = scratch("cdbg_h", [1, 4 * H], F32)

    def sb(name, shape, dtype, st=stack):
        return st.enter_context(nc.sbuf_tensor(name, shape, dtype))

    def ps(name, shape, st):
        return st.enter_context(nc.psum_tensor(name, shape, F32))

    ones_r = sb("ones_r", [1, 512], F32)
    ones_b = sb("ones_b", [1, 512], BF16)
    cneg_s = sb("cneg_s", [128, H], F32)

    fence = sem("fence")
    FCT = [0]

    CHAIN_OPS = {
        "memset", "memzero", "tensor_copy", "tensor_tensor", "tensor_add",
        "tensor_sub", "tensor_mul", "tensor_max", "tensor_scalar",
        "tensor_single_scalar", "tensor_scalar_add", "tensor_scalar_sub",
        "tensor_scalar_mul", "tensor_scalar_max", "tensor_scalar_min",
        "tensor_relu", "tensor_reduce", "reduce_max", "reduce_sum",
        "scalar_tensor_tensor", "reciprocal", "select", "activation",
        "square", "sqrt", "copy", "add", "mul", "sign",
    }
    chain_state = {"scalar": 0, "vector": 0, "gpsimd": 0}

    ext_counts = {}
    chain_ext = {"scalar": None, "vector": None, "gpsimd": None}

    class ChainEng:

        def __init__(self, eng, name):
            self._eng = eng
            self._name = name
            self._sem = sem(f"ch_{name}")
            self._pending_sig = None

        def signal_next(self, signame):
            self._pending_sig = signame
            return self

        def __getattr__(self, attr):
            fn = getattr(self._eng, attr)
            if attr not in CHAIN_OPS:
                return fn

            def wrapped(*a, **k):
                c = chain_state[self._name]
                if c > 0:
                    self._eng.wait_ge(self._sem, c)
                ext = chain_ext[self._name]
                if ext is not None:
                    self._eng.wait_ge(sem(ext[0]), ext[1])
                    chain_ext[self._name] = None
                ins = fn(*a, **k)
                if self._pending_sig is not None:
                    nm = self._pending_sig
                    self._pending_sig = None
                    val = ext_counts.get(nm, 0) + 1
                    ext_counts[nm] = val
                    ins.then_inc(sem(nm), 1)
                    chain_ext[self._name] = (nm, val)
                else:
                    ins.then_inc(self._sem, 1)
                    chain_state[self._name] = c + 1
                return ins
            return wrapped

        def drain(self):
            return self._eng.drain()

    @contextlib.contextmanager
    def phase(name, run=True):
        class Engs:
            pass
        if not run:
            yield None
            return
        with nc.Block() as blk:
            bodies = {}

            class Reg:
                def __getattr__(self, eng):
                    def reg(fn):
                        bodies[eng] = fn
                        return fn
                    return reg
            yield Reg()
            base = FCT[0]
            for eng_name in ("gpsimd", "sync", "scalar", "vector", "tensor"):
                body = bodies.get(eng_name)
                dec = getattr(blk, eng_name)

                def make(body_fn, engn):
                    def _(eng):
                        eng.wait_ge(fence, base)
                        if body_fn is not None:
                            if engn in ("scalar", "vector", "gpsimd"):
                                body_fn(ChainEng(eng, engn))
                            else:
                                body_fn(eng)
                        eng.drain().then_inc(fence, 1)
                    return _
                dec(make(body, eng_name))
            FCT[0] = base + NENG

    def emit_quant2(v, src_ap, out_ap, nblk, am, scl, inv, xdv, y16, sig=None):
        fd = nblk * 32
        v.tensor_reduce(am[:, :nblk], r3(src_ap, 32), AX.X, ALU.max,
                        apply_absolute_value=True)
        v.tensor_scalar_max(am[:, :nblk], am[:, :nblk], QCLAMP)
        v.tensor_single_scalar(scl[:, :nblk].bitcast(I32),
                               am[:, :nblk].bitcast(I32), EXPMASK,
                               ALU.bitwise_and)
        v.tensor_single_scalar(scl[:, :nblk].bitcast(I32),
                               scl[:, :nblk].bitcast(I32), SIXSHIFT,
                               ALU.subtract)
        v.tensor_scalar(inv[:, :nblk].bitcast(I32), scl[:, :nblk].bitcast(I32),
                        -1, INVBASE, ALU.mult, ALU.add)
        v.tensor_tensor(r3(xdv[:, :fd], 32), r3(src_ap, 32),
                        bc(inv[:, :nblk], nblk, 32), ALU.mult)
        v.tensor_scalar(xdv[:, :fd], xdv[:, :fd], PRECLIP, MAGIC,
                        ALU.min, ALU.add)
        if sig is not None:
            v.signal_next(sig)
        ins = v.scalar_tensor_tensor(r3(out_ap, 32), r3(xdv[:, :fd], 32), MAGIC,
                                     bc(scl[:, :nblk], nblk, 32),
                                     ALU.subtract, ALU.mult)
        return ins

    with phase("ph0") as engs:
        @engs.vector
        def _(v):
            v.memset(ones_r[:], 1.0)
            v.memset(ones_b[:], 1.0)

    NT1 = S // 128
    ph1_st = contextlib.ExitStack()
    xbuf = sb("xbuf", [128, 2, E], F32, ph1_st)
    jnk = sb("jnk", [128, 2, E], F32, ph1_st)
    xnbuf = sb("xnbuf", [128, 2, E], BF16, ph1_st)
    stat = sb("stat", [128, 2, 8], F32, ph1_st)

    rec1 = {}

    def ph1_sync_chunk(sy, lo, hi):
        for i in range(lo, min(hi, NT1)):
            if i >= 2:
                sy.wait_ge(sem("s_af"), i - 1)
            rec1["x", i] = R_SY.issue(sy, xbuf[:, i % 2, :],
                                      xs[i * 128:(i + 1) * 128, :])

    def ph1_scalar(sc):
        for i in range(NT1):
            pa = i % 2
            R_SY.wait(sc, rec1["x", i])
            xt = xbuf[:, pa, :]
            sc.activation(jnk[:, 0, :], xt, AF.Copy,
                          accum_out=stat[:, pa, 0:1])
            sc.signal_next("s_st").activation(jnk[:, 1, :], xt, AF.Square,
                          accum_out=stat[:, pa, 1:2])
            sc.wait_ge(sem("v_t"), i + 1)
            sc.signal_next("s_sq").activation(stat[:, pa, 7:8],
                                              stat[:, pa, 6:7], AF.Sqrt)
            sc.wait_ge(sem("v_st"), i + 1)
            if i >= 2:
                R_GS.wait(sc, rec1["xn", i - 2])
            sc.signal_next("s_af").activation(xnbuf[:, pa, :], xt, AF.Identity,
                          scale=stat[:, pa, 4:5], bias=stat[:, pa, 5:6])

    def ph1_vec_pump(v, upto_i):
        inv_n = 1.0 / E
        while ph1_vec_pump.done < min(upto_i, NT1):
            i = ph1_vec_pump.done
            pa = i % 2
            st_ = stat[:, pa]
            v.wait_ge(sem("s_st"), i + 1)
            v.tensor_scalar_mul(st_[:, 2:3], st_[:, 0:1], inv_n)
            v.tensor_scalar_mul(st_[:, 3:4], st_[:, 1:2], inv_n)
            v.scalar_tensor_tensor(st_[:, 6:7], st_[:, 2:3], -1.0, st_[:, 2:3],
                                   ALU.mult, ALU.mult)
            v.scalar_tensor_tensor(st_[:, 6:7], st_[:, 3:4], 1.0, st_[:, 6:7],
                                   ALU.mult, ALU.add)
            v.signal_next("v_t").tensor_scalar_add(st_[:, 6:7], st_[:, 6:7], EPS)
            v.wait_ge(sem("s_sq"), i + 1)
            v.reciprocal(st_[:, 4:5], st_[:, 7:8])
            for _ in range(2):
                v.scalar_tensor_tensor(st_[:, 7:8], st_[:, 4:5], 1.0, st_[:, 4:5],
                                       ALU.mult, ALU.mult)
                v.scalar_tensor_tensor(st_[:, 7:8], st_[:, 7:8], -0.5, st_[:, 6:7],
                                       ALU.mult, ALU.mult)
                v.tensor_scalar_add(st_[:, 7:8], st_[:, 7:8], 1.5)
                v.scalar_tensor_tensor(st_[:, 4:5], st_[:, 7:8], 1.0, st_[:, 4:5],
                                       ALU.mult, ALU.mult)
            v.signal_next("v_st").scalar_tensor_tensor(
                st_[:, 5:6], st_[:, 2:3], -1.0, st_[:, 4:5],
                ALU.mult, ALU.mult)
            ph1_vec_pump.done = i + 1
    ph1_vec_pump.done = 0

    def ph1_gp_pump(g, upto_xn):
        while ph1_gp_pump.done < min(upto_xn, NT1):
            i = ph1_gp_pump.done
            g.wait_ge(sem("s_af"), i + 1)
            rec1["xn", i] = R_GS.issue(g, xn_h[i * 128:(i + 1) * 128, :],
                                       xnbuf[:, i % 2, :])
            ph1_gp_pump.done = i + 1
    ph1_gp_pump.done = 0


    ph2_st = contextlib.ExitStack()
    wbuf = sb("wbuf", [128, NEC, E], BF16, ph2_st)
    xnt = sb("xnt", [128, 2, NEC, 512], BF16, ph2_st)
    t1b = sb("t1b", [128, E], F32, ph2_st)
    t2b = sb("t2b", [128, E], F32, ph2_st)
    nqt = sb("nqt", [128, H], F32, ph2_st)
    nqacc_q = sb("nqacc_q", [128, H], F32, ph2_st)
    nqacc_k = sb("nqacc_k", [128, H], F32, ph2_st)
    vtile = sb("vtile", [128, 2, E], BF16, ph2_st)
    cos_tm = sb("cos_tm", [128, 32, D], BF16, ph2_st)
    nsin_tm = sb("nsin_tm", [128, 32, D], BF16, ph2_st)
    bq_r = sb("bq_r", [1, E], BF16, ph2_st)
    bk_r = sb("bk_r", [1, E], BF16, ph2_st)
    bv_r = sb("bv_r", [1, E], BF16, ph2_st)
    crow_s = sb("crow_s", [1, H], F32, ph2_st)
    nrmt_s = sb("nrmt_s", [16, 2, 128], F32, ph2_st)
    mx_s = sb("mx_s", [16, 4], F32, ph2_st)

    vps = ps("vps", [128, 2, E], ph2_st)

    def hview(ap2d):
        return ap2d.rearrange("p (h d) -> p h d", d=D)

    def hbc(ap1, off, num):
        return bass.AP(ap1.tensor, ap1.offset + off, [ap1.ap[0], [0, H], [1, num]])

    def emit_wload(g, ring, w_hbm, rec, dstbuf=None):
        dstbuf = wbuf if dstbuf is None else dstbuf
        rec["wl"] = []
        for c4 in range(4):
            srcw = w_hbm[c4 * 512:(c4 + 1) * 512, :].rearrange(
                "(c p) f -> p c f", p=128)
            rec["wl"].append(ring.issue(g, dstbuf[:, c4 * 4:(c4 + 1) * 4, :], srcw))

    def proj_tm(tag, w_hbm, b_row, nslabs, kind, run):
        rec = {}
        NTC = nslabs * 4
        dst_tm = {"q": q_tm, "k": k_tm, "v": None}[kind]
        nqacc = {"q": nqacc_q, "k": nqacc_k, "v": None}[kind]

        with phase(f"ph2{tag}", run=run) as engs:
            if engs is None:
                return

            @engs.gpsimd
            def _(g):
                if kind == "q":
                    rec["cos"] = R_GL.issue(g, cos_tm[:],
                                            cos_t.rearrange("(c p) d -> p c d", p=128))
                    rec["nsin"] = R_GL.issue(g, nsin_tm[:],
                                             nsin_t.rearrange("(c p) d -> p c d", p=128))
                    bidx = [R_GL.issue(g, dst[:], srcb[None, :])
                            for dst, srcb in ((bq_r, bq), (bk_r, bkk), (bv_r, bv))]
                    for i in bidx:
                        R_GL.wait(g, i)
                    g.nop().then_inc(sem("bi_done"), 1)
                emit_wload(g, R_GL, w_hbm, rec)
                for tcix in range(NTC):
                    if kind == "q":
                        ph1_gp_pump(g, 2 * tcix + 4)
                    g.wait_ge(sem(f"v_ev{tag}"), tcix + 1)
                    rows = slice(tcix * 128, (tcix + 1) * 128)
                    if kind == "v":
                        for hh in range(H):
                            rec["st", tcix, hh] = R_GS.issue(
                                g, v_h[hh, rows, :],
                                vtile[:, tcix % 2, hh * D:(hh + 1) * D])
                    else:
                        rec["st", tcix] = R_GS.issue(g, dst_tm[rows, :],
                                                     vtile[:, tcix % 2, :])
                if kind == "q":
                    ph1_gp_pump(g, NT1)
                R_GS.wait_all(g)

            @engs.sync
            def _(sy):
                if kind == "q":
                    ph1_sync_chunk(sy, 0, 6)
                if kind == "k":
                    for i in range(NT1 - 8, NT1):
                        R_GS.wait(sy, rec1["xn", i])
                for j in range(nslabs):
                    if j >= 2:
                        sy.wait_ge(sem(f"t_sl{tag}"), j - 1)
                    if kind == "q":
                        for i in range(4 * j, 4 * j + 4):
                            R_GS.wait(sy, rec1["xn", i])
                    for ec in range(NEC):
                        rec["xt", j, ec] = R_SY.issue(
                            sy, xnt[:, j % 2, ec, :],
                            xn_h[j * 512:(j + 1) * 512, ec * 128:(ec + 1) * 128],
                            transpose=True)
                    if kind == "q":
                        ph1_sync_chunk(sy, 6 + 6 * j, 6 + 6 * (j + 1))
                if kind == "q":
                    ph1_sync_chunk(sy, 6 + 6 * nslabs, NT1)

            @engs.tensor
            def _(t):
                for i in rec["wl"]:
                    R_GL.wait(t, i)
                if kind == "q":
                    t.wait_ge(sem("bi_done"), 1)
                for j in range(nslabs):
                    for ec in range(NEC):
                        R_SY.wait(t, rec["xt", j, ec])
                    for tc4 in range(4):
                        tcix = j * 4 + tc4
                        if tcix >= 2:
                            t.wait_ge(sem(f"v_ev{tag}"), tcix - 1)
                        pt = vps[:, tcix % 2, :]
                        tsl = slice(tc4 * 128, (tc4 + 1) * 128)
                        for fs in range(4):
                            po = pt[:, fs * 512:(fs + 1) * 512]
                            for ec in range(NEC):
                                t.matmul(po, xnt[:, j % 2, ec, tsl],
                                         wbuf[:, ec, fs * 512:(fs + 1) * 512],
                                         start=(ec == 0), stop=False)
                            mm = t.matmul(po, ones_b[0:1, 0:128],
                                          b_row[0:1, fs * 512:(fs + 1) * 512],
                                          start=False, stop=True)
                        mm.then_inc(sem(f"t_mm{tag}"), 1)
                    t.drain().then_inc(sem(f"t_sl{tag}"), 1)

            @engs.scalar
            def _(sc):
                if kind == "q":
                    ph1_scalar(sc)

            @engs.vector
            def _(v):
                if kind != "v":
                    v.memset(nqacc[:], 0.0)
                    R_GL.wait(v, rec["cos"] if "cos" in rec else 0)
                    R_GL.wait(v, rec["nsin"] if "nsin" in rec else 0)
                for tcix in range(NTC):
                    pa = tcix % 2
                    if kind == "q":
                        ph1_vec_pump(v, 2 * tcix + 6)
                    v.wait_ge(sem(f"t_mm{tag}"), tcix + 1)
                    if tcix >= 2:
                        if kind == "v":
                            for hh in range(H):
                                R_GS.wait(v, rec["st", tcix - 2, hh])
                        else:
                            R_GS.wait(v, rec["st", tcix - 2])
                    pt = vps[:, pa, :]
                    if kind == "v":
                        for fs in range(4):
                            e2 = v.signal_next(f"v_ev{tag}") if fs == 3 else v
                            e2.tensor_copy(vtile[:, pa, fs * 512:(fs + 1) * 512],
                                           pt[:, fs * 512:(fs + 1) * 512])
                    else:
                        cosc = cos_tm[:, tcix, :]
                        nsinc = nsin_tm[:, tcix, :]
                        pt3 = hview(pt)
                        v.tensor_tensor(hview(t1b[:]), pt3, hbc(cosc, 0, D),
                                        ALU.mult)
                        ob = hview(t2b[:])
                        oa = bass.AP(ob.tensor, ob.offset, [ob.ap[0], [D, H], [1, 64]])
                        ia = bass.AP(pt3.tensor, pt3.offset + 64,
                                     [pt3.ap[0], [D, H], [1, 64]])
                        v.tensor_tensor(oa, ia, hbc(nsinc, 0, 64), ALU.mult)
                        ob2 = bass.AP(ob.tensor, ob.offset + 64,
                                      [ob.ap[0], [D, H], [1, 64]])
                        ib2 = bass.AP(pt3.tensor, pt3.offset,
                                      [pt3.ap[0], [D, H], [1, 64]])
                        v.tensor_tensor(ob2, ib2, hbc(nsinc, 64, 64), ALU.mult)
                        ro = vtile[:, pa, :]
                        v.tensor_add(ro, t1b[:], t2b[:])
                        v.tensor_mul(t1b[:], ro, ro)
                        v.tensor_reduce(nqt[:], hview(t1b[:]), AX.X, ALU.add)
                        v.signal_next(f"v_ev{tag}").tensor_max(
                            nqacc[:], nqacc[:], nqt[:])
                if kind == "q":
                    ph1_vec_pump(v, NT1)

    proj_tm("q", wqt, bq_r, TOWN // 512, "q", run=(upto >= 2))
    proj_tm("k", wkt, bk_r, S // 512, "k", run=(upto >= 2))
    proj_tm("v", wvt, bv_r, S // 512, "v", run=(upto >= 2))

    with phase("ph2d1", run=(upto >= 2)) as engs:
        if engs is not None:
            @engs.gpsimd
            def _(g):
                i1 = R_GS.issue(g, nrm_h[0], nqacc_q[:])
                i2 = R_GS.issue(g, nrm_h[1], nqacc_k[:])
                R_GS.wait(g, i1)
                R_GS.wait(g, i2)

    rec2d = {}
    with phase("ph2d2", run=(upto >= 2)) as engs:
        if engs is not None:
            @engs.sync
            def _(sy):
                for i in range(2):
                    rec2d["nl", i] = R_SY.issue(
                        sy, nrmt_s[:, i, :], nrm_h[i].rearrange("t h -> h t"),
                        allow_slow_non_contiguous=True)

            @engs.vector
            def _(v):
                R_SY.wait(v, rec2d["nl", 0])
                R_SY.wait(v, rec2d["nl", 1])
                v.tensor_reduce(mx_s[:, 0:1], nrmt_s[:, 0, :], AX.X, ALU.max)
                v.tensor_reduce(mx_s[:, 1:2], nrmt_s[:, 1, :], AX.X, ALU.max)
                v.signal_next("cd1").tensor_mul(mx_s[:, 2:3], mx_s[:, 0:1],
                                                 mx_s[:, 1:2])
                v.wait_ge(sem("cd2"), 1)
                v.signal_next("cd3").tensor_scalar_mul(mx_s[:, 3:4], mx_s[:, 3:4],
                                                        -INVSQ)

            @engs.scalar
            def _(sc):
                sc.wait_ge(sem("cd1"), 1)
                sc.signal_next("cd2").activation(mx_s[:, 3:4], mx_s[:, 2:3],
                                                  AF.Sqrt)

            @engs.gpsimd
            def _(g):
                g.wait_ge(sem("cd3"), 1)
                R_GS.wait(g, R_GS.issue(g, crow_h, mx_s[:, 3:4]))

    with phase("ph2d3", run=(upto >= 2)) as engs:
        if engs is not None:
            @engs.sync
            def _(sy):
                rec2d["cl"] = R_SY.issue(sy, crow_s[:],
                                         crow_h.rearrange("h one -> one h"),
                                         allow_slow_non_contiguous=True)

            @engs.tensor
            def _(t):
                R_SY.wait(t, rec2d["cl"])
                t.matmul(vps[:, 0, 0:H], ones_r[0:1, 0:128], crow_s[:],
                         start=True, stop=True).then_inc(sem("cd4"), 1)

            @engs.vector
            def _(v):
                v.wait_ge(sem("cd4"), 1)
                v.tensor_copy(cneg_s[:], vps[:, 0, 0:H])

    ph2_st.close()
    ph1_st.close()

    qw_chunks = []
    for r in range(F1 // 128):
        for c in range(2):
            qw_chunks.append((w1[r * 128:(r + 1) * 128, c * 1024:(c + 1) * 1024],
                              w1q_h[r * 128:(r + 1) * 128, c * 1024:(c + 1) * 1024]))
    for r in range(E // 128):
        for c in range(8):
            qw_chunks.append((w2[r * 128:(r + 1) * 128, c * 1024:(c + 1) * 1024],
                              w2q_h[r * 128:(r + 1) * 128, c * 1024:(c + 1) * 1024]))
    NQWC = len(qw_chunks)

    ph34_st = contextlib.ExitStack()
    av_all = sb("av_all", [128, H, TOWN], BF16, ph34_st)
    wo_s = sb("wo_s", [128, NEC, E], BF16, ph34_st)
    rec34 = {}

    ph3_st = contextlib.ExitStack()
    qw_in = sb("qw_in", [128, 4, 1024], F32, ph3_st)
    qxdv = sb("qxdv", [128, 1024], F32, ph3_st)
    qout = sb("qout", [128, 2, 1024], BF16, ph3_st)
    q3am = sb("q3am", [128, 32], F32, ph3_st)
    q3sc = sb("q3sc", [128, 32], F32, ph3_st)
    q3iv = sb("q3iv", [128, 32], F32, ph3_st)
    kth = sb("kth", [128, 2, S], BF16, ph3_st)
    qth = sb("qth", [128, 2, TOWN], BF16, ph3_st)
    vh3 = sb("vh3", [128, 2, 32, D], BF16, ph3_st)
    p_t = sb("p_t", [128, 4, 512], BF16, ph3_st)
    onec = sb("onec", [128, 1], BF16, ph3_st)
    rsum_s = sb("rsum_s", [1, 512], F32, ph3_st)
    rsb = sb("rsb", [128, 512], F32, ph3_st)
    sps = ps("sps", [128, 3, 512], ph3_st)
    avp = ps("avp", [128, 2, 512], ph3_st)
    ssum = ps("ssum", [128, 2, 512], ph3_st)
    rps = ps("rps", [128, 512], ph3_st)

    NKC = S // 128
    NKP = NKC // 2
    NQB = TOWN // 512

    with phase("ph3", run=(upto >= 3)) as engs:
        if engs is not None:
            @engs.gpsimd
            def _(g):
                g.signal_next("g_one").memset(onec[:], 1.0)
                emit_wload(g, R_GL, wot, rec34, dstbuf=wo_s)
                for i in range(NQWC):
                    g.wait_ge(sem("v_qw"), i + 1)
                    rec34["qwo", i] = R_GS.issue(g, qw_chunks[i][1],
                                                 qout[:, i % 2, :])
                R_GS.wait_all(g)

            @engs.sync
            def _(sy):
                for hh in range(H):
                    if hh >= 2:
                        sy.wait_ge(sem("t_h"), hh - 1)
                    pa = hh % 2
                    ld = [R_SY.issue(sy, kth[:, pa, :],
                                     k_tm[:, hh * 128:(hh + 1) * 128],
                                     transpose=True),
                          R_SY.issue(sy, qth[:, pa, :],
                                     q_tm[:, hh * 128:(hh + 1) * 128],
                                     transpose=True)]
                    for c8 in range(4):
                        ld.append(R_SY.issue(
                            sy, vh3[:, pa, c8 * 8:(c8 + 1) * 8, :],
                            v_h[hh, c8 * 1024:(c8 + 1) * 1024, :]
                            .rearrange("(c p) d -> p c d", p=128)))
                    rec34["ld", hh] = ld
                    for c4 in range(16):
                        i = hh * 16 + c4
                        if i < NQWC:
                            if i >= 4:
                                sy.wait_ge(sem("v_qw"), i - 3)
                            rec34["qwi", i] = R_SY.issue(sy, qw_in[:, i % 4, :],
                                                         qw_chunks[i][0])

            def av_one(t, hh, qb, kc, sums_first=False):
                gqi = hh * NQB + qb
                kcg = gqi * NKC + kc
                pslice = p_t[:, kcg % 4, :]
                if sums_first:
                    t.matmul(ssum[0:1, gqi % 2, :], onec[:], pslice,
                             start=(kc == 0), stop=(kc == NKC - 1))
                    t.matmul(avp[:, gqi % 2, :], vh3[:, hh % 2, kc, :], pslice,
                             start=(kc == 0), stop=(kc == NKC - 1)) \
                        .then_inc(sem("t_av"), 1)
                else:
                    t.matmul(avp[:, gqi % 2, :], vh3[:, hh % 2, kc, :], pslice,
                             start=(kc == 0), stop=(kc == NKC - 1))
                    t.matmul(ssum[0:1, gqi % 2, :], onec[:], pslice,
                             start=(kc == 0), stop=(kc == NKC - 1)) \
                        .then_inc(sem("t_av"), 1)

            @engs.tensor
            def _(t):
                t.wait_ge(sem("g_one"), 1)
                NGQ = H * NQB

                def prev(gqi):
                    return (gqi - 1) // NQB, (gqi - 1) % NQB

                for hh in range(H):
                    for i in rec34["ld", hh]:
                        R_SY.wait(t, i)
                    for qb in range(NQB):
                        gqi = hh * NQB + qb
                        for kc in range(NKC):
                            kcg = gqi * NKC + kc
                            if kcg >= 3:
                                t.wait_ge(sem("s_p"), kcg - 2)
                            t.matmul(sps[:, kcg % 3, :],
                                     kth[:, hh % 2, kc * 128:(kc + 1) * 128],
                                     qth[:, hh % 2, qb * 512:(qb + 1) * 512],
                                     start=True, stop=True).then_inc(sem("t_s"), 1)
                            if kc == 0:
                                if gqi >= 1:
                                    ph, pq = prev(gqi)
                                    t.wait_ge(sem("s_p"), gqi * NKC - 1)
                                    av_one(t, ph, pq, NKC - 2)
                            elif kc == 1:
                                if gqi >= 1:
                                    ph, pq = prev(gqi)
                                    t.wait_ge(sem("s_p"), gqi * NKC)
                                    av_one(t, ph, pq, NKC - 1)
                                    if qb == 0:
                                        t.drain().then_inc(sem("t_h"), 1)
                                    t.wait_ge(sem("v_rcp"), gqi)
                                    t.matmul(rps[:], ones_r[0:1, 0:128], rsum_s[:],
                                             start=True, stop=True) \
                                        .then_inc(sem("t_r"), 1)
                            else:
                                if kc == 2 and gqi >= 1:
                                    t.wait_ge(sem("v_av"), gqi - 1)
                                t.wait_ge(sem("s_p"), kcg - 1)
                                av_one(t, hh, qb, kc - 2)
                t.wait_ge(sem("s_p"), NGQ * NKC - 1)
                av_one(t, H - 1, NQB - 1, NKC - 2)
                t.wait_ge(sem("s_p"), NGQ * NKC)
                av_one(t, H - 1, NQB - 1, NKC - 1)
                t.wait_ge(sem("v_rcp"), NGQ)
                t.matmul(rps[:], ones_r[0:1, 0:128], rsum_s[:],
                         start=True, stop=True).then_inc(sem("t_r"), 1)

            @engs.scalar
            def _(sc):
                for gqi in range(H * NQB):
                    hh = gqi // NQB
                    for kc in range(NKC):
                        kcg = gqi * NKC + kc
                        sc.wait_ge(sem("t_s"), kcg + 1)
                        if kcg >= 4:
                            sc.wait_ge(sem("t_av"), kcg - 3)
                        sc.signal_next("s_p").activation(
                            p_t[:, kcg % 4, :], sps[:, kcg % 3, :],
                            AF.Exp, bias=cneg_s[:, hh:hh + 1], scale=INVSQ)

            @engs.vector
            def _(v):
                for gqi in range(H * NQB):
                    hh, qb = gqi // NQB, gqi % NQB
                    v.wait_ge(sem("t_av"), (gqi + 1) * NKC)
                    v.signal_next("v_rcp").tensor_copy(rsum_s[:],
                                                       ssum[0:1, gqi % 2, :])
                    v.wait_ge(sem("t_r"), gqi + 1)
                    v.reciprocal(rsb[:], rps[:])
                    v.signal_next("v_av").tensor_tensor(
                        av_all[:, hh, qb * 512:(qb + 1) * 512],
                        avp[:, gqi % 2, :], rsb[:], ALU.mult)
                    for c4 in range(4):
                        i = gqi * 4 + c4
                        if i < NQWC:
                            R_SY.wait(v, rec34["qwi", i])
                            if i >= 2:
                                R_GS.wait(v, rec34["qwo", i - 2])
                            emit_quant2(v, qw_in[:, i % 4, :], qout[:, i % 2, :],
                                        32, q3am, q3sc, q3iv, qxdv, None,
                                        sig="v_qw")

    ph3_st.close()

    ph4_st = contextlib.ExitStack()
    aot = sb("aot", [128, 2, E], BF16, ph4_st)
    bo_r = sb("bo_r", [1, E], F32, ph4_st)
    aops = ps("aops", [128, 2, E], ph4_st)
    xft = sb("xft", [128, 2, E], BF16, ph4_st)
    jnk5 = sb("jnk5", [128, E], BF16, ph4_st)
    xlnp = sb("xlnp", [128, E], F32, ph4_st)
    xln5 = sb("xln5", [128, E], BF16, ph4_st)
    stat5 = sb("stat5", [128, 2, 8], F32, ph4_st)
    lnw_s = sb("lnw_s", [128, E], F32, ph4_st)
    lnb_s = sb("lnb_s", [128, E], F32, ph4_st)
    q5am = sb("q5am", [128, 64], F32, ph4_st)
    q5sc = sb("q5sc", [128, 64], F32, ph4_st)
    q5iv = sb("q5iv", [128, 64], F32, ph4_st)
    q5dv = sb("q5dv", [128, E], F32, ph4_st)
    xlq5 = sb("xlq5", [128, 2, E], BF16, ph4_st)

    rec5 = {}
    rec5b = {}
    NT5 = TOWN // 128

    with phase("ph4", run=(upto >= 4)) as engs:
        if engs is not None:
            @engs.gpsimd
            def _(g):
                def xq_pump(upto_i):
                    while xq_pump.done < min(upto_i, NT5):
                        i = xq_pump.done
                        g.wait_ge(sem("v_x5"), i + 1)
                        rec5["xq", i] = R_GS.issue(
                            g, xlnq_h[i * 128:(i + 1) * 128, :],
                            xlq5[:, i % 2, :])
                        xq_pump.done = i + 1
                xq_pump.done = 0
                for tcg in range(TOWN // 128):
                    g.wait_ge(sem("v_eo"), tcg + 1)
                    rec34["aostore", tcg] = R_GS.issue(
                        g, ao_h[tcg * 128:(tcg + 1) * 128, :], aot[:, tcg % 2, :])
                    xq_pump(tcg)
                xq_pump(NT5)
                R_GS.wait_all(g)

            @engs.sync
            def _(sy):
                rec34["bo"] = R_SY.issue(sy, bo_r[:], bo[None, :])
                rec5["lnw"] = R_SY.issue(sy, lnw_s[:], lnw_t)
                rec5["lnb"] = R_SY.issue(sy, lnb_s[:], lnb_t)
                for i in range(NT5):
                    if i >= 2:
                        sy.wait_ge(sem("v_x5"), i - 1)
                    rec5["xf", i] = R_SY.issue(sy, xft[:, i % 2, :],
                                               xn_h[i * 128:(i + 1) * 128, :])

            @engs.tensor
            def _(t):
                for i in rec34["wl"]:
                    R_GL.wait(t, i)
                R_SY.wait(t, rec34["bo"])
                for tcg in range(TOWN // 128):
                    if tcg >= 2:
                        t.wait_ge(sem("v_eo"), tcg - 1)
                    pt = aops[:, tcg % 2, :]
                    for os_ in range(4):
                        po = pt[:, os_ * 512:(os_ + 1) * 512]
                        for hh in range(H):
                            t.matmul(po, av_all[:, hh, tcg * 128:(tcg + 1) * 128],
                                     wo_s[:, hh, os_ * 512:(os_ + 1) * 512],
                                     start=(hh == 0), stop=False)
                        mm = t.matmul(po, ones_r[0:1, 0:128],
                                      bo_r[0:1, os_ * 512:(os_ + 1) * 512],
                                      start=False, stop=True)
                    mm.then_inc(sem("t_mo"), 1)

            @engs.scalar
            def _(sc):
                for i in range(NT5):
                    pa = i % 2
                    R_SY.wait(sc, rec5["xf", i])
                    xt = xft[:, pa, :]
                    sc.activation(jnk5[:], xt, AF.Copy,
                                  accum_out=stat5[:, pa, 0:1])
                    sc.signal_next("s_st5").activation(
                        jnk5[:], xt, AF.Square, accum_out=stat5[:, pa, 1:2])
                    sc.wait_ge(sem("v_t5"), i + 1)
                    sc.signal_next("s_sq5").activation(stat5[:, pa, 7:8],
                                                        stat5[:, pa, 6:7], AF.Sqrt)
                    sc.wait_ge(sem("v_st5"), i + 1)
                    sc.signal_next("s_af5").activation(
                        xlnp[:], xt, AF.Identity,
                        scale=stat5[:, pa, 4:5], bias=stat5[:, pa, 5:6])

            @engs.vector
            def _(v):
                R_SY.wait(v, rec5["lnw"])
                R_SY.wait(v, rec5["lnb"])
                inv_n = 1.0 / E

                def ln5_chunk(i):
                    pa = i % 2
                    st_ = stat5[:, pa]
                    v.wait_ge(sem("s_st5"), i + 1)
                    v.tensor_scalar_mul(st_[:, 2:3], st_[:, 0:1], inv_n)
                    v.tensor_scalar_mul(st_[:, 3:4], st_[:, 1:2], inv_n)
                    v.scalar_tensor_tensor(st_[:, 6:7], st_[:, 2:3], -1.0,
                                           st_[:, 2:3], ALU.mult, ALU.mult)
                    v.scalar_tensor_tensor(st_[:, 6:7], st_[:, 3:4], 1.0,
                                           st_[:, 6:7], ALU.mult, ALU.add)
                    v.signal_next("v_t5").tensor_scalar_add(st_[:, 6:7],
                                                             st_[:, 6:7], EPS)
                    v.wait_ge(sem("s_sq5"), i + 1)
                    v.reciprocal(st_[:, 4:5], st_[:, 7:8])
                    for _ in range(2):
                        v.scalar_tensor_tensor(st_[:, 7:8], st_[:, 4:5], 1.0,
                                               st_[:, 4:5], ALU.mult, ALU.mult)
                        v.scalar_tensor_tensor(st_[:, 7:8], st_[:, 7:8], -0.5,
                                               st_[:, 6:7], ALU.mult, ALU.mult)
                        v.tensor_scalar_add(st_[:, 7:8], st_[:, 7:8], 1.5)
                        v.scalar_tensor_tensor(st_[:, 4:5], st_[:, 7:8], 1.0,
                                               st_[:, 4:5], ALU.mult, ALU.mult)
                    v.signal_next("v_st5").scalar_tensor_tensor(
                        st_[:, 5:6], st_[:, 2:3], -1.0, st_[:, 4:5],
                        ALU.mult, ALU.mult)
                    v.wait_ge(sem("s_af5"), i + 1)
                    v.tensor_mul(xlnp[:], xlnp[:], lnw_s[:])
                    v.tensor_add(xln5[:], xlnp[:], lnb_s[:])
                    if i >= 2:
                        R_GS.wait(v, rec5["xq", i - 2])
                    emit_quant2(v, xln5[:], xlq5[:, pa, :], 64,
                                q5am, q5sc, q5iv, q5dv, None, sig="v_x5")

                for tcg in range(TOWN // 128):
                    v.wait_ge(sem("t_mo"), tcg + 1)
                    if tcg >= 2:
                        R_GS.wait(v, rec34["aostore", tcg - 2])
                    for fs in range(4):
                        e2 = v.signal_next("v_eo") if fs == 3 else v
                        e2.tensor_copy(aot[:, tcg % 2, fs * 512:(fs + 1) * 512],
                                       aops[:, tcg % 2, fs * 512:(fs + 1) * 512])
                    ln5_chunk(tcg)

    ph4_st.close()
    ph34_st.close()

    ph5b_st = contextlib.ExitStack()
    xlqt = sb("xlqt", [128, NEC, TOWN], BF16, ph5b_st)
    w1t = sb("w1t", [128, 4, NEC, 512], BF16, ph5b_st)
    g1b = sb("g1b", [128, 4, 512], BF16, ph5b_st)
    g2b = sb("g2b", [128, 4, 512], BF16, ph5b_st)
    gqb = sb("gqb", [128, 4, 512], BF16, ph5b_st)
    b1r = sb("b1r", [1, F1], F32, ph5b_st)
    wbt = sb("wbt", [128, 4, 512], BF16, ph5b_st)
    q6am = sb("q6am", [128, 4, 16], F32, ph5b_st)
    q6sc = sb("q6sc", [128, 4, 16], F32, ph5b_st)
    q6iv = sb("q6iv", [128, 4, 16], F32, ph5b_st)
    q6dv = sb("q6dv", [128, 4, 512], F32, ph5b_st)
    h1ps = ps("h1ps", [128, 4, 512], ph5b_st)

    wbl_ring = Ring("wbl", 4)
    NFS = F1 // 512
    NTC5 = TOWN // 128
    NG5 = NFS * NTC5
    NWB = (E * F1 * 2) // (512 * 128 * 2)

    with phase("ph5b", run=(upto >= 5)) as engs:
        if engs is not None:
            @engs.gpsimd
            def _(g):
                for i in range(NG5):
                    g.wait_ge(sem("v_gq"), i + 1)
                    fs, tcb = i // NTC5, i % NTC5
                    rows = slice(tcb * 128, (tcb + 1) * 128)
                    rec5b["gq", i] = R_GS.issue(
                        g, gq_h[rows, fs * 512:(fs + 1) * 512], gqb[:, i % 4, :])
                    wbl_ring.wait(g, i)
                    os_, fc = i // 64, i % 64
                    rec5b["wbo", i] = R_GS.issue(
                        g, w2qT_h[fc * 128:(fc + 1) * 128,
                                  os_ * 512:(os_ + 1) * 512], wbt[:, i % 4, :])
                R_GS.wait_all(g)

            @engs.sync
            def _(sy):
                rec5b["b1"] = R_SY.issue(sy, b1r[:], b1[None, :])

                def w1fs(fs):
                    if fs >= 4:
                        sy.wait_ge(sem("t_fs"), fs - 3)
                    for ec in range(NEC):
                        rec5b["w1", fs, ec] = R_SY.issue(
                            sy, w1t[:, fs % 4, ec, :],
                            w1q_h[fs * 512:(fs + 1) * 512,
                                  ec * 128:(ec + 1) * 128],
                            transpose=True)

                def wbtk(fs):
                    for k in range(16):
                        i = fs * 16 + k
                        os_, fc = i // 64, i % 64
                        if i >= 4:
                            R_GS.wait(sy, rec5b["wbo", i - 4])
                        wbl_ring.issue(sy, wbt[:, i % 4, :],
                                       w2q_h[os_ * 512:(os_ + 1) * 512,
                                             fc * 128:(fc + 1) * 128],
                                       transpose=True)

                def xlt(j):
                    for ec in range(NEC):
                        rec5b["xlt", ec, j] = R_SY.issue(
                            sy, xlqt[:, ec, j * 128:(j + 1) * 128],
                            xlnq_h[j * 128:(j + 1) * 128,
                                   ec * 128:(ec + 1) * 128],
                            transpose=True)

                w1fs(0)
                for j in range(4):
                    xlt(j)
                w1fs(1)
                for j in range(4, NT5):
                    xlt(j)
                w1fs(2)
                wbtk(0)
                w1fs(3)
                wbtk(1)
                for fs in range(4, NFS):
                    w1fs(fs)
                    wbtk(fs - 2)
                wbtk(NFS - 2)
                wbtk(NFS - 1)


            @engs.tensor
            def _(t):
                R_SY.wait(t, rec5b["b1"])
                for fs in range(NFS):
                    for ec in range(NEC):
                        R_SY.wait(t, rec5b["w1", fs, ec])
                    for tcb in range(NTC5):
                        gidx = fs * NTC5 + tcb
                        if gidx >= 4:
                            t.wait_ge(sem("s_e1"), gidx - 3)
                        po = h1ps[:, gidx % 4, :]
                        for ec in range(NEC):
                            if fs == 0:
                                R_SY.wait(t, rec5b["xlt", ec, tcb])
                            t.matmul(po, xlqt[:, ec, tcb * 128:(tcb + 1) * 128],
                                     w1t[:, fs % 4, ec, :],
                                     start=(ec == 0), stop=False)
                        mm1 = t.matmul(po, ones_r[0:1, 0:128],
                                       b1r[0:1, fs * 512:(fs + 1) * 512],
                                       start=False, stop=True)
                        mm1.then_inc(sem("t_m1"), 1)
                    t.drain().then_inc(sem("t_fs"), 1)

            @engs.scalar
            def _(sc):
                for gidx in range(NG5):
                    pa = gidx % 4
                    sc.wait_ge(sem("t_m1"), gidx + 1)
                    sc.signal_next("s_e1").activation(g1b[:, pa, :],
                                                      h1ps[:, pa, :], AF.Copy)
                    if gidx >= 4:
                        sc.wait_ge(sem("v_gq"), gidx - 3)
                    sc.signal_next("s_g1").activation(g2b[:, pa, :], g1b[:, pa, :],
                                                       AF.Gelu_apprx_tanh)

            @engs.vector
            def _(v):
                for gidx in range(NG5):
                    pa = gidx % 4
                    v.wait_ge(sem("s_g1"), gidx + 1)
                    if gidx >= 4:
                        R_GS.wait(v, rec5b["gq", gidx - 4])
                    emit_quant2(v, g2b[:, pa, :],
                                gqb[:, pa, :], 16,
                                q6am[:, pa, :], q6sc[:, pa, :],
                                q6iv[:, pa, :], q6dv[:, pa, :], None, sig="v_gq")

    ph5b_st.close()

    ph5c_st = contextlib.ExitStack()
    gqt = sb("gqt", [128, 64, 512], BF16, ph5c_st)
    rec5c = {}
    w2t = sb("w2t", [128, 2, 32, 512], BF16, ph5c_st)
    h2s = sb("h2s", [128, 4, E], BF16, ph5c_st)
    xf6 = sb("xf6", [128, 2, E], BF16, ph5c_st)
    xr6 = sb("xr6", [128, 2, E], F32, ph5c_st)
    ao6 = sb("ao6", [128, 2, E], BF16, ph5c_st)
    ob6 = sb("ob6", [128, 2, E], F32, ph5c_st)
    fo6 = sb("fo6", [128, E], BF16, ph5c_st)
    b2_r = sb("b2_r", [1, E], F32, ph5c_st)
    h2ps = ps("h2ps", [128, 4, 512], ph5c_st)

    with phase("ph5c", run=(upto >= 5)) as engs:
        if engs is not None:
            @engs.gpsimd
            def _(g):
                for gt in range(16):
                    g.wait_ge(sem("v_fin"), gt + 1)
                    rec5c["out", gt] = R_GS.issue(
                        g, out[gt * 128:(gt + 1) * 128, :], ob6[:, gt % 2, :])
                R_GS.wait_all(g)

            @engs.sync
            def _(sy):
                rec5c["b2"] = R_SY.issue(sy, b2_r[:], b2[None, :])

                def w2half(os_, hf):
                    gos_ = w2half.tsb * 4 + os_
                    if gos_ >= 1:
                        sy.wait_ge(sem("t_hf"), (gos_ - 1) * 2 + hf + 1)
                    for fc in range(32):
                        rec5c["w2", gos_, hf, fc] = R_SY.issue(
                            sy, w2t[:, hf, fc, :],
                            w2qT_h[(hf * 32 + fc) * 128:
                                   (hf * 32 + fc + 1) * 128,
                                   os_ * 512:(os_ + 1) * 512])

                for tsb in range(4):
                    w2half.tsb = tsb
                    w2half(0, 0)
                    if tsb >= 1:
                        sy.wait_ge(sem("t_tsb"), tsb)
                    for fc in range(64):
                        rec5c["gqt", tsb, fc] = R_SY.issue(
                            sy, gqt[:, fc, :],
                            gq_h[tsb * 512:(tsb + 1) * 512,
                                 fc * 128:(fc + 1) * 128],
                            transpose=True)
                    for os_ in range(1, 4):
                        w2half(os_, 0)

            @engs.scalar
            def _(sc):
                for tsb in range(4):
                    for os_ in range(4):
                        gos = tsb * 4 + os_
                        hf = 1
                        if gos >= 1:
                            sc.wait_ge(sem("t_hf"), (gos - 1) * 2 + hf + 1)
                        for fc in range(32):
                            rec5c["w2", gos, hf, fc] = R_SC.issue(
                                sc, w2t[:, hf, fc, :],
                                w2qT_h[(hf * 32 + fc) * 128:
                                       (hf * 32 + fc + 1) * 128,
                                       os_ * 512:(os_ + 1) * 512])
                    for tt_ in range(4):
                        gt = tsb * 4 + tt_
                        if gt >= 2:
                            sc.wait_ge(sem("v_fin"), gt - 1)
                        rows = slice(gt * 128, (gt + 1) * 128)
                        rec5c["xr", gt] = R_SC.issue(sc, xr6[:, gt % 2, :], xs[rows, :])
                        rec5c["aor", gt] = R_SC.issue(sc, ao6[:, gt % 2, :], ao_h[rows, :])
                        rec5c["xfr", gt] = R_SC.issue(sc, xf6[:, gt % 2, :], xn_h[rows, :])

            @engs.tensor
            def _(t):
                R_SY.wait(t, rec5c["b2"])
                for tsb in range(4):
                    for os_ in range(4):
                        gos = tsb * 4 + os_
                        for hf in range(2):
                            w2r = R_SY if hf == 0 else R_SC
                            for ii in range(32):
                                w2r.wait(t, rec5c["w2", gos, hf, ii])
                            if hf == 0 and gos >= 1:
                                for tt_ in range(4):
                                    t.wait_ge(sem("v_e2"),
                                              (gos - 1) * 4 + tt_ + 1)
                            for fc in range(32):
                                if os_ == 0:
                                    R_SY.wait(t, rec5c["gqt", tsb,
                                                       hf * 32 + fc])
                                for tt_ in range(4):
                                    t.matmul(h2ps[:, tt_, :],
                                             gqt[:, hf * 32 + fc,
                                                 tt_ * 128:(tt_ + 1) * 128],
                                             w2t[:, hf, fc, :],
                                             start=(hf == 0 and fc == 0),
                                             stop=False)
                            t.drain().then_inc(sem("t_hf"), 1)
                        for tt_ in range(4):
                            t.matmul(h2ps[:, tt_, :], ones_r[0:1, 0:128],
                                     b2_r[0:1, os_ * 512:(os_ + 1) * 512],
                                     start=False, stop=True) \
                                .then_inc(sem("t_m2"), 1)
                    t.drain().then_inc(sem("t_tsb"), 1)

            @engs.vector
            def _(v):
                for tsb in range(4):
                    for os_ in range(4):
                        for tt_ in range(4):
                            gidx = (tsb * 4 + os_) * 4 + tt_
                            v.wait_ge(sem("t_m2"), gidx + 1)
                            v.signal_next("v_e2").tensor_copy(
                                h2s[:, tt_, os_ * 512:(os_ + 1) * 512],
                                h2ps[:, tt_, :])
                    for tt_ in range(4):
                        gt = tsb * 4 + tt_
                        R_SC.wait(v, rec5c["xr", gt])
                        R_SC.wait(v, rec5c["aor", gt])
                        R_SC.wait(v, rec5c["xfr", gt])
                        if gt >= 2:
                            R_GS.wait(v, rec5c["out", gt - 2])
                        v.tensor_add(fo6[:], xf6[:, gt % 2, :], h2s[:, tt_, :])
                        v.scalar_tensor_tensor(ob6[:, gt % 2, :], ao6[:, gt % 2, :],
                                               0.0, xr6[:, gt % 2, :],
                                               ALU.bypass, ALU.add)
                        v.signal_next("v_fin").tensor_add(ob6[:, gt % 2, :],
                                                           ob6[:, gt % 2, :], fo6[:])


    ph5c_st.close()

    stack.close()
    return nc




_ROPE_BASE = 10000.0


def _rope_tables():
    inv_freq = 1.0 / _ROPE_BASE ** (np.arange(0, D, 2, dtype=np.float32) / D)
    pos = np.arange(S, dtype=np.float32)
    freqs = np.outer(pos, inv_freq)
    emb = np.concatenate([freqs, freqs], axis=-1)
    cos = np.cos(emb).astype(np.float32)
    sin = np.sin(emb).astype(np.float32)
    nsin = np.concatenate([-sin[:, :64], sin[:, 64:]], axis=1)
    return cos, nsin


def _stage(inputs, core, shared, cos, nsin):
    b, h = core // 2, core % 2
    x = inputs["L0_attn_x"]
    perm = np.concatenate([np.arange(h * TOWN, (h + 1) * TOWN),
                           np.arange((1 - h) * TOWN, (2 - h) * TOWN)])
    d = dict(shared)
    d["xs"] = np.ascontiguousarray(x[b][perm])
    d["cos_t"] = np.ascontiguousarray(cos[perm])
    d["nsin_t"] = np.ascontiguousarray(nsin[perm])
    return d


def kernel(**inputs):
    from concourse.bass_utils import run_bass_kernel_spmd
    inputs = {k: np.ascontiguousarray(np.asarray(v, dtype=np.float32))
              for k, v in inputs.items()}
    shared = {
        "wqt": np.ascontiguousarray(inputs["L0_attn_Wq"].T),
        "wkt": np.ascontiguousarray(inputs["L0_attn_Wk"].T),
        "wvt": np.ascontiguousarray(inputs["L0_attn_Wv"].T),
        "wot": np.ascontiguousarray(inputs["L0_attn_Wo"].T),
        "w1": inputs["L0_ffn_W1"],
        "w2": inputs["L0_ffn_W2"],
        "bq": inputs["L0_attn_bq"],
        "bk": inputs["L0_attn_bk"],
        "bv": inputs["L0_attn_bv"],
        "bo": inputs["L0_attn_bo"],
        "b1": inputs["L0_ffn_b1"],
        "b2": inputs["L0_ffn_b2"],
        "b1_t": np.ascontiguousarray(
            np.broadcast_to(inputs["L0_ffn_b1"], (128, F1))),
        "lnw_t": np.ascontiguousarray(
            np.broadcast_to(inputs["L0_ffn_ln_w"], (128, E))),
        "lnb_t": np.ascontiguousarray(
            np.broadcast_to(inputs["L0_ffn_ln_b"], (128, E))),
    }
    cos, nsin = _rope_tables()
    in_maps = [_stage(inputs, c, shared, cos, nsin) for c in range(8)]
    nc = build()
    res = run_bass_kernel_spmd(nc, in_maps, core_ids=list(range(8)))
    outp = np.empty((B, S, E), dtype=np.float32)
    for c in range(8):
        b, h = c // 2, c % 2
        outp[b, h * TOWN:(h + 1) * TOWN, :] = res.results[c]["out"]
    return outp
